# version 05
# baseline (speedup 1.0000x reference)
import contextlib
import numpy as np
import ml_dtypes
import concourse.bass as bass
import concourse.mybir as mybir
from concourse.bass_utils import run_bass_kernel_spmd

F32 = mybir.dt.float32
BF16 = mybir.dt.bfloat16
I32 = mybir.dt.int32
AF = mybir.ActivationFunctionType
ALU = mybir.AluOpType
AX = mybir.AxisListType

S = 2048
T = 4096
H = 2048
NH = 16
D = 128
II = 8192
CH = 128
NCH = S // CH
NKT = H // 128
NTT = T // 128
NBH = 2 * NH
LNC = float(np.log(D ** -0.5))

ENGINES = ("sync", "gpsimd", "vector", "scalar", "tensor")


class Sem:
    __slots__ = ("h", "n", "name")

    def __init__(self, h, name):
        self.h = h
        self.n = 0
        self.name = name

    @property
    def now(self):
        return (self, self.n)


class Sched:

    def __init__(self, nc):
        self.nc = nc
        self.ctx = contextlib.ExitStack()
        self.ops = {e: [] for e in ENGINES}
        self.sems = {}
        self.seen = {e: {} for e in ENGINES}
        self.guards = []

    def sem(self, name):
        pool = name.split("_", 1)[1] if "_" in name else name
        if pool not in self.sems:
            h = self.ctx.enter_context(self.nc.semaphore(pool))
            self.sems[pool] = Sem(h, pool)
        return self.sems[pool]

    def emit(self, eng, fn, waits=(), inc=None, by=1):
        tick = None
        if inc is not None:
            inc.n += by
            tick = (inc, by)
        flat = []
        for w in waits:
            if w is None:
                continue
            if isinstance(w, list):
                flat.extend(x for x in w if x is not None)
            else:
                flat.append(w)
        flat = list(self.guards) + flat
        self.ops[eng].append((fn, flat, tick))
        return (inc, inc.n) if inc else None

    def dma(self, eng, out, in_, waits=(), inc=None, **kw):
        assert inc is not None
        return self.emit(eng, lambda e: e.dma_start(out=out, in_=in_, **kw),
                         waits=waits, inc=inc, by=16)

    def flush(self, block):
        for eng in ENGINES:
            if not self.ops[eng]:
                continue

            def body(e, ops_=self.ops[eng], seen=self.seen[eng]):
                for fn, waits, tick in ops_:
                    for sem, cnt in waits:
                        if seen.get(sem.name, -1) >= cnt:
                            continue
                        e.wait_ge(sem.h, cnt)
                        seen[sem.name] = cnt
                    ins = fn(e)
                    if tick is not None:
                        sem, by = tick
                        ins.then_inc(sem.h, by)
            getattr(block, eng)(body)
            self.ops[eng] = []


TENSORS = {
    "x":     ((T, H), F32, None),
    "xn":    ((H, T), BF16, "A"),
    "qs":    ((H, T), BF16, "A1"),
    "ks":    ((H, T), BF16, "A1"),
    "vs":    ((H, T), BF16, "A1"),
    "gsig":  ((H, T), BF16, "A1"),
    "alpha": ((NH, T), F32, "A1"),
    "beta":  ((NH, T), F32, "A1"),
    "hs":    ((II, T), BF16, "A2a"),
    "wd_bf": ((II, H), BF16, "A2a"),
    "gf":    ((T, H), BF16, "A2b"),
    "og":    ((H, T), BF16, "B"),
    "x1":    ((T, H), F32, "C"),
    "xn1":   ((H, T), BF16, "E"),
    "q1r":   ((H, T), BF16, "E"),
    "k1r":   ((512, T), BF16, "E"),
    "v1s":   ((512, T), BF16, "E"),
    "s1":    ((T, H), BF16, "F"),
    "wgT":   ((H, II), BF16, "G0"),
    "wuT":   ((H, II), BF16, "G0"),
    "wdT":   ((II, H), BF16, "G0"),
    "h1":    ((II, T), BF16, "G1"),
    "out":   ((T, H), F32, "G2"),
}
WEIGHTS = {
    "L0_attn_Wq": ((H, H), F32), "L0_attn_Wk": ((H, H), F32),
    "L0_attn_Wv": ((H, H), F32), "L0_attn_Wa": ((H, NH), F32),
    "L0_attn_ba": ((NH,), F32), "L0_attn_Wb": ((H, NH), F32),
    "L0_attn_bb": ((NH,), F32), "L0_attn_Wg": ((H, H), F32),
    "L0_attn_Wo": ((H, H), F32),
    "L0_attn_qcw": ((H, 4), F32), "L0_attn_qcb": ((H,), F32),
    "L0_attn_kcw": ((H, 4), F32), "L0_attn_kcb": ((H,), F32),
    "L0_attn_vcw": ((H, 4), F32), "L0_attn_vcb": ((H,), F32),
    "L0_attn_ln_w": ((D,), F32), "L0_attn_ln_b": ((D,), F32),
    "L0_ffn_w_gate": ((H, II), F32), "L0_ffn_w_up": ((H, II), F32),
    "L0_ffn_w_down": ((II, H), F32),
    "L1_attn_Wq": ((H, H), F32), "L1_attn_Wk": ((H, 512), F32),
    "L1_attn_Wv": ((H, 512), F32), "L1_attn_Wo": ((H, H), F32),
    "L1_ffn_gate_w": ((II, H), I32), "L1_ffn_gate_scale": ((II,), F32),
    "L1_ffn_up_w": ((II, H), I32), "L1_ffn_up_scale": ((II,), F32),
    "L1_ffn_down_w": ((H, II), I32), "L1_ffn_down_scale": ((H,), F32),
}
STAGE_DEPS = {
    "A":   ["x"],
    "A1":  ["xn", "L0_attn_Wq", "L0_attn_Wk", "L0_attn_Wv", "L0_attn_Wg",
            "L0_attn_Wa", "L0_attn_ba", "L0_attn_Wb", "L0_attn_bb",
            "L0_attn_qcw", "L0_attn_qcb", "L0_attn_kcw", "L0_attn_kcb",
            "L0_attn_vcw", "L0_attn_vcb"],
    "A2a": ["xn", "L0_ffn_w_gate", "L0_ffn_w_up", "L0_ffn_w_down"],
    "A2b": ["hs", "wd_bf"],
    "B":   ["qs", "ks", "vs", "gsig", "alpha", "beta",
            "L0_attn_ln_w", "L0_attn_ln_b"],
    "C":   ["og", "x", "gf", "L0_attn_Wo"],
    "E":   ["x1", "L1_attn_Wq", "L1_attn_Wk", "L1_attn_Wv"],
    "F":   ["q1r", "k1r", "v1s", "L1_attn_Wo"],
    "G0":  ["L1_ffn_gate_w", "L1_ffn_up_w", "L1_ffn_down_w",
            "L1_ffn_down_scale"],
    "G1":  ["xn1", "wgT", "wuT", "L1_ffn_gate_scale", "L1_ffn_up_scale"],
    "G2":  ["h1", "wdT", "x1", "s1"],
}
STAGE_OUTS = {
    "A": ["xn"], "A1": ["qs", "ks", "vs", "gsig", "alpha", "beta"],
    "A2a": ["hs", "wd_bf"], "A2b": ["gf"], "B": ["og"], "C": ["x1"],
    "E": ["xn1", "q1r", "k1r", "v1s"], "F": ["s1"],
    "G0": ["wgT", "wuT", "wdT"], "G1": ["h1"], "G2": ["out"],
}
ALL_STAGES = ["A", "A1", "A2a", "A2b", "B", "G0", "C", "E", "F", "G1", "G2"]

GROUPS = [list(ALL_STAGES)]


class Prog:

    def __init__(self, stages):
        self.stages = stages
        self.nc = bass.Bass("TRN2", target_bir_lowering=False, debug=False)
        self.sched = Sched(self.nc)
        need = set()
        for st in stages:
            need.update(STAGE_DEPS[st])
        prod_here = {t for st in stages for t in STAGE_OUTS[st]}
        later = [st for st in ALL_STAGES if st not in stages]
        self.hbm = {}
        self.input_names = []
        self.output_names = []
        for name in sorted(need | prod_here):
            if name in WEIGHTS:
                shape, dt = WEIGHTS[name]
                self.hbm[name] = self.nc.dram_tensor(
                    name, list(shape), dt, kind="ExternalInput").ap()
                self.input_names.append(name)
                continue
            shape, dt, prod = TENSORS[name]
            if name in prod_here:
                used_later = (name == "out") or any(
                    name in STAGE_DEPS[st2] for st2 in later)
                kind = "ExternalOutput" if used_later else "Internal"
                if kind == "ExternalOutput":
                    self.output_names.append(name)
            else:
                kind = "ExternalInput"
                self.input_names.append(name)
            self.hbm[name] = self.nc.dram_tensor(
                name, list(shape), dt, kind=kind).ap()
        self.sb = {}
        self.psn = {}
        self.stage_stack = None
        self.next_guards = []
        self.ready = {}

    def sbuf(self, name, shape, dt):
        if name not in self.sb:
            if self.stage_stack is not None:
                self.sb[name] = self.stage_stack.enter_context(
                    self.nc.sbuf_tensor(name, list(shape), dt,
                                        allow_name_mangling=True))
            else:
                self.sb[name] = self.nc.alloc_sbuf_tensor(
                    name, list(shape), dt, allow_name_mangling=True)
        return self.sb[name]

    def psum(self, name, shape, dt=F32):
        if name not in self.psn:
            if self.stage_stack is not None:
                self.psn[name] = self.stage_stack.enter_context(
                    self.nc.psum_tensor(name, list(shape), dt,
                                        allow_name_mangling=True))
            else:
                self.psn[name] = self.nc.alloc_psum_tensor(
                    name, list(shape), dt, allow_name_mangling=True)
        return self.psn[name]

    def begin_stage(self):
        self.stage_stack = contextlib.ExitStack()
        self.sb = {}
        self.psn = {}
        self.next_guards = []

    def end_stage(self):
        self.stage_stack.close()
        self.stage_stack = None
        self.sched.guards = list(self.next_guards)

    def rdy(self, name):
        return self.ready.get(name)

    def finish(self, st_sem):
        n = st_sem.n
        self.sched.emit("sync", lambda e, n=n: e.wait_ge(st_sem.h, n),
                        waits=())
        self.next_guards.append((st_sem, n))


def build_consts(p: Prog):
    sc = p.sched
    ones = p.sbuf("c_ones", (128, 128), BF16)
    idm = p.sbuf("c_idm", (128, 128), BF16)
    g_done = sc.sem("c_gp")
    z = sc.emit("gpsimd", lambda e: e.memset(ones[:], 1.0), inc=g_done)
    rdy = sc.emit("gpsimd",
                  lambda e: e.affine_select(idm[:], ones[:], [[1, 128]],
                                            ALU.is_equal, 0.0, base=0,
                                            channel_multiplier=-1),
                  waits=[z], inc=g_done)
    p.consts = {"ones": ones, "idm": idm, "idm_rdy": rdy}


def stage_A(p: Prog):
    stage_ln_generic(p, "x", "xn", "A")


def stage_ln_generic(p: Prog, srcname, dstname, tag):
    sc = p.sched
    x, xn = p.hbm[srcname], p.hbm[dstname]
    xbuf = [p.sbuf(f"{tag}_x{i}", (128, H), F32) for i in range(2)]
    st_t = p.sbuf(tag + "_stats", (128, 8), F32)
    xntm = [p.sbuf(f"{tag}_xntm{i}", (128, H), BF16) for i in range(2)]
    xnfm = [p.sbuf(f"{tag}_xnfm{i}", (128, H), BF16) for i in range(2)]
    pst = [p.psum(f"{tag}_pst{i}", (128, H), BF16) for i in range(2)]
    v, s_, tr = (sc.sem(n) for n in (tag + "_v", tag + "_s", tag + "_tr"))
    lx = [sc.sem(tag + "_lx0"), sc.sem(tag + "_lx1")]
    st = [sc.sem(tag + "_st0"), sc.sem(tag + "_st1")]
    idm = p.consts["idm"]
    idm_rdy = p.consts["idm_rdy"]

    free = [None, None]
    free_tm = [None, None]
    free_fm = [None, None]
    free_ps = [None, None]
    for tt in range(NTT):
        pa = tt % 2
        c0 = 4 * pa
        ld = sc.dma("sync", xbuf[pa][:], x[tt * 128:(tt + 1) * 128, :],
                    waits=[free[pa], p.rdy(srcname)], inc=lx[pa])
        r1 = sc.emit("vector", lambda e, pa=pa, c0=c0:
                     e.reduce_sum(st_t[:, c0:c0 + 1], xbuf[pa][:], axis=AX.X),
                     waits=[ld], inc=v)
        r2 = sc.emit("vector", lambda e, c0=c0:
                     e.tensor_scalar(st_t[:, c0 + 1:c0 + 2], st_t[:, c0:c0 + 1],
                                     1.0 / H, None, ALU.mult),
                     waits=[r1], inc=v)
        r3 = sc.emit("vector", lambda e, pa=pa, c0=c0:
                     e.tensor_scalar(xbuf[pa][:], xbuf[pa][:],
                                     st_t[:, c0 + 1:c0 + 2], None, ALU.subtract),
                     waits=[r2], inc=v)
        r4 = sc.emit("scalar", lambda e, pa=pa, c0=c0:
                     e.activation(xntm[pa][:], xbuf[pa][:], AF.Square,
                                  accum_out=st_t[:, c0 + 2:c0 + 3]),
                     waits=[r3, free_tm[pa]], inc=s_)
        r5 = sc.emit("vector", lambda e, c0=c0:
                     e.tensor_scalar(st_t[:, c0 + 3:c0 + 4], st_t[:, c0 + 2:c0 + 3],
                                     1.0 / H, 1e-5, ALU.mult, ALU.add),
                     waits=[r4], inc=v)
        r6 = sc.emit("scalar", lambda e, c0=c0:
                     e.activation(st_t[:, c0 + 2:c0 + 3], st_t[:, c0 + 3:c0 + 4],
                                  AF.Sqrt),
                     waits=[r5], inc=s_)
        r7 = sc.emit("vector", lambda e, c0=c0:
                     e.reciprocal(st_t[:, c0:c0 + 1], st_t[:, c0 + 2:c0 + 3]),
                     waits=[r6], inc=v)
        r8 = sc.emit("vector", lambda e, pa=pa, c0=c0:
                     e.tensor_scalar(xntm[pa][:], xbuf[pa][:], st_t[:, c0:c0 + 1],
                                     None, ALU.mult),
                     waits=[r7, r4], inc=v)
        free[pa] = r8

        def trans(e, pa=pa):
            for fb in range(16):
                ins = e.transpose(pst[pa][:, fb * 128:(fb + 1) * 128],
                                  xntm[pa][:, fb * 128:(fb + 1) * 128], idm[:])
            return ins
        r9 = sc.emit("tensor", trans, waits=[r8, idm_rdy, free_ps[pa]], inc=tr)
        free_tm[pa] = r9
        r10 = sc.emit("scalar", lambda e, pa=pa:
                      e.activation(xnfm[pa][:], pst[pa][:], AF.Copy),
                      waits=[r9, free_fm[pa]], inc=s_)
        free_ps[pa] = r10
        wr = sc.dma("sync",
                    xn.rearrange("(fb p) t -> p fb t", p=128)[:, :, tt * 128:(tt + 1) * 128],
                    xnfm[pa].ap().rearrange("p (fb c) -> p fb c", fb=16),
                    waits=[r10], inc=st[pa])
        free_fm[pa] = wr
    p.ready[dstname] = [(st[0], st[0].n), (st[1], st[1].n)]
    p.finish(st[0])
    p.finish(st[1])


def load_xn_res(p: Prog, name="xn"):
    sc = p.sched
    xn = p.hbm[name]
    xn_sb = p.sbuf(f"{name}_res", (128, NKT * T), BF16)
    lsem = sc.sem(f"{name}_res_l")
    for kt in range(NKT):
        sc.dma("sync", xn_sb[:, kt * T:(kt + 1) * T],
               xn[kt * 128:(kt + 1) * 128, :],
               waits=[p.rdy(name)], inc=lsem)
    return xn_sb, (lsem, lsem.n)


def stage_A1(p: Prog):
    sc = p.sched
    xn_sb, xn_rdy = load_xn_res(p)
    lw = [sc.sem("A1_lw0"), sc.sem("A1_lw1")]
    lcw = [sc.sem("A1_lcw0"), sc.sem("A1_lcw1")]
    lab = sc.sem("A1_lab")
    gp = sc.sem("A1_gp")
    mm = sc.sem("A1_mm")
    ve = sc.sem("A1_v")
    se = sc.sem("A1_s")
    st = [sc.sem("A1_st0"), sc.sem("A1_st1")]
    stab = sc.sem("A1_stab")
    wbuf = [p.sbuf(f"A1_w{i}", (128, NKT * 128), BF16) for i in range(2)]
    cwb = [p.sbuf(f"A1_cw{i}", (128, 5), F32) for i in range(2)]
    lin = p.sbuf("A1_lin", (128, 3 + S), F32)
    cv = [p.sbuf(f"A1_cv{i}", (128, 512), F32) for i in range(2)]
    ot = [p.sbuf(f"A1_o{i}", (128, 512), BF16) for i in range(2)]
    ps = [p.psum(f"A1_ps{i}", (128, 512), F32) for i in range(2)]

    projs = [("q", "L0_attn_Wq", "L0_attn_qcw", "L0_attn_qcb", "qs", AF.Silu),
             ("k", "L0_attn_Wk", "L0_attn_kcw", "L0_attn_kcb", "ks", AF.Silu),
             ("v", "L0_attn_Wv", "L0_attn_vcw", "L0_attn_vcb", "vs", AF.Silu),
             ("g", "L0_attn_Wg", None, None, "gsig", AF.Sigmoid)]
    wfree = [None, None]
    cwfree = [None, None]
    psfree = [None, None]
    cvfree = [None, None]
    otfree = [None, None]
    linfree = None
    n = 0
    for pi, (pname, wname, cwname, cbname, oname, act) in enumerate(projs):
        W = p.hbm[wname]
        dst = p.hbm[oname]
        for db in range(16):
            wp = n % 2
            lw_t = sc.dma(
                "gpsimd",
                wbuf[wp].ap().rearrange("p (kt c) -> p kt c", kt=NKT),
                W.rearrange("(kt p) n -> p kt n", p=128)[:, :, db * 128:(db + 1) * 128],
                waits=[wfree[wp]], inc=lw[wp])
            if cwname is not None:
                c1 = sc.dma("gpsimd", cwb[wp][:, 0:4],
                            p.hbm[cwname][db * 128:(db + 1) * 128, :],
                            waits=[cwfree[wp]], inc=lcw[wp])
                c2 = sc.dma("gpsimd", cwb[wp][:, 4:5],
                            p.hbm[cbname][db * 128:(db + 1) * 128][:, None],
                            inc=lcw[wp])
                cw_rdy = c2
            else:
                cw_rdy = None
            for bseq in range(2):
                if cwname is not None:
                    z = sc.emit("gpsimd", lambda e: e.memset(lin[:, 0:3], 0.0),
                                waits=[linfree], inc=gp)
                for tg in range(4):
                    pp = (n * 8 + bseq * 4 + tg) % 2
                    tok0 = bseq * S + tg * 512

                    def mmg(e, wp=wp, tok0=tok0, pp=pp):
                        for kt in range(NKT):
                            ins = e.matmul(ps[pp][:],
                                           wbuf[wp][:, kt * 128:(kt + 1) * 128],
                                           xn_sb[:, kt * T + tok0: kt * T + tok0 + 512],
                                           start=(kt == 0), stop=(kt == NKT - 1))
                        return ins
                    g = sc.emit("tensor", mmg,
                                waits=[lw_t, xn_rdy, psfree[pp]], inc=mm)
                    if cwname is not None:
                        ev = sc.emit("scalar", lambda e, tg=tg, pp=pp:
                                     e.activation(lin[:, 3 + tg * 512: 3 + (tg + 1) * 512],
                                                  ps[pp][:], AF.Copy),
                                     waits=[g, z], inc=se)
                        psfree[pp] = ev
                        t3 = sc.emit("vector", lambda e, tg=tg, pp=pp, wp=wp:
                                     e.tensor_scalar(cv[pp][:],
                                                     lin[:, 3 + tg * 512: 3 + tg * 512 + 512],
                                                     cwb[wp][:, 3:4], cwb[wp][:, 4:5],
                                                     ALU.mult, ALU.add),
                                     waits=[ev, cw_rdy, cvfree[pp]], inc=ve)
                        acc = t3
                        for j in (2, 1, 0):
                            acc = sc.emit("vector", lambda e, tg=tg, pp=pp, wp=wp, j=j:
                                          e.scalar_tensor_tensor(
                                              cv[pp][:],
                                              lin[:, j + tg * 512: j + tg * 512 + 512],
                                              cwb[wp][:, j:j + 1], cv[pp][:],
                                              ALU.mult, ALU.add),
                                          waits=[acc], inc=ve)
                        if tg == 3:
                            linfree = acc
                        a = sc.emit("scalar", lambda e, pp=pp, act=act:
                                    e.activation(ot[pp][:], cv[pp][:], act),
                                    waits=[acc, otfree[pp]], inc=se)
                        cvfree[pp] = a
                    else:
                        a = sc.emit("scalar", lambda e, pp=pp, act=act:
                                    e.activation(ot[pp][:], ps[pp][:], act),
                                    waits=[g, otfree[pp]], inc=se)
                        psfree[pp] = a
                    wr = sc.dma("sync",
                                dst[db * 128:(db + 1) * 128, tok0:tok0 + 512],
                                ot[pp][:], waits=[a], inc=st[pp])
                    otfree[pp] = wr
            wfree[wp] = g
            if cwname is not None:
                cwfree[wp] = t3
            n += 1

    wab = p.sbuf("A1_wab", (128, NKT * 16), BF16)
    bab = p.sbuf("A1_bab", (16, 1), F32)
    absb = [p.sbuf(f"A1_ab{i}", (16, 512), F32) for i in range(2)]
    psab = [p.psum(f"A1_psab{i}", (16, 512), F32) for i in range(2)]
    evp = [None, None]
    wrp = [None, None]
    ab_wfree = None
    ab_bfree = None
    for name, wname, bname, dname in (("a", "L0_attn_Wa", "L0_attn_ba", "alpha"),
                                      ("b", "L0_attn_Wb", "L0_attn_bb", "beta")):
        lw_t = None
        for kt in range(NKT):
            lw_t = sc.dma("gpsimd", wab[:, kt * 16:(kt + 1) * 16],
                          p.hbm[wname][kt * 128:(kt + 1) * 128, :],
                          waits=[ab_wfree if kt == 0 else None], inc=lab)
        lb = sc.dma("gpsimd", bab[:], p.hbm[bname][:, None],
                    waits=[ab_bfree], inc=lab)
        for tgg in range(8):
            pp = tgg % 2

            def mmab(e, tgg=tgg, pp=pp):
                for kt in range(NKT):
                    ins = e.matmul(psab[pp][:],
                                   wab[:, kt * 16:(kt + 1) * 16],
                                   xn_sb[:, kt * T + tgg * 512: kt * T + tgg * 512 + 512],
                                   start=(kt == 0), stop=(kt == NKT - 1))
                return ins
            g = sc.emit("tensor", mmab, waits=[lw_t, xn_rdy, evp[pp]], inc=mm)
            ev = sc.emit("scalar", lambda e, pp=pp:
                         e.activation(absb[pp][:], psab[pp][:], AF.Sigmoid,
                                      bias=bab[:]),
                         waits=[g, lb, wrp[pp]], inc=se)
            evp[pp] = ev
            wr = sc.dma("sync", p.hbm[dname][:, tgg * 512:(tgg + 1) * 512],
                        absb[pp][:], waits=[ev], inc=stab)
            wrp[pp] = wr
        ab_wfree = g
        ab_bfree = ev

    for nm in STAGE_OUTS["A1"]:
        p.ready[nm] = [(st[0], st[0].n), (st[1], st[1].n), (stab, stab.n)]
    p.finish(st[0])
    p.finish(st[1])
    p.finish(stab)



def stage_A2a(p: Prog):
    sc = p.sched
    xn_sb, xn_rdy = load_xn_res(p)
    wg, wu, hsd = p.hbm["L0_ffn_w_gate"], p.hbm["L0_ffn_w_up"], p.hbm["hs"]
    lgs = sc.sem("A2a_lg"); lus = sc.sem("A2a_lu")
    lwd = [sc.sem("A2a_lwd0"), sc.sem("A2a_lwd1")]
    mm = sc.sem("A2a_mm"); ve = sc.sem("A2a_v")
    se = sc.sem("A2a_s"); gp = sc.sem("A2a_gp")
    st = [sc.sem("A2a_st0"), sc.sem("A2a_st1")]
    swd = [sc.sem("A2a_swd0"), sc.sem("A2a_swd1")]
    NGB = 16
    wgb = p.sbuf("A2a_wg", (128, 4 * NKT * 128), BF16)
    wub = p.sbuf("A2a_wu", (128, 4 * NKT * 128), BF16)
    hg = [p.sbuf(f"A2a_hg{i}", (128, 512), F32) for i in range(2)]
    hh = [p.sbuf(f"A2a_hh{i}", (128, 4 * 512), BF16) for i in range(2)]
    sq = [p.sbuf(f"A2a_sq{i}", (128, 4 * 512), BF16) for i in range(2)]
    bcmp = [p.sbuf(f"A2a_b{i}", (128, 6 * 512), BF16) for i in range(2)]
    wt = [p.sbuf(f"A2a_wt{i}", (128, 4 * 512), BF16) for i in range(2)]
    psg = [p.psum(f"A2a_psg{i}", (128, 512), F32) for i in range(2)]
    psu = [p.psum(f"A2a_psu{i}", (128, 512), F32) for i in range(2)]

    wfree = None
    gfree = [[None, None], [None, None]]
    hgfree = [None, None]
    hhfree = [None, None]
    sqfree = [None, None]
    bfree = [None, None]
    wtfree = [None, None]
    PAIRS = [(0, 1), (0, 2), (0, 3), (1, 2), (1, 3), (2, 3)]
    for gb in range(NGB):
        lg = lu = None
        for m in range(4):
            col0 = (m * 2048 + gb * 128)
            lg = sc.dma("gpsimd",
                        wgb.ap().rearrange("p (m kt c) -> p (m kt) c", m=4, kt=NKT)[:, m * NKT:(m + 1) * NKT, :],
                        wg.rearrange("(kt p) n -> p kt n", p=128)[:, :, col0:col0 + 128],
                        waits=[wfree if m == 0 else None], inc=lgs)
            lu = sc.dma("gpsimd",
                        wub.ap().rearrange("p (m kt c) -> p (m kt) c", m=4, kt=NKT)[:, m * NKT:(m + 1) * NKT, :],
                        wu.rearrange("(kt p) n -> p kt n", p=128)[:, :, col0:col0 + 128],
                        inc=lus)
        for tg in range(8):
            pa = (gb * 8 + tg) % 2
            tok0 = tg * 512
            hdeps = []
            for m in range(4):
                gmm = sc.emit("tensor", lambda e, m=m, tok0=tok0, pa=pa:
                              mk_mm2(e, psg[pa], wgb[:, m * NKT * 128:(m + 1) * NKT * 128],
                                     xn_sb, tok0),
                              waits=[lg, xn_rdy, gfree[pa][0]], inc=mm)
                umm = sc.emit("tensor", lambda e, m=m, tok0=tok0, pa=pa:
                              mk_mm2(e, psu[pa], wub[:, m * NKT * 128:(m + 1) * NKT * 128],
                                     xn_sb, tok0),
                              waits=[lu, gfree[pa][1]], inc=mm)
                e1 = sc.emit("scalar", lambda e, pa=pa:
                             e.activation(hg[pa][:], psg[pa][:], AF.Silu),
                             waits=[gmm, hgfree[pa]], inc=se)
                gfree[pa][0] = e1
                e2 = sc.emit("vector", lambda e, pa=pa, m=m:
                             e.tensor_tensor(hh[pa][:, m * 512:(m + 1) * 512],
                                             hg[pa][:], psu[pa][:], ALU.mult),
                             waits=[umm, e1, hhfree[pa]], inc=ve)
                gfree[pa][1] = e2
                hgfree[pa] = e2
                hdeps.append(e2)
                s0 = sc.emit("gpsimd", lambda e, pa=pa, m=m:
                             e.tensor_tensor(sq[pa][:, m * 512:(m + 1) * 512],
                                             hh[pa][:, m * 512:(m + 1) * 512],
                                             hh[pa][:, m * 512:(m + 1) * 512], ALU.mult),
                             waits=[e2, sqfree[pa]], inc=gp)
                hdeps.append(s0)
            sqv = lambda m, pa: sq[pa][:, m * 512:(m + 1) * 512]
            cb = []
            for ci, (i, j) in enumerate(PAIRS):
                cb.append(sc.emit("vector", lambda e, pa=pa, ci=ci, i=i, j=j:
                          e.tensor_tensor(bcmp[pa][:, ci * 512:(ci + 1) * 512],
                                          sq[pa][:, i * 512:(i + 1) * 512],
                                          sq[pa][:, j * 512:(j + 1) * 512],
                                          ALU.is_ge),
                          waits=hdeps + [bfree[pa]], inc=ve))
            sqfree[pa] = cb[-1]
            B_ = lambda ci, pa=pa: bcmp[pa][:, ci * 512:(ci + 1) * 512]
            W_ = lambda i, pa=pa: wt[pa][:, i * 512:(i + 1) * 512]
            deps = cb + [wtfree[pa]]
            a0 = sc.emit("gpsimd", lambda e, pa=pa: e.tensor_add(
                wt[pa][:, 0:512], bcmp[pa][:, 0:512], bcmp[pa][:, 512:1024]),
                waits=deps, inc=gp)
            a1 = sc.emit("vector", lambda e, pa=pa: e.tensor_add(
                wt[pa][:, 512:1024], bcmp[pa][:, 3 * 512:4 * 512],
                bcmp[pa][:, 4 * 512:5 * 512]), waits=deps, inc=ve)
            a2 = sc.emit("gpsimd", lambda e, pa=pa: e.tensor_sub(
                wt[pa][:, 1024:1536], bcmp[pa][:, 5 * 512:6 * 512],
                bcmp[pa][:, 512:1024]), waits=deps, inc=gp)
            a3 = sc.emit("vector", lambda e, pa=pa: e.tensor_add(
                wt[pa][:, 1536:2048], bcmp[pa][:, 2 * 512:3 * 512],
                bcmp[pa][:, 4 * 512:5 * 512]), waits=deps, inc=ve)
            a0b = sc.emit("gpsimd", lambda e, pa=pa: e.tensor_add(
                wt[pa][:, 0:512], wt[pa][:, 0:512], bcmp[pa][:, 2 * 512:3 * 512]),
                waits=[a0], inc=gp)
            a1b = sc.emit("vector", lambda e, pa=pa: e.tensor_sub(
                wt[pa][:, 512:1024], wt[pa][:, 512:1024], bcmp[pa][:, 0:512]),
                waits=[a1], inc=ve)
            a2b = sc.emit("gpsimd", lambda e, pa=pa: e.tensor_sub(
                wt[pa][:, 1024:1536], wt[pa][:, 1024:1536], bcmp[pa][:, 3 * 512:4 * 512]),
                waits=[a2], inc=gp)
            a3b = sc.emit("vector", lambda e, pa=pa: e.tensor_add(
                wt[pa][:, 1536:2048], wt[pa][:, 1536:2048], bcmp[pa][:, 5 * 512:6 * 512]),
                waits=[a3], inc=ve)
            bfree[pa] = a3b
            thr = [(a0b, ALU.is_ge, 2.0, "gpsimd", gp), (a1b, ALU.is_ge, 1.0, "vector", ve),
                   (a2b, ALU.is_ge, 0.0, "gpsimd", gp), (a3b, ALU.is_le, 1.0, "vector", ve)]
            mdep = []
            for i, (dep, op, c, eng, sm) in enumerate(thr):
                mdep.append(sc.emit(eng, lambda e, pa=pa, i=i, op=op, c=c:
                            e.tensor_single_scalar(wt[pa][:, i * 512:(i + 1) * 512],
                                                   wt[pa][:, i * 512:(i + 1) * 512],
                                                   c, op),
                            waits=[dep], inc=sm))
            aps = []
            for i in range(4):
                eng, sm = ("vector", ve) if i % 2 == 0 else ("gpsimd", gp)
                aps.append(sc.emit(eng, lambda e, pa=pa, i=i:
                           e.tensor_tensor(hh[pa][:, i * 512:(i + 1) * 512],
                                           hh[pa][:, i * 512:(i + 1) * 512],
                                           wt[pa][:, i * 512:(i + 1) * 512], ALU.mult),
                           waits=[mdep[i]], inc=sm))
            wtfree[pa] = aps[3]
            wr = None
            for m in range(4):
                wr = sc.dma("sync",
                            hsd[m * 2048 + gb * 128: m * 2048 + gb * 128 + 128,
                                tok0:tok0 + 512],
                            hh[pa][:, m * 512:(m + 1) * 512],
                            waits=[aps[m]], inc=st[pa])
            hhfree[pa] = wr
        wfree = umm
    wdin, wdout = p.hbm["L0_ffn_w_down"], p.hbm["wd_bf"]
    wdb = [p.sbuf(f"A2a_wd{i}", (128, H // 2), BF16) for i in range(2)]
    wdfree = [None, None]
    for ib2 in range(II // 64):
        pa = ib2 % 2
        r0 = ib2 * 64
        l = sc.dma("gpsimd",
                   wdb[pa][:],
                   wdin.rearrange("(q p) c -> p q c", p=128)[:, ib2 // 2, :][:, (ib2 % 2) * 1024:(ib2 % 2) * 1024 + 1024],
                   waits=[wdfree[pa]], inc=lwd[pa])
        w = sc.dma("sync",
                   wdout.rearrange("(q p) c -> p q c", p=128)[:, ib2 // 2, :][:, (ib2 % 2) * 1024:(ib2 % 2) * 1024 + 1024],
                   wdb[pa][:],
                   waits=[l], inc=swd[pa])
        wdfree[pa] = w
    p.ready["hs"] = [(st[0], st[0].n), (st[1], st[1].n)]
    p.ready["wd_bf"] = [(swd[0], swd[0].n), (swd[1], swd[1].n)]
    p.finish(st[0])
    p.finish(st[1])
    p.finish(swd[0])
    p.finish(swd[1])


def mk_mm2(e, dst, src, xn_sb, tok0):
    for kt in range(NKT):
        ins = e.matmul(dst[:], src[:, kt * 128:(kt + 1) * 128],
                       xn_sb[:, kt * T + tok0: kt * T + tok0 + 512],
                       start=(kt == 0), stop=(kt == NKT - 1))
    return ins


def stage_A2b(p: Prog):
    sc = p.sched
    hsd, wdb, gfd = p.hbm["hs"], p.hbm["wd_bf"], p.hbm["gf"]
    lw = sc.sem("A2b_lw")
    lh = [sc.sem("A2b_lh0"), sc.sem("A2b_lh1")]
    mm = sc.sem("A2b_mm")
    se = sc.sem("A2b_s")
    st = [sc.sem("A2b_st0"), sc.sem("A2b_st1")]
    NIB = II // 128
    wd_res = p.sbuf("A2b_wd", (128, NIB * 1024), BF16)
    hcol = [p.sbuf(f"A2b_h{i}", (128, NIB * 128), BF16) for i in range(2)]
    gfb = [p.sbuf(f"A2b_gf{i}", (128, 1024), BF16) for i in range(2)]
    psf = [[p.psum(f"A2b_ps{i}_{j}", (128, 512), F32) for j in range(2)]
           for i in range(2)]
    wd_rdy = None
    hfree = [None, None]
    psfree = [[None, None], [None, None]]
    gffree = [None, None]
    for half in range(2):
        for ib in range(NIB):
            wd_rdy = sc.dma("gpsimd", wd_res[:, ib * 1024:(ib + 1) * 1024],
                            wdb[ib * 128:(ib + 1) * 128,
                                half * 1024:(half + 1) * 1024],
                            waits=[p.rdy("wd_bf"),
                                   psfree[0][1] if ib == 0 else None,
                                   psfree[1][1] if ib == 0 else None],
                            inc=lw)
        for tt in range(NTT):
            pa = tt % 2
            lc = sc.dma("sync", hcol[pa][:],
                        hsd.rearrange("(ib p) t -> p ib t", p=128)[:, :, tt * 128:(tt + 1) * 128],
                        waits=[p.rdy("hs"), hfree[pa]], inc=lh[pa])

            def mmd(e, pa=pa):
                for ib in range(NIB):
                    for fg in range(2):
                        ins = e.matmul(psf[pa][fg][:],
                                       hcol[pa][:, ib * 128:(ib + 1) * 128],
                                       wd_res[:, ib * 1024 + fg * 512: ib * 1024 + fg * 512 + 512],
                                       start=(ib == 0), stop=(ib == NIB - 1))
                return ins
            g = sc.emit("tensor", mmd,
                        waits=[wd_rdy, lc, psfree[pa][0], psfree[pa][1]], inc=mm)
            hfree[pa] = g
            ev = None
            for fg in range(2):
                ev = sc.emit("scalar", lambda e, pa=pa, fg=fg:
                             e.activation(gfb[pa][:, fg * 512:(fg + 1) * 512],
                                          psf[pa][fg][:], AF.Silu),
                             waits=[g, gffree[pa]], inc=se)
                psfree[pa][fg] = ev
            wr = sc.dma("sync",
                        gfd[tt * 128:(tt + 1) * 128, half * 1024:(half + 1) * 1024],
                        gfb[pa][:], waits=[ev], inc=st[pa])
            gffree[pa] = wr
    p.ready["gf"] = [(st[0], st[0].n), (st[1], st[1].n)]
    p.finish(st[0])
    p.finish(st[1])



B_LIMIT_CH = None
B_LIMIT_BH = None
B_PRO = 99
B_NO_MEMSET = False
B_INST = 99
INV_STAGES = 1

def stage_B(p: Prog):
    sc = p.sched
    nc = p.nc
    qs, ks, vs, gs = p.hbm["qs"], p.hbm["ks"], p.hbm["vs"], p.hbm["gsig"]
    ogd = p.hbm["og"]
    rows_d = nc.dram_tensor("B_rows", [NBH * NCH, 256], F32, kind="Internal").ap()
    gbd = nc.dram_tensor("B_gbd", [NBH * NCH], F32, kind="Internal").ap()
    gcd = nc.dram_tensor("B_gcd", [NBH * NCH], F32, kind="Internal").ap()

    lb1 = sc.sem("B_l1"); lb2 = sc.sem("B_l2"); lb3 = sc.sem("B_l3")
    ve = sc.sem("B_v"); se = sc.sem("B_s"); gp = sc.sem("B_g")
    mm = sc.sem("B_mm"); rxs = sc.sem("B_rx")
    st = [sc.sem(f"B_st{i}") for i in range(4)]
    ld = [sc.sem(f"B_ld{i}") for i in range(4)]
    lr = [sc.sem(f"B_lr{i}") for i in range(4)]
    PBbc = p.psum("B_ps_bc2", (128, 512), F32)
    PBkk = p.psum("B_ps_kk2", (128, 512), F32)
    PBinv = p.psum("B_ps_inv2", (128, 512), F32)
    PBsd = p.psum("B_ps_sd", (128, 512), F32)
    PBu = p.psum("B_ps_u2", (128, 512), F32)
    PBo = p.psum("B_ps_o2", (128, 512), F32)
    PBfa = p.psum("B_ps_fa", (128, 768), BF16)
    PBfb = p.psum("B_ps_fb", (128, 256), BF16)
    pstr = PBbc

    tA = p.sbuf("B_tA", (16, T), F32)
    tB = p.sbuf("B_tB", (16, T), F32)
    tL1 = p.sbuf("B_tL1", (16, T), F32)
    tL2 = p.sbuf("B_tL2", (16, T), F32)
    tG = p.sbuf("B_tG", (16, T), F32)
    tH = p.sbuf("B_tH", (16, T), F32)
    on16 = p.sbuf("B_on16", (16, S), F32)
    sm16 = p.sbuf("B_sm16", (16, 96), F32)
    al = lambda: tA[:]
    bt = lambda: tB[:]
    lna_ = lambda: tL1[:]
    lnb_ = lambda: tL2[:]
    G_ = lambda: tG[:]
    Hp_ = lambda: tH[:]

    l1 = sc.dma("sync", tA[:], p.hbm["alpha"], waits=[p.rdy("alpha")], inc=lb1)
    l2 = sc.dma("sync", tB[:], p.hbm["beta"], waits=[p.rdy("beta")], inc=lb2)
    e1 = sc.emit("scalar", lambda e: e.activation(lna_(), al(), AF.Ln),
                 waits=[l1], inc=se)
    e2 = sc.emit("scalar", lambda e: e.activation(lnb_(), bt(), AF.Ln),
                 waits=[l2], inc=se)
    z1 = sc.emit("gpsimd", lambda e: e.memset(on16[:], 1.0), inc=gp)
    g1 = sc.emit("vector", lambda e: e.tensor_tensor_scan(
        G_()[:, 0:S], lna_()[:, 0:S], on16[:], 0.0, ALU.add, ALU.mult),
        waits=[e1, z1], inc=ve)
    g2 = sc.emit("vector", lambda e: e.tensor_tensor_scan(
        G_()[:, S:T], lna_()[:, S:T], on16[:], 0.0, ALU.add, ALU.mult),
        waits=[e1, z1], inc=ve)
    h1 = sc.emit("vector", lambda e: e.scalar_tensor_tensor(
        Hp_(), G_(), 2.0 * LNC, lna_(), ALU.add, ALU.subtract),
        waits=[g1, g2, e2], inc=ve)
    h2 = sc.emit("vector", lambda e: e.tensor_tensor(Hp_(), Hp_(), lnb_(), ALU.add),
                 waits=[h1], inc=ve)

    if B_PRO < 2:
        p.finish(st[0])
        return
    idmf = p.sbuf("B_idmf", (128, 128), F32)
    onesf = p.sbuf("B_onesf", (128, 128), F32)
    zi = sc.emit("gpsimd", lambda e: e.memset(onesf[:], 1.0), inc=gp)
    idr = sc.emit("gpsimd", lambda e: e.affine_select(
        idmf[:], onesf[:], [[1, 128]], ALU.is_equal, 0.0, base=0,
        channel_multiplier=-1), waits=[zi], inc=gp)
    onesb = p.consts["ones"]
    idm = p.consts["idm"]
    cmask = p.sbuf("B_cmask", (128, 256), F32)
    zt128 = p.sbuf("B_zt", (128, 128), F32)
    zn = sc.emit("gpsimd", lambda e: e.memset(zt128[:], 0.0), inc=gp)
    mk1 = sc.emit("gpsimd", lambda e: e.affine_select(
        cmask[:, 0:128], zt128[:], [[1, 128]], ALU.is_gt, -1e30, base=0,
        channel_multiplier=-1), waits=[zn], inc=gp)
    mk2 = sc.emit("gpsimd", lambda e: e.affine_select(
        cmask[:, 128:256], zt128[:], [[1, 128]], ALU.is_ge, -1e30, base=0,
        channel_multiplier=-1), waits=[zn], inc=gp)

    posNG = p.sbuf("B_posNG", (128, 512), F32)
    posB = p.sbuf("B_posB", (128, 512), F32)
    posBGC = p.sbuf("B_posBGC", (128, 512), F32)

    def transpose_seq(srcfn, dstT, deps):
        done = []
        for b_ in range(2):
            for blk in range(NCH):
                col = (b_ * 16 + blk) * 16
                tr = sc.emit("tensor", lambda e, b_=b_, blk=blk, srcfn=srcfn:
                             e.transpose(pstr[:, (blk % 4) * 128:(blk % 4) * 128 + 16],
                                         srcfn()[:, b_ * S + blk * 128: b_ * S + blk * 128 + 128],
                                         idmf[0:16, 0:16]),
                             waits=list(deps) + [idr,
                                    done[-1] if done else None],
                             inc=mm)
                ev = sc.emit("vector" if (blk % 2 == 0) else "scalar",
                             lambda e, col=col, blk=blk, dstT=dstT:
                             (e.tensor_copy(dstT[:, col:col + 16],
                                            pstr[:, (blk % 4) * 128:(blk % 4) * 128 + 16])
                              if blk % 2 == 0 else
                              e.activation(dstT[:, col:col + 16],
                                           pstr[:, (blk % 4) * 128:(blk % 4) * 128 + 16],
                                           AF.Copy)),
                             waits=[tr], inc=(ve if blk % 2 == 0 else se))
                done.append(ev)
        return done[-1], done[-2]

    if B_PRO < 3:
        p.finish(st[0])
        return
    tb1, tb2 = transpose_seq(bt, posB, [l2])
    if B_PRO < 4:
        p.finish(st[0])
        return
    gpv = sc.emit("vector", lambda e: e.tensor_scalar_add(lna_(), G_(), LNC),
                  waits=[h1], inc=ve)
    ngv = sc.emit("vector", lambda e: e.tensor_scalar_mul(al(), G_(), -1.0),
                  waits=[g1, g2, e1], inc=ve)
    tn1, tn2 = transpose_seq(al, posNG, [ngv])
    if B_PRO < 5:
        p.finish(st[0])
        return
    b1 = sc.emit("scalar", lambda e: e.activation(bt(), Hp_(), AF.Exp),
                 waits=[h2, tb1, tb2], inc=se)
    bgcv = sc.emit("vector", lambda e: e.tensor_scalar_mul(
        bt(), bt(), -float(np.exp(-LNC))), waits=[b1], inc=ve)
    tg1, tg2 = transpose_seq(bt, posBGC, [bgcv])

    if B_PRO < 6:
        p.finish(st[0])
        return
    zb = sc.emit("gpsimd", lambda e: e.memset(sm16[:, 0:32], 0.0), inc=gp)
    c1 = sc.emit("vector", lambda e: e.tensor_copy(
        sm16[:, 1:16], G_()[:, 127:2047:128]), waits=[g1, zb], inc=ve)
    c2 = sc.emit("vector", lambda e: e.tensor_copy(
        sm16[:, 17:32], G_()[:, S + 127:S + 2047:128]), waits=[g2, zb], inc=ve)
    c3 = sc.emit("vector", lambda e: e.tensor_copy(
        sm16[:, 32:48], G_()[:, 127:2048:128]), waits=[g1], inc=ve)
    c4 = sc.emit("vector", lambda e: e.tensor_copy(
        sm16[:, 48:64], G_()[:, S + 127:S + 2048:128]), waits=[g2], inc=ve)
    c5 = sc.emit("vector", lambda e: e.tensor_tensor(
        sm16[:, 64:96], sm16[:, 32:64], sm16[:, 0:32], ALU.subtract),
        waits=[c1, c2, c3, c4], inc=ve)
    c6 = sc.emit("scalar", lambda e: e.activation(sm16[:, 64:96], sm16[:, 64:96],
                                                  AF.Exp), waits=[c5], inc=se)
    rd = rows_d.rearrange("(b h c) w -> h b c w", b=2, h=16)
    for b_ in range(2):
        w1 = sc.dma("sync", rd[:, b_, :, 0:128],
                    Hp_()[:, b_ * S:(b_ + 1) * S].rearrange("h (c w) -> h c w", c=NCH),
                    waits=[h2], inc=rxs)
        w2 = sc.dma("sync", rd[:, b_, :, 128:256],
                    lna_()[:, b_ * S:(b_ + 1) * S].rearrange("h (c w) -> h c w", c=NCH),
                    waits=[gpv], inc=rxs)
        w6 = sc.dma("sync", gcd.rearrange("(b h c) -> h b c", b=2, h=16)[:, b_, :],
                    sm16[:, 64 + b_ * 16:64 + (b_ + 1) * 16],
                    waits=[c6], inc=rxs,
                    allow_slow_non_contiguous=True)
        w7 = sc.dma("sync", gbd.rearrange("(b h c) -> h b c", b=2, h=16)[:, b_, :],
                    sm16[:, b_ * 16:(b_ + 1) * 16],
                    waits=[zb, c1, c2], inc=rxs,
                    allow_slow_non_contiguous=True)

    if B_PRO < 7:
        p.finish(st[0])
        return
    rows_ready = (rxs, rxs.n)
    gcrow = p.sbuf("B_gcrow", (1, 512), F32)
    ones1f = p.sbuf("B_ones1f", (1, 128), F32)
    gC_pos = p.sbuf("B_gCpos", (128, 512), F32)
    zo = sc.emit("gpsimd", lambda e: e.memset(ones1f[:], 1.0), inc=gp)
    lg = sc.dma("sync", gcrow[:], gcd[None, :], waits=[rows_ready], inc=lb3)
    lb3_all = (lb3, lb3.n)
    bc1 = sc.emit("tensor", lambda e: e.matmul(pstr[:], ones1f[:], gcrow[:],
                                               start=True, stop=True),
                  waits=[zo, lb3_all, tg1, tg2], inc=mm)
    bc1e = sc.emit("scalar", lambda e: e.activation(gC_pos[:], pstr[:], AF.Copy),
                   waits=[bc1], inc=se)
    gB_neg = p.sbuf("B_gBneg", (128, 512), F32)
    lb5 = sc.sem("B_l5")
    lgb = sc.dma("sync", gcrow[:], gbd[None, :], waits=[rows_ready, bc1], inc=lb5)
    bc2 = sc.emit("tensor", lambda e: e.matmul(pstr[:], ones1f[:], gcrow[:],
                                               start=True, stop=True),
                  waits=[lgb, bc1e], inc=mm)
    bc2e = sc.emit("scalar", lambda e: e.activation(gB_neg[:], pstr[:], AF.Copy,
                                                    scale=-1.0),
                   waits=[bc2], inc=se)

    if B_PRO < 8:
        p.finish(st[0])
        return
    STf_t = p.sbuf("B_STf", (128, NBH * 128), F32)
    STb_t = p.sbuf("B_STb", (128, NBH * 128), BF16)
    STf = [STf_t[:, i * 128:(i + 1) * 128] for i in range(NBH)]
    STb = [STb_t[:, i * 128:(i + 1) * 128] for i in range(NBH)]
    if B_NO_MEMSET:
        stz = [None for _ in range(NBH)]
    else:
        z1_ = sc.emit("gpsimd", lambda e: e.memset(STf_t[:], 0.0), inc=gp)
        z2_ = sc.emit("gpsimd", lambda e: e.memset(STb_t[:], 0.0),
                      waits=[z1_], inc=gp)
        stz = [z2_ for _ in range(NBH)]


    DEP = 4
    qt = [p.sbuf(f"B_q{i}", (128, 128), BF16) for i in range(DEP)]
    kt = [p.sbuf(f"B_k{i}", (128, 128), BF16) for i in range(DEP)]
    vt = [p.sbuf(f"B_v{i}", (128, 128), BF16) for i in range(DEP)]
    gt = [p.sbuf(f"B_gt{i}", (128, 128), BF16) for i in range(DEP)]
    rowb = [p.sbuf(f"B_row{i}", (1, 256), F32) for i in range(DEP)]
    bci = [p.sbuf(f"B_bci{i}", (128, 256), F32) for i in range(DEP)]
    dm = [p.sbuf(f"B_dm{i}", (128, 256), F32) for i in range(DEP)]
    kkb = [p.sbuf(f"B_kkb{i}", (128, 128), BF16) for i in range(DEP)]
    kqb = [p.sbuf(f"B_kqb{i}", (128, 128), BF16) for i in range(DEP)]
    ndev = [p.sbuf(f"B_ndev{i}", (128, 128), BF16) for i in range(DEP)]
    nmath = [p.sbuf(f"B_nmath{i}", (128, 128), BF16) for i in range(DEP)]
    atb = [p.sbuf(f"B_atb{i}", (128, 128), BF16) for i in range(DEP)]
    ipn = [p.sbuf(f"B_ipn{i}", (128, 128), BF16) for i in range(DEP)]
    pdev = [[p.sbuf(f"B_pdev{i}_{j}", (128, 128), BF16) for j in range(2)]
            for i in range(DEP)]
    pmath = [[p.sbuf(f"B_pmath{i}_{j}", (128, 128), BF16) for j in range(2)]
             for i in range(DEP)]
    sdev = [[p.sbuf(f"B_sdev{i}_{j}", (128, 128), BF16) for j in range(2)]
            for i in range(DEP)]
    ktm = [p.sbuf(f"B_ktm{i}", (128, 128), BF16) for i in range(DEP)]
    kw = [p.sbuf(f"B_kw{i}", (128, 128), BF16) for i in range(DEP)]
    bv = [p.sbuf(f"B_bv{i}", (128, 128), BF16) for i in range(DEP)]
    wtb = [p.sbuf(f"B_wtb{i}", (128, 128), BF16) for i in range(DEP)]
    ub = [p.sbuf(f"B_ub{i}", (128, 128), BF16) for i in range(DEP)]
    udec = [p.sbuf(f"B_udec{i}", (128, 128), BF16) for i in range(DEP)]
    gcol = [p.sbuf(f"B_gcol{i}", (128, 1), F32) for i in range(DEP)]
    psc = [p.sbuf(f"B_psc{i}", (128, 128), F32) for i in range(DEP)]
    osb = [p.sbuf(f"B_osb{i}", (128, 128), F32) for i in range(DEP)]
    oc_ = [p.sbuf(f"B_oc{i}", (128, 128), F32) for i in range(DEP)]
    sqs = [p.sbuf(f"B_sqs{i}", (128, 128), F32) for i in range(DEP)]
    stt_ = [p.sbuf(f"B_stt{i}", (128, 4), F32) for i in range(DEP)]
    olnb = [p.sbuf(f"B_oln{i}", (128, 128), BF16) for i in range(DEP)]
    ogb = [p.sbuf(f"B_og{i}", (128, 128), BF16) for i in range(DEP)]
    lnw_col = p.sbuf("B_lnwc", (128, 1), F32)
    lnb_col = p.sbuf("B_lnbc", (128, 1), F32)
    lb4 = sc.sem("B_l4")
    lnc1 = sc.dma("sync", lnw_col[:], p.hbm["L0_attn_ln_w"][:, None], inc=lb4)
    lnc2 = sc.dma("sync", lnb_col[:], p.hbm["L0_attn_ln_b"][:, None], inc=lb4)
    lnc_all = (lb4, lb4.n)

    prev = {}
    prev["bc_free"] = bc2e
    V = [None] * (NCH * NBH)

    def phase1(i):
        c, bh = divmod(i, NBH)
        b_, h_ = bh // 16, bh % 16
        pa = i % DEP
        bq = (i % 2) * 256
        t0 = b_ * S + c * 128
        row_i = (b_ * 16 + h_) * 16 + c
        col = bh * 16 + c
        colT = (b_ * 16 + c) * 16 + h_
        d = {}
        sc.dma("gpsimd", qt[pa][:], qs[h_ * 128:(h_ + 1) * 128, t0:t0 + 128],
               waits=[p.rdy("qs"), prev.get(f"ld_free{pa}")], inc=ld[pa])
        sc.dma("gpsimd", kt[pa][:], ks[h_ * 128:(h_ + 1) * 128, t0:t0 + 128],
               waits=[p.rdy("ks")], inc=ld[pa])
        sc.dma("gpsimd", vt[pa][:], vs[h_ * 128:(h_ + 1) * 128, t0:t0 + 128],
               waits=[p.rdy("vs")], inc=ld[pa])
        d["lg"] = sc.dma("gpsimd", gt[pa][:],
                         gs[h_ * 128:(h_ + 1) * 128, t0:t0 + 128],
                         waits=[p.rdy("gsig")], inc=ld[pa])
        d["lq"] = d["lk"] = d["lv"] = d["lg"]
        l_r = sc.dma("scalar", rowb[pa][:], rows_d[row_i, 0:256][None, :],
                     waits=[rows_ready, prev.get(f"lr_free{pa}")], inc=lr[pa])
        bcm = sc.emit("tensor", lambda e, pa=pa: e.matmul(
            PBbc[:, bq:bq + 256], ones1f[:], rowb[pa][:], start=True, stop=True),
            waits=[l_r, zo, prev.get("bc_free")], inc=mm)
        prev[f"lr_free{pa}"] = bcm
        badd = sc.emit("vector", lambda e, pa=pa: e.tensor_tensor(
            bci[pa][:], PBbc[:, bq:bq + 256], cmask[:], ALU.add),
            waits=[bcm, mk1, mk2, prev.get(f"bci_free{pa}")], inc=ve)
        prev["bc_free"] = badd
        d["ex1"] = sc.emit("scalar", lambda e, pa=pa, colT=(b_ * 16 + c) * 16 + h_:
                           e.activation(dm[pa][:], bci[pa][:], AF.Exp,
                                        bias=posNG[:, colT:colT + 1]),
                           waits=[badd, tn1, tn2, prev.get(f"dm_free{pa}")],
                           inc=se)
        prev[f"bci_free{pa}"] = d["ex1"]
        d["gcr"] = sc.emit("scalar", lambda e, pa=pa, colT=colT, col=col:
                           e.activation(gcol[pa][:], posNG[:, colT:colT + 1],
                                        AF.Exp, scale=-1.0,
                                        bias=gB_neg[:, col:col + 1]),
                           waits=[tn1, tn2, bc2e,
                                  prev.get(f"gcol_free{pa}")], inc=se)
        kkm = sc.emit("tensor", lambda e, pa=pa, bq=bq: e.matmul(
            PBkk[:, bq:bq + 128], kt[pa][:], kt[pa][:], start=True, stop=True),
            waits=[d["lk"], prev.get(f"kk_free{i % 2}"),
                   prev.get(f"kk_free2{i % 2}")], inc=mm)
        kqm = sc.emit("tensor", lambda e, pa=pa, bq=bq: e.matmul(
            PBkk[:, bq + 128:bq + 256], kt[pa][:], qt[pa][:], start=True,
            stop=True),
            waits=[d["lq"], kkm], inc=mm)
        kke = sc.emit("scalar", lambda e, pa=pa, bq=bq: e.activation(
            kkb[pa][:], PBkk[:, bq:bq + 128], AF.Copy, scale=-1.0),
            waits=[kqm, prev.get(f"kkb_free{pa}")], inc=se)
        kqe = sc.emit("scalar", lambda e, pa=pa, bq=bq: e.activation(
            kqb[pa][:], PBkk[:, bq + 128:bq + 256], AF.Copy),
            waits=[kqm, prev.get(f"kqb_free{pa}")], inc=se)
        prev[f"kk_free{i % 2}"] = kqe
        prev[f"kk_free2{i % 2}"] = kke
        d["kke"], d["kqe"] = kke, kqe
        return d

    def phase2(i, d):
        pa = i % DEP
        qq = i % 2
        n1 = sc.emit("gpsimd", lambda e, pa=pa: e.tensor_tensor(
            ndev[pa][:], dm[pa][:, 0:128], kkb[pa][:], ALU.mult),
            waits=[d["ex1"], d["kke"], prev.get(f"ndev_free{pa}")], inc=gp)
        d["at2"] = sc.emit("gpsimd", lambda e, pa=pa: e.tensor_tensor(
            atb[pa][:], dm[pa][:, 128:256], kqb[pa][:], ALU.mult),
            waits=[d["ex1"], d["kqe"], prev.get(f"atb_free{pa}")], inc=gp)
        prev[f"dm_free{pa}"] = d["at2"]
        prev[f"kkb_free{pa}"] = n1
        prev[f"kqb_free{pa}"] = d["at2"]
        ntr = sc.emit("tensor", lambda e, pa=pa, qq=qq: e.transpose(
            PBfa[:, 256 + qq * 384:384 + qq * 384], ndev[pa][:], idm[:]),
            waits=[n1, p.consts["idm_rdy"], prev.get(f"fa3_free{qq}")], inc=mm)
        n2 = sc.emit("vector", lambda e, pa=pa, qq=qq: e.tensor_copy(
            nmath[pa][:], PBfa[:, 256 + qq * 384:384 + qq * 384]),
            waits=[ntr, prev.get(f"nmath_free{pa}")], inc=ve)
        prev[f"fa3_free{qq}"] = n2
        ip = sc.emit("gpsimd", lambda e, pa=pa: e.tensor_tensor(
            ipn[pa][:], ndev[pa][:], idm[:], ALU.add),
            waits=[n1, p.consts["idm_rdy"], prev.get(f"ipn_free{pa}")], inc=gp)
        scur, sdep = ipn[pa], ip
        cd, cm = ndev[pa], nmath[pa]
        cd_dep, cm_dep = n1, n2
        supd = None
        for k in range(1, INV_STAGES + 1):
            pd, pm = pdev[pa][k % 2], pmath[pa][k % 2]
            d1 = sc.emit("tensor", lambda e, cd=cd, cm=cm, qq=qq: e.matmul(
                PBinv[:, qq * 256:qq * 256 + 128], cm[:], cd[:], start=True,
                stop=True),
                waits=[cd_dep, cm_dep, prev.get(f"inv_free{qq}"),
                       prev.get(f"inv_free2{qq}")], inc=mm) \
                if k <= INV_STAGES - 1 else None
            d2 = sc.emit("tensor", lambda e, cd=cd, cm=cm, qq=qq: e.matmul(
                PBinv[:, qq * 256 + 128:qq * 256 + 256], cd[:], cm[:],
                start=True, stop=True),
                waits=[cd_dep, cm_dep, prev.get(f"inv_free{qq}"),
                       prev.get(f"inv_free2{qq}")], inc=mm)
            ieng, isem = (("vector", ve) if k % 2 else ("scalar", se))
            pde = sc.emit(ieng, lambda e, pd=pd, qq=qq: (
                e.tensor_copy(pd[:], PBinv[:, qq * 256:qq * 256 + 128])
                if e.__class__.__name__ != "BassScalarEngine"
                else e.activation(pd[:], PBinv[:, qq * 256:qq * 256 + 128],
                                  AF.Copy)),
                waits=[d1, d2, prev.get(f"pd{pa}{k%2}_free")], inc=isem) \
                if d1 is not None else None
            pme = sc.emit(ieng, lambda e, pm=pm, qq=qq: (
                e.tensor_copy(pm[:], PBinv[:, qq * 256 + 128:qq * 256 + 256])
                if e.__class__.__name__ != "BassScalarEngine"
                else e.activation(pm[:], PBinv[:, qq * 256 + 128:qq * 256 + 256],
                                  AF.Copy)),
                waits=[d2, prev.get(f"pm{pa}{k%2}_free")], inc=isem)
            prev[f"inv_free{qq}"] = pme
            prev[f"inv_free2{qq}"] = pde
            supd = sc.emit("tensor", lambda e, pm=pm, scur=scur, qq=qq: e.matmul(
                PBsd[:, qq * 256:qq * 256 + 128], pm[:], scur[:], start=True,
                stop=True),
                waits=[pme, sdep, prev.get(f"S_free{qq}")], inc=mm)
            prev[f"pd{pa}{k%2}_free"] = supd if pde is None else None
            prev[f"pm{pa}{k%2}_free"] = supd
            snew = sdev[pa][k % 2]
            sadd = sc.emit("vector", lambda e, snew=snew, scur=scur, qq=qq:
                           e.tensor_tensor(snew[:], PBsd[:, qq * 256:qq * 256 + 128],
                                           scur[:], ALU.add),
                           waits=[supd, sdep, prev.get(f"sdev{pa}{k%2}_free")],
                           inc=ve)
            prev[f"S_free{qq}"] = sadd
            cd, cm = pd, pm
            cd_dep, cm_dep = pde, pme
            scur, sdep = snew, sadd
        prev[f"ipn_free{pa}"] = sdep
        prev[f"ndev_free{pa}"] = sdep
        prev[f"nmath_free{pa}"] = sdep
        d["TT"], d["tt_dep"], d["supd"] = scur, sdep, supd
        return d

    def phase3(i, d):
        c, bh = divmod(i, NBH)
        pa = i % DEP
        qq = i % 2
        TT, tt_dep = d["TT"], d["tt_dep"]
        colT = ((bh // 16) * 16 + c) * 16 + (bh % 16)
        fq = qq * 384
        trk = sc.emit("tensor", lambda e, pa=pa, fq=fq: e.transpose(
            PBfa[:, fq:fq + 128], kt[pa][:], idm[:]),
            waits=[d["lk"], p.consts["idm_rdy"], prev.get(f"fa1_free{qq}")],
            inc=mm)
        trv = sc.emit("tensor", lambda e, pa=pa, fq=fq: e.transpose(
            PBfa[:, fq + 128:fq + 256], vt[pa][:], idm[:]),
            waits=[d["lv"], trk, prev.get(f"fa2_free{qq}")], inc=mm)
        kte = sc.emit("vector", lambda e, pa=pa, fq=fq: e.tensor_copy(
            ktm[pa][:], PBfa[:, fq:fq + 128]),
            waits=[trv, prev.get(f"ktm_free{pa}")], inc=ve)
        kwe = sc.emit("vector", lambda e, pa=pa, colT=colT, fq=fq: e.tensor_scalar(
            kw[pa][:], PBfa[:, fq:fq + 128], posBGC[:, colT:colT + 1], None,
            ALU.mult),
            waits=[trv, tg1, tg2, prev.get(f"kw_free{pa}")], inc=ve)
        bve = sc.emit("vector", lambda e, pa=pa, colT=colT, fq=fq: e.tensor_scalar(
            bv[pa][:], PBfa[:, fq + 128:fq + 256], posB[:, colT:colT + 1], None,
            ALU.mult),
            waits=[trv, tb1, tb2, prev.get(f"bv_free{pa}")], inc=ve)
        prev[f"fa1_free{qq}"] = kwe
        prev[f"fa2_free{qq}"] = bve
        uq = qq * 256
        wtm = sc.emit("tensor", lambda e, pa=pa, TT=TT, uq=uq: e.matmul(
            PBu[:, uq + 128:uq + 256], kw[pa][:], TT[:], start=True, stop=True),
            waits=[kwe, tt_dep, prev.get(f"U_free{qq}"),
                   prev.get(f"U_free2{qq}")], inc=mm)
        wte = sc.emit("vector", lambda e, pa=pa, uq=uq: e.tensor_copy(
            wtb[pa][:], PBu[:, uq + 128:uq + 256]),
            waits=[wtm, prev.get(f"wtb_free{pa}")], inc=ve)
        prev[f"kw_free{pa}"] = wtm
        um1 = sc.emit("tensor", lambda e, pa=pa, TT=TT, uq=uq: e.matmul(
            PBu[:, uq:uq + 128], TT[:], bv[pa][:], start=True, stop=False),
            waits=[tt_dep, bve, wte], inc=mm)
        um2 = sc.emit("tensor", lambda e, pa=pa, bh=bh, uq=uq: e.matmul(
            PBu[:, uq:uq + 128], wtb[pa][:], STb[bh], start=False, stop=True),
            waits=[um1, wte, stz[bh], prev.get(f"stb{bh}")], inc=mm)
        prev[f"sdev{pa}{INV_STAGES % 2}_free"] = um1
        prev[f"sdev{pa}{(INV_STAGES - 1) % 2}_free"] = d["supd"]
        prev[f"bv_free{pa}"] = um1
        prev[f"wtb_free{pa}"] = um2
        ue = sc.emit("scalar", lambda e, pa=pa, uq=uq: e.activation(
            ub[pa][:], PBu[:, uq:uq + 128], AF.Copy),
            waits=[um2, prev.get(f"ub_free{pa}")], inc=se)
        ude = sc.emit("vector", lambda e, pa=pa, uq=uq: e.tensor_scalar(
            udec[pa][:], PBu[:, uq:uq + 128], dm[pa][:, 127:128], None,
            ALU.mult),
            waits=[um2, ue, d["ex1"], prev.get(f"udec_free{pa}")], inc=ve)
        prev[f"U_free{qq}"] = ude
        prev[f"U_free2{qq}"] = ue
        om1 = sc.emit("tensor", lambda e, pa=pa, uq=uq: e.matmul(
            PBo[:, uq:uq + 128], atb[pa][:], ub[pa][:], start=True, stop=True),
            waits=[d["at2"], ue, prev.get(f"O_free{qq}")], inc=mm)
        om2 = sc.emit("tensor", lambda e, pa=pa, bh=bh, uq=uq: e.matmul(
            PBo[:, uq + 128:uq + 256], qt[pa][:], STb[bh], start=True,
            stop=True),
            waits=[d["lq"], om1, stz[bh], prev.get(f"stb{bh}")], inc=mm)
        pse = sc.emit("scalar", lambda e, pa=pa, uq=uq: e.activation(
            psc[pa][:], PBo[:, uq + 128:uq + 256], AF.Copy,
            scale=gcol[pa][:, 0:1]),
            waits=[om2, d["gcr"], prev.get(f"psc_free{pa}")], inc=se)
        prev[f"gcol_free{pa}"] = pse
        prev[f"atb_free{pa}"] = om1
        prev[f"ub_free{pa}"] = om1
        oad = sc.emit("vector", lambda e, pa=pa, uq=uq: e.tensor_tensor(
            osb[pa][:], PBo[:, uq:uq + 128], psc[pa][:], ALU.add),
            waits=[pse, om1, prev.get(f"osb_free{pa}")], inc=ve)
        prev[f"O_free{qq}"] = oad
        prev[f"psc_free{pa}"] = oad
        d.update(kte=kte, ude=ude, um2=um2, om2=om2, oad=oad)
        return d

    def phase4(i, d):
        c, bh = divmod(i, NBH)
        b_, h_ = bh // 16, bh % 16
        pa = i % DEP
        qq = i % 2
        sq_ = qq * 256
        t0 = b_ * S + c * 128
        col = bh * 16 + c
        dsm = sc.emit("tensor", lambda e, pa=pa, sq_=sq_: e.matmul(
            PBsd[:, sq_ + 128:sq_ + 256], ktm[pa][:], udec[pa][:], start=True,
            stop=True),
            waits=[d["kte"], d["ude"], prev.get(f"dst_free{qq}")], inc=mm)
        stu = sc.emit("vector", lambda e, pa=pa, bh=bh, col=col, sq_=sq_:
                      e.scalar_tensor_tensor(
                          STf[bh], STf[bh], gC_pos[:, col:col + 1],
                          PBsd[:, sq_ + 128:sq_ + 256], ALU.mult, ALU.add),
                      waits=[dsm, bc1e, stz[bh], d["um2"], d["om2"]], inc=ve)
        stbu = sc.emit("gpsimd", lambda e, bh=bh: e.tensor_copy(
            STb[bh], STf[bh]), waits=[stu], inc=gp)
        prev[f"stb{bh}"] = stbu
        prev[f"dst_free{qq}"] = stu
        prev[f"ktm_free{pa}"] = dsm
        prev[f"udec_free{pa}"] = dsm
        rs1 = sc.emit("vector", lambda e, pa=pa: e.reduce_sum(
            stt_[pa][:, 0:1], osb[pa][:], axis=AX.X),
            waits=[d["oad"]], inc=ve)
        ctr = sc.emit("vector", lambda e, pa=pa: e.tensor_scalar(
            oc_[pa][:], osb[pa][:], float(D), stt_[pa][:, 0:1],
            ALU.mult, ALU.subtract),
            waits=[rs1, prev.get(f"oc_free{pa}")], inc=ve)
        prev[f"osb_free{pa}"] = ctr
        sqa = sc.emit("scalar", lambda e, pa=pa: e.activation(
            sqs[pa][:], oc_[pa][:], AF.Square, accum_out=stt_[pa][:, 2:3]),
            waits=[ctr], inc=se)
        t1_ = sc.emit("vector", lambda e, pa=pa: e.tensor_scalar(
            stt_[pa][:, 3:4], stt_[pa][:, 2:3], 1.0 / D,
            float(D * D) * 1e-5, ALU.mult, ALU.add),
            waits=[sqa], inc=ve)
        ln2 = sc.emit("scalar", lambda e, pa=pa: e.activation(
            stt_[pa][:, 2:3], stt_[pa][:, 3:4], AF.Ln),
            waits=[t1_], inc=se)
        rse = sc.emit("scalar", lambda e, pa=pa: e.activation(
            stt_[pa][:, 0:1], stt_[pa][:, 2:3], AF.Exp, scale=-0.5),
            waits=[ln2], inc=se)
        nrm = sc.emit("vector", lambda e, pa=pa: e.tensor_scalar(
            olnb[pa][:], oc_[pa][:], stt_[pa][:, 0:1], None, ALU.mult),
            waits=[rse, prev.get(f"olnb_free{pa}")], inc=ve)
        prev[f"oc_free{pa}"] = nrm
        tro = sc.emit("tensor", lambda e, pa=pa, qq=qq: e.transpose(
            PBfb[:, qq * 128:qq * 128 + 128], olnb[pa][:], idm[:]),
            waits=[nrm, prev.get(f"fb_free{qq}")], inc=mm)
        prev[f"olnb_free{pa}"] = tro
        og1 = sc.emit("vector", lambda e, pa=pa, qq=qq: e.tensor_scalar(
            ogb[pa][:], PBfb[:, qq * 128:qq * 128 + 128], lnw_col[:, 0:1],
            lnb_col[:, 0:1], ALU.mult, ALU.add),
            waits=[tro, lnc_all, prev.get(f"og_free{pa}")], inc=ve)
        prev[f"fb_free{qq}"] = og1
        oge = sc.emit("vector", lambda e, pa=pa: e.tensor_tensor(
            ogb[pa][:], ogb[pa][:], gt[pa][:], ALU.mult),
            waits=[og1, d["lg"]], inc=ve)
        prev[f"ld_free{pa}"] = oge
        wr = sc.dma("sync", ogd[h_ * 128:(h_ + 1) * 128, t0:t0 + 128],
                    ogb[pa][:], waits=[oge], inc=st[pa])
        prev[f"og_free{pa}"] = wr

    NI = (B_LIMIT_CH if B_LIMIT_CH is not None else NCH) * \
         (B_LIMIT_BH if B_LIMIT_BH is not None else NBH)
    if B_LIMIT_BH is not None and B_LIMIT_BH != NBH:
        NI = (B_LIMIT_CH if B_LIMIT_CH is not None else NCH) * B_LIMIT_BH
    insts = []
    for c in range(B_LIMIT_CH if B_LIMIT_CH is not None else NCH):
        for bh in range(B_LIMIT_BH if B_LIMIT_BH is not None else NBH):
            insts.append(c * NBH + bh)
    dstate = {}
    for t in range(len(insts) + 3):
        if t < len(insts):
            dstate[insts[t]] = phase1(insts[t])
        if 1 <= t < len(insts) + 1:
            i = insts[t - 1]
            dstate[i] = phase2(i, dstate[i])
        if 2 <= t < len(insts) + 2:
            i = insts[t - 2]
            dstate[i] = phase3(i, dstate[i])
        if 3 <= t:
            i = insts[t - 3]
            phase4(i, dstate[i])
            del dstate[i]
    p.ready["og"] = [(s_, s_.n) for s_ in st]
    for s_ in st:
        p.finish(s_)



def stage_C(p: Prog):
    sc = p.sched
    ogd, wod = p.hbm["og"], p.hbm["L0_attn_Wo"]
    xd, gfd, x1d = p.hbm["x"], p.hbm["gf"], p.hbm["x1"]
    lwo = sc.sem("C_lw")
    lo = [sc.sem("C_lo0"), sc.sem("C_lo1")]
    lx = [sc.sem("C_lx0"), sc.sem("C_lx1")]
    lg = [sc.sem("C_lg0"), sc.sem("C_lg1")]
    mm = sc.sem("C_mm"); ve = sc.sem("C_v"); se = sc.sem("C_s")
    st = [sc.sem("C_st0"), sc.sem("C_st1")]
    wo_sb = p.sbuf("C_wo", (128, NKT * H), BF16)
    ogc = [p.sbuf(f"C_og{i}", (128, NKT * 128), BF16) for i in range(2)]
    xb = [p.sbuf(f"C_x{i}", (128, H), F32) for i in range(2)]
    gfb = [p.sbuf(f"C_gf{i}", (128, H), BF16) for i in range(2)]
    x1b = [p.sbuf(f"C_x1{i}", (128, H), F32) for i in range(2)]
    psf = [[p.psum(f"C_ps0_{j}", (128, 512), F32) for j in range(4)]]
    psf.append(psf[0])
    wo_rdy = None
    for kt in range(NKT):
        wo_rdy = sc.dma("gpsimd", wo_sb[:, kt * H:(kt + 1) * H],
                        wod[kt * 128:(kt + 1) * 128, :], inc=lwo)
    ogfree = [None, None]
    xfree = [None, None]
    gffree = [None, None]
    x1free = [None, None]
    psfree = [[None] * 4, [None] * 4]
    for tt in range(NTT):
        pa = tt % 2
        lc = sc.dma("sync", ogc[pa][:],
                    ogd.rearrange("(kt p) t -> p kt t", p=128)[:, :, tt * 128:(tt + 1) * 128],
                    waits=[p.rdy("og"), ogfree[pa]], inc=lo[pa])
        lxx = sc.dma("sync", xb[pa][:], xd[tt * 128:(tt + 1) * 128, :],
                     waits=[xfree[pa]], inc=lx[pa])
        lgg = sc.dma("sync", gfb[pa][:], gfd[tt * 128:(tt + 1) * 128, :],
                     waits=[p.rdy("gf"), gffree[pa]], inc=lg[pa])
        gmm = None
        for fg in range(4):
            def mmc(e, pa=pa, fg=fg):
                for kt in range(NKT):
                    ins = e.matmul(psf[pa][fg][:],
                                   ogc[pa][:, kt * 128:(kt + 1) * 128],
                                   wo_sb[:, kt * H + fg * 512: kt * H + fg * 512 + 512],
                                   start=(kt == 0), stop=(kt == NKT - 1))
                return ins
            gmm = sc.emit("tensor", mmc,
                          waits=[lc, wo_rdy, psfree[pa][fg],
                                 psfree[1 - pa][fg]], inc=mm)
            psfree[pa][fg] = gmm
        ogfree[pa] = gmm
        ev = None
        for fg in range(4):
            t1 = sc.emit("vector", lambda e, pa=pa, fg=fg: e.tensor_tensor(
                x1b[pa][:, fg * 512:(fg + 1) * 512], psf[pa][fg][:],
                gfb[pa][:, fg * 512:(fg + 1) * 512], ALU.mult),
                waits=[psfree[pa][fg], lgg, x1free[pa]], inc=ve)
            ev = sc.emit("vector", lambda e, pa=pa, fg=fg: e.tensor_tensor(
                x1b[pa][:, fg * 512:(fg + 1) * 512],
                x1b[pa][:, fg * 512:(fg + 1) * 512],
                xb[pa][:, fg * 512:(fg + 1) * 512], ALU.add),
                waits=[t1, lxx], inc=ve)
            psfree[pa][fg] = ev
        xfree[pa] = ev
        gffree[pa] = ev
        wr = sc.dma("sync", x1d[tt * 128:(tt + 1) * 128, :], x1b[pa][:],
                    waits=[ev], inc=st[pa])
        x1free[pa] = wr
    p.ready["x1"] = [(st[0], st[0].n), (st[1], st[1].n)]
    p.finish(st[0])
    p.finish(st[1])


def stage_E(p: Prog):
    stage_ln_generic(p, "x1", "xn1", "E")
    sc = p.sched
    xn_sb, xn_rdy = load_xn_res(p, "xn1")
    lw = [sc.sem("E_lw0"), sc.sem("E_lw1")]
    mm = sc.sem("E_mm"); ve = sc.sem("E_v"); se = sc.sem("E_s")
    gp = sc.sem("E_gp")
    st = [sc.sem("E_st0"), sc.sem("E_st1")]
    wbuf = [p.sbuf(f"E_w{i}", (128, NKT * 128), BF16) for i in range(2)]
    ot = [p.sbuf(f"E_o{i}", (128, 512), BF16) for i in range(2)]
    rt1 = p.sbuf("E_rt1", (64, 512), F32)
    rt2 = p.sbuf("E_rt2", (64, 512), F32)
    rt3 = p.sbuf("E_rt3", (64, 512), F32)
    ps = [p.psum(f"E_ps{i}", (128, 512), F32) for i in range(2)]
    pfr = p.psum("E_pfr", (64, 512), F32)
    cosb = p.sbuf("E_cos", (64, T), BF16)
    sinb = p.sbuf("E_sin", (64, T), BF16)
    invf = p.sbuf("E_invf", (1, 64), F32)
    tvec = p.sbuf("E_tvec", (1, 512), F32)
    frw = p.sbuf("E_frw", (64, 512), F32)
    fri = p.sbuf("E_fri", (64, 512), I32)

    z1 = sc.emit("gpsimd", lambda e: e.iota(invf[:], [[1, 64]], base=0,
                                            channel_multiplier=0,
                                            allow_small_or_imprecise_dtypes=True),
                 inc=gp)
    iv = sc.emit("scalar", lambda e: e.activation(
        invf[:], invf[:], AF.Exp, scale=-float(np.log(10000.0) / 64.0)),
        waits=[z1], inc=se)
    PI = float(np.pi)
    tabdep = []
    for tg in range(8):
        pos0 = (tg % 4) * 512
        zt = sc.emit("gpsimd", lambda e, pos0=pos0: e.iota(
            tvec[:], [[1, 512]], base=pos0, channel_multiplier=0,
            allow_small_or_imprecise_dtypes=True),
            waits=[tabdep[-1] if tabdep else None], inc=gp)
        fr = sc.emit("tensor", lambda e: e.matmul(pfr[:], invf[:], tvec[:],
                                                  start=True, stop=True),
                     waits=[zt, iv, tabdep[-1] if tabdep else None], inc=mm)
        m1 = sc.emit("vector", lambda e: e.tensor_scalar_mul(
            frw[:], pfr[:], 1.0 / (2 * PI)), waits=[fr], inc=ve)
        m2 = sc.emit("vector", lambda e: e.tensor_copy(fri[:], frw[:]),
                     waits=[m1], inc=ve)
        m3 = sc.emit("vector", lambda e: e.tensor_copy(rt1[:], fri[:]),
                     waits=[m2], inc=ve)
        m4 = sc.emit("vector", lambda e: e.tensor_tensor(
            frw[:], frw[:], rt1[:], ALU.subtract), waits=[m3], inc=ve)
        m5 = sc.emit("vector", lambda e: e.tensor_single_scalar(
            rt1[:], frw[:], 0.5, ALU.is_gt), waits=[m4], inc=ve)
        m6 = sc.emit("vector", lambda e: e.scalar_tensor_tensor(
            frw[:], rt1[:], -1.0, frw[:], ALU.mult, ALU.add),
            waits=[m5], inc=ve)
        s1 = sc.emit("scalar", lambda e, tg=tg: e.activation(
            sinb[:, tg * 512:(tg + 1) * 512], frw[:], AF.Sin,
            scale=2 * PI), waits=[m6], inc=se)
        c1 = sc.emit("vector", lambda e: e.tensor_scalar_add(frw[:], frw[:],
                                                             0.25),
                     waits=[s1], inc=ve)
        c2 = sc.emit("vector", lambda e: e.tensor_single_scalar(
            rt1[:], frw[:], 0.5, ALU.is_gt), waits=[c1], inc=ve)
        c3 = sc.emit("vector", lambda e: e.scalar_tensor_tensor(
            frw[:], rt1[:], -1.0, frw[:], ALU.mult, ALU.add),
            waits=[c2], inc=ve)
        c4 = sc.emit("scalar", lambda e, tg=tg: e.activation(
            cosb[:, tg * 512:(tg + 1) * 512], frw[:], AF.Sin,
            scale=2 * PI), waits=[c3], inc=se)
        tabdep.append(c4)
    tab_rdy = tabdep[-1]

    projs = [("L1_attn_Wq", "q1r", 16, True),
             ("L1_attn_Wk", "k1r", 4, True),
             ("L1_attn_Wv", "v1s", 4, False)]
    wfree = [None, None]
    psfree = [None, None]
    otfree = [None, None]
    n = 0
    for wname, oname, ndb, rope in projs:
        W = p.hbm[wname]
        dst = p.hbm[oname]
        for db in range(ndb):
            wp = n % 2
            lw_t = sc.dma(
                "gpsimd",
                wbuf[wp].ap().rearrange("p (kt c) -> p kt c", kt=NKT),
                W.rearrange("(kt p) n -> p kt n", p=128)[:, :, db * 128:(db + 1) * 128],
                waits=[wfree[wp]], inc=lw[wp])
            for tg in range(8):
                pp = (n * 8 + tg) % 2
                tok0 = tg * 512

                def mmp(e, wp=wp, tok0=tok0, pp=pp):
                    for kt in range(NKT):
                        ins = e.matmul(ps[pp][:],
                                       wbuf[wp][:, kt * 128:(kt + 1) * 128],
                                       xn_sb[:, kt * T + tok0: kt * T + tok0 + 512],
                                       start=(kt == 0), stop=(kt == NKT - 1))
                    return ins
                g = sc.emit("tensor", mmp, waits=[lw_t, xn_rdy, psfree[pp]], inc=mm)
                if rope:
                    cs = lambda: cosb[0:64, tok0:tok0 + 512]
                    sn = lambda: sinb[0:64, tok0:tok0 + 512]
                    t1 = sc.emit("vector", lambda e, pp=pp, tok0=tok0: e.tensor_tensor(
                        rt1[:], ps[pp][0:64, :], cosb[0:64, tok0:tok0 + 512], ALU.mult),
                        waits=[g, tab_rdy], inc=ve)
                    t2 = sc.emit("vector", lambda e, pp=pp, tok0=tok0: e.tensor_tensor(
                        rt2[:], ps[pp][64:128, :], sinb[0:64, tok0:tok0 + 512], ALU.mult),
                        waits=[g], inc=ve)
                    t3 = sc.emit("vector", lambda e, pp=pp, tok0=tok0: e.tensor_tensor(
                        rt3[:], ps[pp][0:64, :], sinb[0:64, tok0:tok0 + 512], ALU.mult),
                        waits=[g], inc=ve)
                    t4 = sc.emit("vector", lambda e, pp=pp, tok0=tok0: e.tensor_tensor(
                        ps[pp][64:128, :], ps[pp][64:128, :],
                        cosb[0:64, tok0:tok0 + 512], ALU.mult),
                        waits=[t2], inc=ve)
                    r1 = sc.emit("vector", lambda e, pp=pp: e.tensor_tensor(
                        ot[pp][0:64, :], rt1[:], rt2[:], ALU.subtract),
                        waits=[t1, t2, otfree[pp]], inc=ve)
                    r2 = sc.emit("vector", lambda e, pp=pp: e.tensor_tensor(
                        ot[pp][64:128, :], rt3[:], ps[pp][64:128, :], ALU.add),
                        waits=[t3, t4, r1], inc=ve)
                    a = r2
                    psfree[pp] = r2
                else:
                    a = sc.emit("scalar", lambda e, pp=pp: e.activation(
                        ot[pp][:], ps[pp][:], AF.Copy),
                        waits=[g, otfree[pp]], inc=se)
                    psfree[pp] = a
                wr = sc.dma("sync", dst[db * 128:(db + 1) * 128, tok0:tok0 + 512],
                            ot[pp][:], waits=[a], inc=st[pp])
                otfree[pp] = wr
            wfree[wp] = g
            n += 1
    for nm in ("q1r", "k1r", "v1s"):
        p.ready[nm] = [(st[0], st[0].n), (st[1], st[1].n)]
    p.finish(st[0])
    p.finish(st[1])



def stage_F(p: Prog):
    sc = p.sched
    qd, kd, vd = p.hbm["q1r"], p.hbm["k1r"], p.hbm["v1s"]
    wod, s1d = p.hbm["L1_attn_Wo"], p.hbm["s1"]
    lk = sc.sem("F_lk")
    lq = [sc.sem("F_lq0"), sc.sem("F_lq1")]
    lv = sc.sem("F_lv")
    lwo = [sc.sem("F_lw0"), sc.sem("F_lw1")]
    lat = [sc.sem("F_la0"), sc.sem("F_la1")]
    mm = sc.sem("F_mm"); ve = sc.sem("F_v"); se = sc.sem("F_s"); gp = sc.sem("F_g")
    st = [sc.sem("F_st0"), sc.sem("F_st1")]
    SCALE = float(D ** -0.5)
    EB = -10.0

    k_sb = p.sbuf("F_k", (128, 4 * T), BF16)
    vtm = p.sbuf("F_vtm", (128, 4 * 16 * 128), BF16)
    vcol = p.sbuf("F_vcol", (128, 4 * 128), BF16)
    qt = [p.sbuf(f"F_q{i}", (128, 512), BF16) for i in range(2)]
    e_sb = [p.sbuf(f"F_e{i}", (128, 16 * 512), BF16) for i in range(2)]
    denr = [p.sbuf(f"F_denr{i}", (1, 512), F32) for i in range(2)]
    rb = [p.sbuf(f"F_rb{i}", (128, 512), F32) for i in range(2)]
    ao = [p.sbuf(f"F_ao{i}", (128, 512), BF16) for i in range(2)]
    at_res = p.sbuf("F_atres", (128, 16 * S), BF16)
    ones1f = p.sbuf("F_ones1f", (1, 128), F32)
    wo_sb = [p.sbuf(f"F_wo{i}", (128, NKT * 512), BF16) for i in range(2)]
    s1b = [p.sbuf(f"F_s1{i}", (128, 512), BF16) for i in range(2)]
    pss = [p.psum(f"F_pss{i}", (128, 512), F32) for i in range(2)]
    psd = p.psum("F_psd", (1, 512), F32)
    psb_t = p.psum("F_psb", (128, 512), F32)
    psb = [psb_t, psb_t]
    psa = [p.psum(f"F_psa{i}", (128, 512), F32) for i in range(2)]
    psvt = p.psum("F_psvt", (128, 128), BF16)

    ebc = p.sbuf("F_ebc", (128, 1), F32)
    zo = sc.emit("gpsimd", lambda e: e.memset(ones1f[:], 1.0), inc=gp)
    zeb = sc.emit("gpsimd", lambda e: e.memset(ebc[:], EB), inc=gp)
    k_rdy = None
    for kv in range(4):
        k_rdy = sc.dma("sync", k_sb[:, kv * T:(kv + 1) * T],
                       kd[kv * 128:(kv + 1) * 128, :],
                       waits=[p.rdy("k1r")], inc=lk)
    idm = p.consts["idm"]

    qfree = [None, None]
    efree = [None, None]
    aofree = [None, None]
    ssfree = [None, None]
    avfree = [None, None]
    prev_den = [None, None]
    prev_rb = [None, None]
    prev_bc = [None, None]
    vtmfree = None
    vcolfree = None
    bcfree = None
    atfree = None
    s1free = [None, None]
    wofree = [None, None]
    n = 0
    for b_ in range(2):
        tdone = []
        for i_ in range(16):
            lvc = sc.dma("sync", vcol[:],
                         vd[:, b_ * S + i_ * 128: b_ * S + (i_ + 1) * 128]
                         .rearrange("(kv p) t -> p kv t", p=128),
                         waits=[p.rdy("v1s"), vcolfree], inc=lv)
            for kv in range(4):
                tr = sc.emit("tensor", lambda e, kv=kv, i_=i_: e.transpose(
                    psvt[:], vcol[:, kv * 128:(kv + 1) * 128], idm[:]),
                    waits=[lvc, p.consts["idm_rdy"],
                           tdone[-1] if tdone else atfree], inc=mm)
                ev = sc.emit("vector", lambda e, kv=kv, i_=i_: e.tensor_copy(
                    vtm[:, (kv * 16 + i_) * 128:(kv * 16 + i_) * 128 + 128],
                    psvt[:]),
                    waits=[tr, vtmfree], inc=ve)
                tdone.append(ev)
            vcolfree = tdone[-1]
        vtm_rdy = tdone[-1]
        for h_ in range(NH):
            kv = h_ // 4
            for qg in range(4):
                pa = n % 2
                q0 = b_ * S + qg * 512
                lqq = sc.dma("scalar", qt[pa][:],
                             qd[h_ * 128:(h_ + 1) * 128, q0:q0 + 512],
                             waits=[p.rdy("q1r"), qfree[pa]], inc=lq[pa])
                expp = [ssfree[0], ssfree[1]]
                exps = []
                for kt2 in range(16):
                    kp2 = kt2 % 2
                    sm_ = sc.emit("tensor", lambda e, pa=pa, kp2=kp2, kv=kv, kt2=kt2, b_=b_: e.matmul(
                        pss[kp2][:], k_sb[:, kv * T + b_ * S + kt2 * 128:
                                          kv * T + b_ * S + kt2 * 128 + 128],
                        qt[pa][:], start=True, stop=True),
                        waits=[k_rdy, lqq, expp[kp2]], inc=mm)
                    ex_ = sc.emit("scalar", lambda e, pa=pa, kp2=kp2, kt2=kt2: e.activation(
                        e_sb[pa][:, kt2 * 512:(kt2 + 1) * 512], pss[kp2][:],
                        AF.Exp, scale=SCALE, bias=ebc[:, 0:1]),
                        waits=[sm_, zeb, efree[pa]], inc=se)
                    expp[kp2] = ex_
                    exps.append(ex_)
                exl = exps[-1]
                ssfree[0], ssfree[1] = expp[0], expp[1]
                den_mm = None
                for kt2 in range(16):
                    den_mm = sc.emit("tensor", lambda e, pa=pa, kt2=kt2: e.matmul(
                        psd[0:1, :],
                        p.consts["ones"][:, 0:1],
                        e_sb[pa][:, kt2 * 512:(kt2 + 1) * 512],
                        start=(kt2 == 0), stop=(kt2 == 15)),
                        waits=[exps[kt2], p.consts["idm_rdy"],
                               prev_den[pa]], inc=mm)
                rcp = sc.emit("vector", lambda e, pa=pa: e.reciprocal(
                    denr[pa][:], psd[0:1, :]),
                    waits=[den_mm, prev_rb[pa]], inc=ve)
                prev_den[0] = rcp
                prev_den[1] = rcp
                bcm = sc.emit("tensor", lambda e, pa=pa: e.matmul(
                    psb[pa][:], ones1f[:], denr[pa][:], start=True, stop=True),
                    waits=[rcp, prev_bc[0], prev_bc[1]], inc=mm)
                rbe = sc.emit("scalar", lambda e, pa=pa: e.activation(
                    rb[pa][:], psb[pa][:], AF.Copy),
                    waits=[bcm], inc=se)
                prev_bc[pa] = rbe
                avm = None
                for kt2 in range(16):
                    avm = sc.emit("tensor", lambda e, pa=pa, kv=kv, kt2=kt2: e.matmul(
                        psa[pa][:], vtm[:, (kv * 16 + kt2) * 128:(kv * 16 + kt2) * 128 + 128],
                        e_sb[pa][:, kt2 * 512:(kt2 + 1) * 512],
                        start=(kt2 == 0), stop=(kt2 == 15)),
                        waits=[vtm_rdy, exl, avfree[pa]], inc=mm)
                efree[pa] = avm
                qfree[pa] = avm
                nrm = sc.emit("vector", lambda e, pa=pa: e.tensor_tensor(
                    ao[pa][:], psa[pa][:], rb[pa][:], ALU.mult),
                    waits=[avm, rbe, aofree[pa]], inc=ve)
                avfree[pa] = nrm
                prev_rb[pa] = nrm
                cp = sc.emit("vector", lambda e, pa=pa, h_=h_, qg=qg: e.tensor_copy(
                    at_res[:, h_ * S + qg * 512: h_ * S + qg * 512 + 512]
                    if False else
                    at_res.ap().rearrange("p (kt t) -> p kt t", kt=16)[:, h_, qg * 512:qg * 512 + 512],
                    ao[pa][:]),
                    waits=[nrm, atfree], inc=ve)
                aofree[pa] = cp
                n += 1
        at_rdy = cp
        for fg in range(4):
            wp = fg % 2
            lw_t = sc.dma("gpsimd",
                          wo_sb[wp].ap().rearrange("p (kt c) -> p kt c", kt=NKT),
                          wod.rearrange("(kt p) n -> p kt n", p=128)[:, :, fg * 512:(fg + 1) * 512],
                          waits=[wofree[wp]], inc=lwo[wp])
            for tt in range(16):
                pa = tt % 2

                def mmw(e, wp=wp, tt=tt, pa=pa):
                    for kt in range(NKT):
                        ins = e.matmul(psa[pa][:],
                                       at_res.ap().rearrange("p (kt t) -> p kt t", kt=16)[:, kt, tt * 128:(tt + 1) * 128],
                                       wo_sb[wp][:, kt * 512:(kt + 1) * 512],
                                       start=(kt == 0), stop=(kt == NKT - 1))
                    return ins
                g = sc.emit("tensor", mmw, waits=[lw_t, at_rdy, avfree[pa]], inc=mm)
                ev = sc.emit("scalar", lambda e, pa=pa: e.activation(
                    s1b[pa][:], psa[pa][:], AF.Copy),
                    waits=[g, s1free[pa]], inc=se)
                avfree[pa] = ev
                wr = sc.dma("sync",
                            s1d[b_ * S + tt * 128: b_ * S + (tt + 1) * 128,
                                fg * 512:(fg + 1) * 512],
                            s1b[pa][:], waits=[ev], inc=st[pa])
                s1free[pa] = wr
            wofree[wp] = g
        atfree = g
    p.ready["s1"] = [(st[0], st[0].n), (st[1], st[1].n)]
    p.finish(st[0])
    p.finish(st[1])


def stage_G0(p: Prog):
    sc = p.sched
    jobs = [("L1_ffn_gate_w", "wgT", II, H, None),
            ("L1_ffn_up_w", "wuT", II, H, None),
            ("L1_ffn_down_w", "wdT", H, II, "L1_ffn_down_scale")]
    li = [sc.sem("G0_li0"), sc.sem("G0_li1")]
    lsc = sc.sem("G0_lsc")
    mm = sc.sem("G0_mm"); ve = sc.sem("G0_v"); se = sc.sem("G0_s")
    st = [sc.sem("G0_st0"), sc.sem("G0_st1")]
    src_i = [p.sbuf(f"G0_i{i}", (128, 512), I32) for i in range(2)]
    src_b = [p.sbuf(f"G0_b{i}", (128, 512), BF16) for i in range(2)]
    dst_b = [p.sbuf(f"G0_d{i}", (128, 512), BF16) for i in range(2)]
    dscol = p.sbuf("G0_ds", (128, 1), F32)
    pst_t = p.psum("G0_ps", (128, 1024), BF16)
    pst = [pst_t[:, 0:512], pst_t[:, 512:1024]]
    idm = p.consts["idm"]
    ifree = [None, None]
    bfree = [None, None]
    dfree = [None, None]
    pfree = [None, None]
    n = 0
    for wname, dname, R, C_, scname in jobs:
        W, Dst = p.hbm[wname], p.hbm[dname]
        for rb_ in range(R // 128):
            lsc_t = None
            if scname is not None:
                lsc_t = sc.dma("gpsimd", dscol[:],
                               p.hbm[scname][rb_ * 128:(rb_ + 1) * 128][:, None],
                               waits=[bfree[0], bfree[1]], inc=lsc)
            for cb in range(C_ // 512):
                pa = n % 2
                l = sc.dma("sync", src_i[pa][:],
                           W[rb_ * 128:(rb_ + 1) * 128, cb * 512:(cb + 1) * 512],
                           waits=[ifree[pa]], inc=li[pa])
                if scname is None:
                    cv = sc.emit("vector", lambda e, pa=pa: e.tensor_scalar_sub(
                        src_b[pa][:], src_i[pa][:], 128.0),
                        waits=[l, bfree[pa]], inc=ve)
                else:
                    cv = sc.emit("vector", lambda e, pa=pa: e.tensor_scalar(
                        src_b[pa][:], src_i[pa][:], -128.0, dscol[:, 0:1],
                        ALU.add, ALU.mult),
                        waits=[l, lsc_t, bfree[pa]], inc=ve)
                ifree[pa] = cv

                def trg(e, pa=pa):
                    for j in range(4):
                        ins = e.transpose(pst[pa][:, j * 128:(j + 1) * 128],
                                          src_b[pa][:, j * 128:(j + 1) * 128],
                                          idm[:])
                    return ins
                tr = sc.emit("tensor", trg,
                             waits=[cv, p.consts["idm_rdy"], pfree[0],
                                    pfree[1]], inc=mm)
                bfree[pa] = tr
                ev = sc.emit("scalar", lambda e, pa=pa: e.activation(
                    dst_b[pa][:], pst[pa][:, :], AF.Copy),
                    waits=[tr, dfree[pa]], inc=se)
                pfree[pa] = ev
                wr = sc.dma("sync",
                            Dst.rearrange("(q p) r -> p q r", p=128)[:, 4 * cb:4 * cb + 4,
                                                                     rb_ * 128:(rb_ + 1) * 128],
                            dst_b[pa].ap().rearrange("p (j c) -> p j c", j=4),
                            waits=[ev], inc=st[pa])
                dfree[pa] = wr
                n += 1
    for nm in ("wgT", "wuT", "wdT"):
        p.ready[nm] = [(st[0], st[0].n), (st[1], st[1].n)]
    p.finish(st[0])
    p.finish(st[1])


def stage_G1(p: Prog):
    sc = p.sched
    xn_sb, xn_rdy = load_xn_res(p, "xn1")
    wg, wu, h1d = p.hbm["wgT"], p.hbm["wuT"], p.hbm["h1"]
    lgs = [sc.sem("G1_lg0"), sc.sem("G1_lg1")]
    lus = [sc.sem("G1_lu0"), sc.sem("G1_lu1")]
    lsc = sc.sem("G1_lsc")
    mm = sc.sem("G1_mm"); ve = sc.sem("G1_v"); se = sc.sem("G1_s")
    st = [sc.sem("G1_st0"), sc.sem("G1_st1")]
    wgb = [p.sbuf(f"G1_wg{i}", (128, NKT * 128), BF16) for i in range(2)]
    wub = [p.sbuf(f"G1_wu{i}", (128, NKT * 128), BF16) for i in range(2)]
    scg = [p.sbuf(f"G1_scg{i}", (128, 2), F32) for i in range(2)]
    hg = [p.sbuf(f"G1_hg{i}", (128, 512), F32) for i in range(2)]
    hb = [p.sbuf(f"G1_hb{i}", (128, 512), BF16) for i in range(2)]
    psg = [p.psum(f"G1_psg{i}", (128, 512), F32) for i in range(2)]
    psu = [p.psum(f"G1_psu{i}", (128, 512), F32) for i in range(2)]
    wfree = [None, None]
    scfree = [None, None]
    gfree = [[None, None], [None, None]]
    hgfree = [None, None]
    hbfree = [None, None]
    for ib in range(II // 128):
        wp = ib % 2
        lg = sc.dma("gpsimd", wgb[wp].ap().rearrange("p (kt c) -> p kt c", kt=NKT),
                    wg.rearrange("(kt p) n -> p kt n", p=128)[:, :, ib * 128:(ib + 1) * 128],
                    waits=[wfree[wp], p.rdy("wgT")], inc=lgs[wp])
        lu = sc.dma("gpsimd", wub[wp].ap().rearrange("p (kt c) -> p kt c", kt=NKT),
                    wu.rearrange("(kt p) n -> p kt n", p=128)[:, :, ib * 128:(ib + 1) * 128],
                    waits=[p.rdy("wuT")], inc=lgs[wp])
        ls1 = sc.dma("gpsimd", scg[wp][:, 0:1],
                     p.hbm["L1_ffn_gate_scale"][ib * 128:(ib + 1) * 128][:, None],
                     waits=[scfree[wp]], inc=lsc)
        ls2 = sc.dma("gpsimd", scg[wp][:, 1:2],
                     p.hbm["L1_ffn_up_scale"][ib * 128:(ib + 1) * 128][:, None],
                     inc=lsc)
        for tg in range(8):
            pa = (ib * 8 + tg) % 2
            tok0 = tg * 512
            g1 = sc.emit("tensor", lambda e, wp=wp, tok0=tok0, pa=pa:
                         mk_mm2(e, psg[pa], wgb[wp], xn_sb, tok0),
                         waits=[lg, xn_rdy, gfree[pa][0]], inc=mm)
            g2 = sc.emit("tensor", lambda e, wp=wp, tok0=tok0, pa=pa:
                         mk_mm2(e, psu[pa], wub[wp], xn_sb, tok0),
                         waits=[lu, gfree[pa][1]], inc=mm)
            e1 = sc.emit("scalar", lambda e, pa=pa, wp=wp:
                         e.activation(hg[pa][:], psg[pa][:], AF.Silu,
                                      scale=scg[wp][:, 0:1]),
                         waits=[g1, ls2, hgfree[pa]], inc=se)
            gfree[pa][0] = e1
            e2 = sc.emit("vector", lambda e, pa=pa, wp=wp:
                         e.scalar_tensor_tensor(hb[pa][:], psu[pa][:],
                                                scg[wp][:, 1:2], hg[pa][:],
                                                ALU.mult, ALU.mult),
                         waits=[g2, e1, hbfree[pa]], inc=ve)
            gfree[pa][1] = e2
            hgfree[pa] = e2
            wr = sc.dma("sync", h1d[ib * 128:(ib + 1) * 128, tok0:tok0 + 512],
                        hb[pa][:], waits=[e2], inc=st[pa])
            hbfree[pa] = wr
        wfree[wp] = g2
        scfree[wp] = e1
    p.ready["h1"] = [(st[0], st[0].n), (st[1], st[1].n)]
    p.finish(st[0])
    p.finish(st[1])


def stage_G2(p: Prog):
    sc = p.sched
    h1d, wdd = p.hbm["h1"], p.hbm["wdT"]
    x1d, s1d, outd = p.hbm["x1"], p.hbm["s1"], p.hbm["out"]
    lw = sc.sem("G2_lw")
    lh = [sc.sem("G2_lh0"), sc.sem("G2_lh1")]
    lx = [sc.sem("G2_lx0"), sc.sem("G2_lx1")]
    ls = [sc.sem("G2_ls0"), sc.sem("G2_ls1")]
    mm = sc.sem("G2_mm"); ve = sc.sem("G2_v"); se = sc.sem("G2_s")
    st = [sc.sem("G2_st0"), sc.sem("G2_st1")]
    NIB = II // 128
    wd_res = p.sbuf("G2_wd", (128, NIB * 1024), BF16)
    hcol = [p.sbuf(f"G2_h{i}", (128, NIB * 128), BF16) for i in range(2)]
    gfb = [p.sbuf(f"G2_gf{i}", (128, 1024), F32) for i in range(2)]
    xb = [p.sbuf(f"G2_x{i}", (128, 1024), F32) for i in range(2)]
    s1b = [p.sbuf(f"G2_s1{i}", (128, 1024), BF16) for i in range(2)]
    ob = [p.sbuf(f"G2_o{i}", (128, 1024), F32) for i in range(2)]
    psf = [[p.psum(f"G2_ps{i}_{j}", (128, 512), F32) for j in range(2)]
           for i in range(2)]
    wd_rdy = None
    hfree = [None, None]
    psfree = [[None, None], [None, None]]
    gffree = [None, None]
    xfree = [None, None]
    s1free = [None, None]
    ofree = [None, None]
    for half in range(2):
        for ib in range(NIB):
            wd_rdy = sc.dma("gpsimd", wd_res[:, ib * 1024:(ib + 1) * 1024],
                            wdd[ib * 128:(ib + 1) * 128,
                                half * 1024:(half + 1) * 1024],
                            waits=[p.rdy("wdT"),
                                   psfree[0][1] if ib == 0 else None,
                                   psfree[1][1] if ib == 0 else None],
                            inc=lw)
        for tt in range(NTT):
            pa = tt % 2
            lc = sc.dma("sync", hcol[pa][:],
                        h1d.rearrange("(ib p) t -> p ib t", p=128)[:, :, tt * 128:(tt + 1) * 128],
                        waits=[p.rdy("h1"), hfree[pa]], inc=lh[pa])
            lxx = sc.dma("sync", xb[pa][:],
                         x1d[tt * 128:(tt + 1) * 128,
                             half * 1024:(half + 1) * 1024],
                         waits=[p.rdy("x1"), xfree[pa]], inc=lx[pa])
            ls1 = sc.dma("scalar", s1b[pa][:],
                         s1d[tt * 128:(tt + 1) * 128,
                             half * 1024:(half + 1) * 1024],
                         waits=[p.rdy("s1"), s1free[pa]], inc=ls[pa])

            def mmd(e, pa=pa):
                for ib in range(NIB):
                    for fgx in range(2):
                        ins = e.matmul(psf[pa][fgx][:],
                                       hcol[pa][:, ib * 128:(ib + 1) * 128],
                                       wd_res[:, ib * 1024 + fgx * 512: ib * 1024 + fgx * 512 + 512],
                                       start=(ib == 0), stop=(ib == NIB - 1))
                return ins
            g = sc.emit("tensor", mmd,
                        waits=[wd_rdy, lc, psfree[pa][0], psfree[pa][1]], inc=mm)
            hfree[pa] = g
            ev = wr = None
            for fgx in range(2):
                ev = sc.emit("scalar", lambda e, pa=pa, fgx=fgx: e.activation(
                    gfb[pa][:, fgx * 512:(fgx + 1) * 512], psf[pa][fgx][:],
                    AF.Silu), waits=[g, gffree[pa]], inc=se)
                psfree[pa][fgx] = ev
            col0 = half * 1024
            m1 = sc.emit("vector", lambda e, pa=pa: e.tensor_tensor(
                ob[pa][:], gfb[pa][:, 0:1024],
                s1b[pa][:], ALU.mult),
                waits=[ev, ls1, ofree[pa]], inc=ve)
            gffree[pa] = m1
            m2 = sc.emit("vector", lambda e, pa=pa: e.tensor_tensor(
                ob[pa][:], ob[pa][:],
                xb[pa][:], ALU.add),
                waits=[m1, lxx], inc=ve)
            xfree[pa] = m2
            s1free[pa] = m2
            wr = sc.dma("sync",
                        outd[tt * 128:(tt + 1) * 128, half * 1024:(half + 1) * 1024],
                        ob[pa][:], waits=[m2], inc=st[pa])
            ofree[pa] = wr
    p.finish(st[0])
    p.finish(st[1])


STAGE_FN = {
    "A": stage_A,
    "A1": stage_A1,
    "A2a": stage_A2a,
    "A2b": stage_A2b,
    "B": stage_B,
    "C": stage_C,
    "E": stage_E,
    "F": stage_F,
    "G0": stage_G0,
    "G1": stage_G1,
    "G2": stage_G2,
}

N_CORES = 8


def _ffn0_perm():
    return np.arange(II).reshape(II // 4, 4).T.reshape(-1)


def _rope_perm():
    idx = []
    for h in range(NH):
        base = h * D
        idx.extend(base + np.arange(0, D, 2))
        idx.extend(base + np.arange(1, D, 2))
    return np.asarray(idx)


STAGE_PERSIST = {"G0"}


def build_program(stages):
    p = Prog(stages)
    with p.nc.Block() as block:
        build_consts(p)
        p.sched.flush(block)
        for st in stages:
            if st in STAGE_PERSIST:
                saved_sb, saved_ps = p.sb, p.psn
                p.sb, p.psn = {}, {}
                ng = p.next_guards
                p.next_guards = []
                STAGE_FN[st](p)
                p.sched.flush(block)
                p.sb, p.psn = saved_sb, saved_ps
                p.next_guards = ng
            else:
                p.begin_stage()
                STAGE_FN[st](p)
                p.sched.flush(block)
                p.end_stage()
    return p


def kernel(**inputs):
    xfull = np.ascontiguousarray(np.asarray(inputs["L0_attn_x"], dtype=np.float32))
    perm = _rope_perm()
    kperm = np.concatenate([g * D + np.arange(0, D, 2) for g in range(4)]
                           + [g * D + np.arange(1, D, 2) for g in range(4)])
    weights = {}
    for name in WEIGHTS:
        arr = np.asarray(inputs[name])
        weights[name] = np.ascontiguousarray(arr)
    wq = weights["L1_attn_Wq"]
    weights["L1_attn_Wq"] = np.ascontiguousarray(wq[:, perm])
    wk = weights["L1_attn_Wk"]
    kp = []
    for g in range(4):
        kp.extend(g * D + np.arange(0, D, 2))
        kp.extend(g * D + np.arange(1, D, 2))
    weights["L1_attn_Wk"] = np.ascontiguousarray(wk[:, np.asarray(kp)])
    fperm = _ffn0_perm()
    weights["L0_ffn_w_gate"] = np.ascontiguousarray(weights["L0_ffn_w_gate"][:, fperm])
    weights["L0_ffn_w_up"] = np.ascontiguousarray(weights["L0_ffn_w_up"][:, fperm])
    weights["L0_ffn_w_down"] = np.ascontiguousarray(weights["L0_ffn_w_down"][fperm, :])

    pool = [dict() for _ in range(N_CORES)]
    for c in range(N_CORES):
        pool[c]["x"] = np.ascontiguousarray(
            xfull[2 * c:2 * c + 2].reshape(T, H))
        for name, arr in weights.items():
            pool[c][name] = arr

    groups = [g for g in GROUPS if any(st in STAGE_FN for st in g)]
    outs = None
    for stages in groups:
        stages = [st for st in stages if st in STAGE_FN]
        if not stages:
            continue
        p = build_program(stages)
        in_maps = [{n: pool[c][n] for n in p.input_names} for c in range(N_CORES)]
        res = run_bass_kernel_spmd(p.nc, in_maps, core_ids=list(range(N_CORES)))
        for c in range(N_CORES):
            for n in p.output_names:
                pool[c][n] = np.asarray(res.results[c][n])
        outs = res
    out = np.stack([pool[c]["out"].reshape(2, S, H) for c in range(N_CORES)])
    return out.reshape(16, S, H).astype(np.float32)


# version 06 (final)
# speedup vs baseline: 1.0134x; 1.0134x over previous
import contextlib
import numpy as np
import ml_dtypes
import concourse.bass as bass
import concourse.mybir as mybir
from concourse.bass_utils import run_bass_kernel_spmd

F32 = mybir.dt.float32
BF16 = mybir.dt.bfloat16
I32 = mybir.dt.int32
AF = mybir.ActivationFunctionType
ALU = mybir.AluOpType
AX = mybir.AxisListType

S = 2048
T = 4096
H = 2048
NH = 16
D = 128
II = 8192
CH = 128
NCH = S // CH
NKT = H // 128
NTT = T // 128
NBH = 2 * NH
LNC = float(np.log(D ** -0.5))

ENGINES = ("sync", "gpsimd", "vector", "scalar", "tensor")


class Sem:
    __slots__ = ("h", "n", "name")

    def __init__(self, h, name):
        self.h = h
        self.n = 0
        self.name = name

    @property
    def now(self):
        return (self, self.n)


class Sched:

    def __init__(self, nc):
        self.nc = nc
        self.ctx = contextlib.ExitStack()
        self.ops = {e: [] for e in ENGINES}
        self.sems = {}
        self.seen = {e: {} for e in ENGINES}
        self.guards = []

    def sem(self, name):
        pool = name.split("_", 1)[1] if "_" in name else name
        if pool not in self.sems:
            h = self.ctx.enter_context(self.nc.semaphore(pool))
            self.sems[pool] = Sem(h, pool)
        return self.sems[pool]

    def emit(self, eng, fn, waits=(), inc=None, by=1):
        tick = None
        if inc is not None:
            inc.n += by
            tick = (inc, by)
        flat = []
        for w in waits:
            if w is None:
                continue
            if isinstance(w, list):
                flat.extend(x for x in w if x is not None)
            else:
                flat.append(w)
        flat = list(self.guards) + flat
        self.ops[eng].append((fn, flat, tick))
        return (inc, inc.n) if inc else None

    def dma(self, eng, out, in_, waits=(), inc=None, **kw):
        assert inc is not None
        return self.emit(eng, lambda e: e.dma_start(out=out, in_=in_, **kw),
                         waits=waits, inc=inc, by=16)

    def flush(self, block):
        for eng in ENGINES:
            if not self.ops[eng]:
                continue

            def body(e, ops_=self.ops[eng], seen=self.seen[eng]):
                for fn, waits, tick in ops_:
                    for sem, cnt in waits:
                        if seen.get(sem.name, -1) >= cnt:
                            continue
                        e.wait_ge(sem.h, cnt)
                        seen[sem.name] = cnt
                    ins = fn(e)
                    if tick is not None:
                        sem, by = tick
                        ins.then_inc(sem.h, by)
            getattr(block, eng)(body)
            self.ops[eng] = []


TENSORS = {
    "x":     ((T, H), F32, None),
    "xn":    ((H, T), BF16, "A"),
    "qs":    ((H, T), BF16, "A1"),
    "ks":    ((H, T), BF16, "A1"),
    "vs":    ((H, T), BF16, "A1"),
    "gsig":  ((H, T), BF16, "A1"),
    "alpha": ((NH, T), F32, "A1"),
    "beta":  ((NH, T), F32, "A1"),
    "hs":    ((NTT, II // 128, 128, 128), BF16, "A2a"),
    "wd_bf": ((II, H), BF16, "A2a"),
    "gf":    ((T, H), BF16, "A2b"),
    "og":    ((H, T), BF16, "B"),
    "x1":    ((T, H), F32, "C"),
    "xn1":   ((H, T), BF16, "E"),
    "q1r":   ((H, T), BF16, "E"),
    "k1r":   ((512, T), BF16, "E"),
    "v1s":   ((512, T), BF16, "E"),
    "s1":    ((T, H), BF16, "F"),
    "wgT":   ((H, II), BF16, "G0"),
    "wuT":   ((H, II), BF16, "G0"),
    "wdT":   ((II, H), BF16, "G0"),
    "h1":    ((NTT, II // 128, 128, 128), BF16, "G1"),
    "out":   ((T, H), F32, "G2"),
}
WEIGHTS = {
    "L0_attn_Wq": ((H, H), F32), "L0_attn_Wk": ((H, H), F32),
    "L0_attn_Wv": ((H, H), F32), "L0_attn_Wa": ((H, NH), F32),
    "L0_attn_ba": ((NH,), F32), "L0_attn_Wb": ((H, NH), F32),
    "L0_attn_bb": ((NH,), F32), "L0_attn_Wg": ((H, H), F32),
    "L0_attn_Wo": ((H, H), F32),
    "L0_attn_qcw": ((H, 4), F32), "L0_attn_qcb": ((H,), F32),
    "L0_attn_kcw": ((H, 4), F32), "L0_attn_kcb": ((H,), F32),
    "L0_attn_vcw": ((H, 4), F32), "L0_attn_vcb": ((H,), F32),
    "L0_attn_ln_w": ((D,), F32), "L0_attn_ln_b": ((D,), F32),
    "L0_ffn_w_gate": ((H, II), F32), "L0_ffn_w_up": ((H, II), F32),
    "L0_ffn_w_down": ((II, H), F32),
    "L1_attn_Wq": ((H, H), F32), "L1_attn_Wk": ((H, 512), F32),
    "L1_attn_Wv": ((H, 512), F32), "L1_attn_Wo": ((H, H), F32),
    "L1_ffn_gate_w": ((II, H), I32), "L1_ffn_gate_scale": ((II,), F32),
    "L1_ffn_up_w": ((II, H), I32), "L1_ffn_up_scale": ((II,), F32),
    "L1_ffn_down_w": ((H, II), I32), "L1_ffn_down_scale": ((H,), F32),
}
STAGE_DEPS = {
    "A":   ["x"],
    "A1":  ["xn", "L0_attn_Wq", "L0_attn_Wk", "L0_attn_Wv", "L0_attn_Wg",
            "L0_attn_Wa", "L0_attn_ba", "L0_attn_Wb", "L0_attn_bb",
            "L0_attn_qcw", "L0_attn_qcb", "L0_attn_kcw", "L0_attn_kcb",
            "L0_attn_vcw", "L0_attn_vcb"],
    "A2a": ["xn", "L0_ffn_w_gate", "L0_ffn_w_up", "L0_ffn_w_down"],
    "A2b": ["hs", "wd_bf"],
    "B":   ["qs", "ks", "vs", "gsig", "alpha", "beta",
            "L0_attn_ln_w", "L0_attn_ln_b"],
    "C":   ["og", "x", "gf", "L0_attn_Wo"],
    "E":   ["x1", "L1_attn_Wq", "L1_attn_Wk", "L1_attn_Wv"],
    "F":   ["q1r", "k1r", "v1s", "L1_attn_Wo"],
    "G0":  ["L1_ffn_gate_w", "L1_ffn_up_w", "L1_ffn_down_w",
            "L1_ffn_down_scale"],
    "G1":  ["xn1", "wgT", "wuT", "L1_ffn_gate_scale", "L1_ffn_up_scale"],
    "G2":  ["h1", "wdT", "x1", "s1"],
}
STAGE_OUTS = {
    "A": ["xn"], "A1": ["qs", "ks", "vs", "gsig", "alpha", "beta"],
    "A2a": ["hs", "wd_bf"], "A2b": ["gf"], "B": ["og"], "C": ["x1"],
    "E": ["xn1", "q1r", "k1r", "v1s"], "F": ["s1"],
    "G0": ["wgT", "wuT", "wdT"], "G1": ["h1"], "G2": ["out"],
}
ALL_STAGES = ["A", "A1", "A2a", "A2b", "B", "G0", "C", "E", "F", "G1", "G2"]

GROUPS = [list(ALL_STAGES)]


class Prog:

    def __init__(self, stages):
        self.stages = stages
        self.nc = bass.Bass("TRN2", target_bir_lowering=False, debug=False)
        self.sched = Sched(self.nc)
        need = set()
        for st in stages:
            need.update(STAGE_DEPS[st])
        prod_here = {t for st in stages for t in STAGE_OUTS[st]}
        later = [st for st in ALL_STAGES if st not in stages]
        self.hbm = {}
        self.input_names = []
        self.output_names = []
        for name in sorted(need | prod_here):
            if name in WEIGHTS:
                shape, dt = WEIGHTS[name]
                self.hbm[name] = self.nc.dram_tensor(
                    name, list(shape), dt, kind="ExternalInput").ap()
                self.input_names.append(name)
                continue
            shape, dt, prod = TENSORS[name]
            if name in prod_here:
                used_later = (name == "out") or any(
                    name in STAGE_DEPS[st2] for st2 in later)
                kind = "ExternalOutput" if used_later else "Internal"
                if kind == "ExternalOutput":
                    self.output_names.append(name)
            else:
                kind = "ExternalInput"
                self.input_names.append(name)
            self.hbm[name] = self.nc.dram_tensor(
                name, list(shape), dt, kind=kind).ap()
        self.sb = {}
        self.psn = {}
        self.stage_stack = None
        self.next_guards = []
        self.ready = {}

    def sbuf(self, name, shape, dt):
        if name not in self.sb:
            if self.stage_stack is not None:
                self.sb[name] = self.stage_stack.enter_context(
                    self.nc.sbuf_tensor(name, list(shape), dt,
                                        allow_name_mangling=True))
            else:
                self.sb[name] = self.nc.alloc_sbuf_tensor(
                    name, list(shape), dt, allow_name_mangling=True)
        return self.sb[name]

    def psum(self, name, shape, dt=F32):
        if name not in self.psn:
            if self.stage_stack is not None:
                self.psn[name] = self.stage_stack.enter_context(
                    self.nc.psum_tensor(name, list(shape), dt,
                                        allow_name_mangling=True))
            else:
                self.psn[name] = self.nc.alloc_psum_tensor(
                    name, list(shape), dt, allow_name_mangling=True)
        return self.psn[name]

    def begin_stage(self):
        self.stage_stack = contextlib.ExitStack()
        self.sb = {}
        self.psn = {}
        self.next_guards = []

    def end_stage(self):
        self.stage_stack.close()
        self.stage_stack = None
        self.sched.guards = list(self.next_guards)

    def rdy(self, name):
        return self.ready.get(name)

    def finish(self, st_sem):
        n = st_sem.n
        self.sched.emit("sync", lambda e, n=n: e.wait_ge(st_sem.h, n),
                        waits=())
        self.next_guards.append((st_sem, n))


def build_consts(p: Prog):
    sc = p.sched
    ones = p.sbuf("c_ones", (128, 128), BF16)
    idm = p.sbuf("c_idm", (128, 128), BF16)
    g_done = sc.sem("c_gp")
    z = sc.emit("gpsimd", lambda e: e.memset(ones[:], 1.0), inc=g_done)
    rdy = sc.emit("gpsimd",
                  lambda e: e.affine_select(idm[:], ones[:], [[1, 128]],
                                            ALU.is_equal, 0.0, base=0,
                                            channel_multiplier=-1),
                  waits=[z], inc=g_done)
    p.consts = {"ones": ones, "idm": idm, "idm_rdy": rdy}


def stage_A(p: Prog):
    stage_ln_generic(p, "x", "xn", "A")


def stage_ln_generic(p: Prog, srcname, dstname, tag):
    sc = p.sched
    x, xn = p.hbm[srcname], p.hbm[dstname]
    xbuf = [p.sbuf(f"{tag}_x{i}", (128, H), F32) for i in range(2)]
    st_t = p.sbuf(tag + "_stats", (128, 8), F32)
    xntm = [p.sbuf(f"{tag}_xntm{i}", (128, H), BF16) for i in range(2)]
    xnfm = [p.sbuf(f"{tag}_xnfm{i}", (128, H), BF16) for i in range(2)]
    pst = [p.psum(f"{tag}_pst{i}", (128, H), BF16) for i in range(2)]
    v, s_, tr = (sc.sem(n) for n in (tag + "_v", tag + "_s", tag + "_tr"))
    lx = [sc.sem(tag + "_lx0"), sc.sem(tag + "_lx1")]
    st = [sc.sem(tag + "_st0"), sc.sem(tag + "_st1")]
    idm = p.consts["idm"]
    idm_rdy = p.consts["idm_rdy"]

    free = [None, None]
    free_tm = [None, None]
    free_fm = [None, None]
    free_ps = [None, None]
    for tt in range(NTT):
        pa = tt % 2
        c0 = 4 * pa
        ld = sc.dma("sync", xbuf[pa][:], x[tt * 128:(tt + 1) * 128, :],
                    waits=[free[pa], p.rdy(srcname)], inc=lx[pa])
        r1 = sc.emit("vector", lambda e, pa=pa, c0=c0:
                     e.reduce_sum(st_t[:, c0:c0 + 1], xbuf[pa][:], axis=AX.X),
                     waits=[ld], inc=v)
        r2 = sc.emit("vector", lambda e, c0=c0:
                     e.tensor_scalar(st_t[:, c0 + 1:c0 + 2], st_t[:, c0:c0 + 1],
                                     1.0 / H, None, ALU.mult),
                     waits=[r1], inc=v)
        r3 = sc.emit("vector", lambda e, pa=pa, c0=c0:
                     e.tensor_scalar(xbuf[pa][:], xbuf[pa][:],
                                     st_t[:, c0 + 1:c0 + 2], None, ALU.subtract),
                     waits=[r2], inc=v)
        r4 = sc.emit("scalar", lambda e, pa=pa, c0=c0:
                     e.activation(xntm[pa][:], xbuf[pa][:], AF.Square,
                                  accum_out=st_t[:, c0 + 2:c0 + 3]),
                     waits=[r3, free_tm[pa]], inc=s_)
        r5 = sc.emit("vector", lambda e, c0=c0:
                     e.tensor_scalar(st_t[:, c0 + 3:c0 + 4], st_t[:, c0 + 2:c0 + 3],
                                     1.0 / H, 1e-5, ALU.mult, ALU.add),
                     waits=[r4], inc=v)
        r6 = sc.emit("scalar", lambda e, c0=c0:
                     e.activation(st_t[:, c0 + 2:c0 + 3], st_t[:, c0 + 3:c0 + 4],
                                  AF.Sqrt),
                     waits=[r5], inc=s_)
        r7 = sc.emit("vector", lambda e, c0=c0:
                     e.reciprocal(st_t[:, c0:c0 + 1], st_t[:, c0 + 2:c0 + 3]),
                     waits=[r6], inc=v)
        r8 = sc.emit("vector", lambda e, pa=pa, c0=c0:
                     e.tensor_scalar(xntm[pa][:], xbuf[pa][:], st_t[:, c0:c0 + 1],
                                     None, ALU.mult),
                     waits=[r7, r4], inc=v)
        free[pa] = r8

        def trans(e, pa=pa):
            for fb in range(16):
                ins = e.transpose(pst[pa][:, fb * 128:(fb + 1) * 128],
                                  xntm[pa][:, fb * 128:(fb + 1) * 128], idm[:])
            return ins
        r9 = sc.emit("tensor", trans, waits=[r8, idm_rdy, free_ps[pa]], inc=tr)
        free_tm[pa] = r9
        r10 = sc.emit("scalar", lambda e, pa=pa:
                      e.activation(xnfm[pa][:], pst[pa][:], AF.Copy),
                      waits=[r9, free_fm[pa]], inc=s_)
        free_ps[pa] = r10
        wr = sc.dma("sync",
                    xn.rearrange("(fb p) t -> p fb t", p=128)[:, :, tt * 128:(tt + 1) * 128],
                    xnfm[pa].ap().rearrange("p (fb c) -> p fb c", fb=16),
                    waits=[r10], inc=st[pa])
        free_fm[pa] = wr
    p.ready[dstname] = [(st[0], st[0].n), (st[1], st[1].n)]
    p.finish(st[0])
    p.finish(st[1])


def load_xn_res(p: Prog, name="xn"):
    sc = p.sched
    xn = p.hbm[name]
    xn_sb = p.sbuf(f"{name}_res", (128, NKT * T), BF16)
    lsem = sc.sem(f"{name}_res_l")
    for kt in range(NKT):
        sc.dma("sync", xn_sb[:, kt * T:(kt + 1) * T],
               xn[kt * 128:(kt + 1) * 128, :],
               waits=[p.rdy(name)], inc=lsem)
    return xn_sb, (lsem, lsem.n)


def stage_A1(p: Prog):
    sc = p.sched
    xn_sb, xn_rdy = load_xn_res(p)
    lw = [sc.sem("A1_lw0"), sc.sem("A1_lw1")]
    lcw = [sc.sem("A1_lcw0"), sc.sem("A1_lcw1")]
    lab = sc.sem("A1_lab")
    gp = sc.sem("A1_gp")
    mm = sc.sem("A1_mm")
    ve = sc.sem("A1_v")
    se = sc.sem("A1_s")
    st = [sc.sem("A1_st0"), sc.sem("A1_st1")]
    stab = sc.sem("A1_stab")
    wbuf = [p.sbuf(f"A1_w{i}", (128, NKT * 128), BF16) for i in range(2)]
    cwb = [p.sbuf(f"A1_cw{i}", (128, 5), F32) for i in range(2)]
    lin = p.sbuf("A1_lin", (128, 3 + S), F32)
    cv = [p.sbuf(f"A1_cv{i}", (128, 512), F32) for i in range(2)]
    ot = [p.sbuf(f"A1_o{i}", (128, 512), BF16) for i in range(2)]
    ps = [p.psum(f"A1_ps{i}", (128, 512), F32) for i in range(2)]

    projs = [("q", "L0_attn_Wq", "L0_attn_qcw", "L0_attn_qcb", "qs", AF.Silu),
             ("k", "L0_attn_Wk", "L0_attn_kcw", "L0_attn_kcb", "ks", AF.Silu),
             ("v", "L0_attn_Wv", "L0_attn_vcw", "L0_attn_vcb", "vs", AF.Silu),
             ("g", "L0_attn_Wg", None, None, "gsig", AF.Sigmoid)]
    wfree = [None, None]
    cwfree = [None, None]
    psfree = [None, None]
    cvfree = [None, None]
    otfree = [None, None]
    linfree = None
    n = 0
    for pi, (pname, wname, cwname, cbname, oname, act) in enumerate(projs):
        W = p.hbm[wname]
        dst = p.hbm[oname]
        for db in range(16):
            wp = n % 2
            lw_t = sc.dma(
                "gpsimd",
                wbuf[wp].ap().rearrange("p (kt c) -> p kt c", kt=NKT),
                W.rearrange("(kt p) n -> p kt n", p=128)[:, :, db * 128:(db + 1) * 128],
                waits=[wfree[wp]], inc=lw[wp])
            if cwname is not None:
                c1 = sc.dma("gpsimd", cwb[wp][:, 0:4],
                            p.hbm[cwname][db * 128:(db + 1) * 128, :],
                            waits=[cwfree[wp]], inc=lcw[wp])
                c2 = sc.dma("gpsimd", cwb[wp][:, 4:5],
                            p.hbm[cbname][db * 128:(db + 1) * 128][:, None],
                            inc=lcw[wp])
                cw_rdy = c2
            else:
                cw_rdy = None
            for bseq in range(2):
                if cwname is not None:
                    z = sc.emit("gpsimd", lambda e: e.memset(lin[:, 0:3], 0.0),
                                waits=[linfree], inc=gp)
                for tg in range(4):
                    pp = (n * 8 + bseq * 4 + tg) % 2
                    tok0 = bseq * S + tg * 512

                    def mmg(e, wp=wp, tok0=tok0, pp=pp):
                        for kt in range(NKT):
                            ins = e.matmul(ps[pp][:],
                                           wbuf[wp][:, kt * 128:(kt + 1) * 128],
                                           xn_sb[:, kt * T + tok0: kt * T + tok0 + 512],
                                           start=(kt == 0), stop=(kt == NKT - 1))
                        return ins
                    g = sc.emit("tensor", mmg,
                                waits=[lw_t, xn_rdy, psfree[pp]], inc=mm)
                    if cwname is not None:
                        ev = sc.emit("scalar", lambda e, tg=tg, pp=pp:
                                     e.activation(lin[:, 3 + tg * 512: 3 + (tg + 1) * 512],
                                                  ps[pp][:], AF.Copy),
                                     waits=[g, z], inc=se)
                        psfree[pp] = ev
                        t3 = sc.emit("vector", lambda e, tg=tg, pp=pp, wp=wp:
                                     e.tensor_scalar(cv[pp][:],
                                                     lin[:, 3 + tg * 512: 3 + tg * 512 + 512],
                                                     cwb[wp][:, 3:4], cwb[wp][:, 4:5],
                                                     ALU.mult, ALU.add),
                                     waits=[ev, cw_rdy, cvfree[pp]], inc=ve)
                        acc = t3
                        for j in (2, 1, 0):
                            acc = sc.emit("vector", lambda e, tg=tg, pp=pp, wp=wp, j=j:
                                          e.scalar_tensor_tensor(
                                              cv[pp][:],
                                              lin[:, j + tg * 512: j + tg * 512 + 512],
                                              cwb[wp][:, j:j + 1], cv[pp][:],
                                              ALU.mult, ALU.add),
                                          waits=[acc], inc=ve)
                        if tg == 3:
                            linfree = acc
                        a = sc.emit("scalar", lambda e, pp=pp, act=act:
                                    e.activation(ot[pp][:], cv[pp][:], act),
                                    waits=[acc, otfree[pp]], inc=se)
                        cvfree[pp] = a
                    else:
                        a = sc.emit("scalar", lambda e, pp=pp, act=act:
                                    e.activation(ot[pp][:], ps[pp][:], act),
                                    waits=[g, otfree[pp]], inc=se)
                        psfree[pp] = a
                    wr = sc.dma("sync",
                                dst[db * 128:(db + 1) * 128, tok0:tok0 + 512],
                                ot[pp][:], waits=[a], inc=st[pp])
                    otfree[pp] = wr
            wfree[wp] = g
            if cwname is not None:
                cwfree[wp] = t3
            n += 1

    wab = p.sbuf("A1_wab", (128, NKT * 16), BF16)
    bab = p.sbuf("A1_bab", (16, 1), F32)
    absb = [p.sbuf(f"A1_ab{i}", (16, 512), F32) for i in range(2)]
    psab = [p.psum(f"A1_psab{i}", (16, 512), F32) for i in range(2)]
    evp = [None, None]
    wrp = [None, None]
    ab_wfree = None
    ab_bfree = None
    for name, wname, bname, dname in (("a", "L0_attn_Wa", "L0_attn_ba", "alpha"),
                                      ("b", "L0_attn_Wb", "L0_attn_bb", "beta")):
        lw_t = None
        for kt in range(NKT):
            lw_t = sc.dma("gpsimd", wab[:, kt * 16:(kt + 1) * 16],
                          p.hbm[wname][kt * 128:(kt + 1) * 128, :],
                          waits=[ab_wfree if kt == 0 else None], inc=lab)
        lb = sc.dma("gpsimd", bab[:], p.hbm[bname][:, None],
                    waits=[ab_bfree], inc=lab)
        for tgg in range(8):
            pp = tgg % 2

            def mmab(e, tgg=tgg, pp=pp):
                for kt in range(NKT):
                    ins = e.matmul(psab[pp][:],
                                   wab[:, kt * 16:(kt + 1) * 16],
                                   xn_sb[:, kt * T + tgg * 512: kt * T + tgg * 512 + 512],
                                   start=(kt == 0), stop=(kt == NKT - 1))
                return ins
            g = sc.emit("tensor", mmab, waits=[lw_t, xn_rdy, evp[pp]], inc=mm)
            ev = sc.emit("scalar", lambda e, pp=pp:
                         e.activation(absb[pp][:], psab[pp][:], AF.Sigmoid,
                                      bias=bab[:]),
                         waits=[g, lb, wrp[pp]], inc=se)
            evp[pp] = ev
            wr = sc.dma("sync", p.hbm[dname][:, tgg * 512:(tgg + 1) * 512],
                        absb[pp][:], waits=[ev], inc=stab)
            wrp[pp] = wr
        ab_wfree = g
        ab_bfree = ev

    for nm in STAGE_OUTS["A1"]:
        p.ready[nm] = [(st[0], st[0].n), (st[1], st[1].n), (stab, stab.n)]
    p.finish(st[0])
    p.finish(st[1])
    p.finish(stab)



def stage_A2a(p: Prog):
    sc = p.sched
    xn_sb, xn_rdy = load_xn_res(p)
    wg, wu, hsd = p.hbm["L0_ffn_w_gate"], p.hbm["L0_ffn_w_up"], p.hbm["hs"]
    lgs = sc.sem("A2a_lg"); lus = sc.sem("A2a_lu")
    lwd = [sc.sem("A2a_lwd0"), sc.sem("A2a_lwd1")]
    mm = sc.sem("A2a_mm"); ve = sc.sem("A2a_v")
    se = sc.sem("A2a_s"); gp = sc.sem("A2a_gp")
    st = [sc.sem("A2a_st0"), sc.sem("A2a_st1")]
    swd = [sc.sem("A2a_swd0"), sc.sem("A2a_swd1")]
    NGB = 16
    wgb = p.sbuf("A2a_wg", (128, 4 * NKT * 128), BF16)
    wub = p.sbuf("A2a_wu", (128, 4 * NKT * 128), BF16)
    hg = [p.sbuf(f"A2a_hg{i}", (128, 512), F32) for i in range(2)]
    hh = [p.sbuf(f"A2a_hh{i}", (128, 4 * 512), BF16) for i in range(2)]
    sq = [p.sbuf(f"A2a_sq{i}", (128, 4 * 512), BF16) for i in range(2)]
    bcmp = [p.sbuf(f"A2a_b{i}", (128, 6 * 512), BF16) for i in range(2)]
    wt = [p.sbuf(f"A2a_wt{i}", (128, 4 * 512), BF16) for i in range(2)]
    psg = [p.psum(f"A2a_psg{i}", (128, 512), F32) for i in range(2)]
    psu = [p.psum(f"A2a_psu{i}", (128, 512), F32) for i in range(2)]

    wfree = None
    gfree = [[None, None], [None, None]]
    hgfree = [None, None]
    hhfree = [None, None]
    sqfree = [None, None]
    bfree = [None, None]
    wtfree = [None, None]
    PAIRS = [(0, 1), (0, 2), (0, 3), (1, 2), (1, 3), (2, 3)]
    for gb in range(NGB):
        lg = lu = None
        for m in range(4):
            col0 = (m * 2048 + gb * 128)
            lg = sc.dma("gpsimd",
                        wgb.ap().rearrange("p (m kt c) -> p (m kt) c", m=4, kt=NKT)[:, m * NKT:(m + 1) * NKT, :],
                        wg.rearrange("(kt p) n -> p kt n", p=128)[:, :, col0:col0 + 128],
                        waits=[wfree if m == 0 else None], inc=lgs)
            lu = sc.dma("gpsimd",
                        wub.ap().rearrange("p (m kt c) -> p (m kt) c", m=4, kt=NKT)[:, m * NKT:(m + 1) * NKT, :],
                        wu.rearrange("(kt p) n -> p kt n", p=128)[:, :, col0:col0 + 128],
                        inc=lus)
        for tg in range(8):
            pa = (gb * 8 + tg) % 2
            tok0 = tg * 512
            hdeps = []
            for m in range(4):
                gmm = sc.emit("tensor", lambda e, m=m, tok0=tok0, pa=pa:
                              mk_mm2(e, psg[pa], wgb[:, m * NKT * 128:(m + 1) * NKT * 128],
                                     xn_sb, tok0),
                              waits=[lg, xn_rdy, gfree[pa][0]], inc=mm)
                umm = sc.emit("tensor", lambda e, m=m, tok0=tok0, pa=pa:
                              mk_mm2(e, psu[pa], wub[:, m * NKT * 128:(m + 1) * NKT * 128],
                                     xn_sb, tok0),
                              waits=[lu, gfree[pa][1]], inc=mm)
                e1 = sc.emit("scalar", lambda e, pa=pa:
                             e.activation(hg[pa][:], psg[pa][:], AF.Silu),
                             waits=[gmm, hgfree[pa]], inc=se)
                gfree[pa][0] = e1
                e2 = sc.emit("vector", lambda e, pa=pa, m=m:
                             e.tensor_tensor(hh[pa][:, m * 512:(m + 1) * 512],
                                             hg[pa][:], psu[pa][:], ALU.mult),
                             waits=[umm, e1, hhfree[pa]], inc=ve)
                gfree[pa][1] = e2
                hgfree[pa] = e2
                hdeps.append(e2)
                s0 = sc.emit("gpsimd", lambda e, pa=pa, m=m:
                             e.tensor_tensor(sq[pa][:, m * 512:(m + 1) * 512],
                                             hh[pa][:, m * 512:(m + 1) * 512],
                                             hh[pa][:, m * 512:(m + 1) * 512], ALU.mult),
                             waits=[e2, sqfree[pa]], inc=gp)
                hdeps.append(s0)
            sqv = lambda m, pa: sq[pa][:, m * 512:(m + 1) * 512]
            cb = []
            for ci, (i, j) in enumerate(PAIRS):
                cb.append(sc.emit("vector", lambda e, pa=pa, ci=ci, i=i, j=j:
                          e.tensor_tensor(bcmp[pa][:, ci * 512:(ci + 1) * 512],
                                          sq[pa][:, i * 512:(i + 1) * 512],
                                          sq[pa][:, j * 512:(j + 1) * 512],
                                          ALU.is_ge),
                          waits=hdeps + [bfree[pa]], inc=ve))
            sqfree[pa] = cb[-1]
            B_ = lambda ci, pa=pa: bcmp[pa][:, ci * 512:(ci + 1) * 512]
            W_ = lambda i, pa=pa: wt[pa][:, i * 512:(i + 1) * 512]
            deps = cb + [wtfree[pa]]
            a0 = sc.emit("gpsimd", lambda e, pa=pa: e.tensor_add(
                wt[pa][:, 0:512], bcmp[pa][:, 0:512], bcmp[pa][:, 512:1024]),
                waits=deps, inc=gp)
            a1 = sc.emit("vector", lambda e, pa=pa: e.tensor_add(
                wt[pa][:, 512:1024], bcmp[pa][:, 3 * 512:4 * 512],
                bcmp[pa][:, 4 * 512:5 * 512]), waits=deps, inc=ve)
            a2 = sc.emit("gpsimd", lambda e, pa=pa: e.tensor_sub(
                wt[pa][:, 1024:1536], bcmp[pa][:, 5 * 512:6 * 512],
                bcmp[pa][:, 512:1024]), waits=deps, inc=gp)
            a3 = sc.emit("vector", lambda e, pa=pa: e.tensor_add(
                wt[pa][:, 1536:2048], bcmp[pa][:, 2 * 512:3 * 512],
                bcmp[pa][:, 4 * 512:5 * 512]), waits=deps, inc=ve)
            a0b = sc.emit("gpsimd", lambda e, pa=pa: e.tensor_add(
                wt[pa][:, 0:512], wt[pa][:, 0:512], bcmp[pa][:, 2 * 512:3 * 512]),
                waits=[a0], inc=gp)
            a1b = sc.emit("vector", lambda e, pa=pa: e.tensor_sub(
                wt[pa][:, 512:1024], wt[pa][:, 512:1024], bcmp[pa][:, 0:512]),
                waits=[a1], inc=ve)
            a2b = sc.emit("gpsimd", lambda e, pa=pa: e.tensor_sub(
                wt[pa][:, 1024:1536], wt[pa][:, 1024:1536], bcmp[pa][:, 3 * 512:4 * 512]),
                waits=[a2], inc=gp)
            a3b = sc.emit("vector", lambda e, pa=pa: e.tensor_add(
                wt[pa][:, 1536:2048], wt[pa][:, 1536:2048], bcmp[pa][:, 5 * 512:6 * 512]),
                waits=[a3], inc=ve)
            bfree[pa] = a3b
            thr = [(a0b, ALU.is_ge, 2.0, "gpsimd", gp), (a1b, ALU.is_ge, 1.0, "vector", ve),
                   (a2b, ALU.is_ge, 0.0, "gpsimd", gp), (a3b, ALU.is_le, 1.0, "vector", ve)]
            mdep = []
            for i, (dep, op, c, eng, sm) in enumerate(thr):
                mdep.append(sc.emit(eng, lambda e, pa=pa, i=i, op=op, c=c:
                            e.tensor_single_scalar(wt[pa][:, i * 512:(i + 1) * 512],
                                                   wt[pa][:, i * 512:(i + 1) * 512],
                                                   c, op),
                            waits=[dep], inc=sm))
            aps = []
            for i in range(4):
                eng, sm = ("vector", ve) if i % 2 == 0 else ("gpsimd", gp)
                aps.append(sc.emit(eng, lambda e, pa=pa, i=i:
                           e.tensor_tensor(hh[pa][:, i * 512:(i + 1) * 512],
                                           hh[pa][:, i * 512:(i + 1) * 512],
                                           wt[pa][:, i * 512:(i + 1) * 512], ALU.mult),
                           waits=[mdep[i]], inc=sm))
            wtfree[pa] = aps[3]
            wr = None
            for m in range(4):
                wr = sc.dma("sync",
                            hsd[tg * 4:tg * 4 + 4, m * 16 + gb]
                            .rearrange("a p t -> p a t"),
                            hh[pa][:, m * 512:(m + 1) * 512]
                            .rearrange("p (a t) -> p a t", a=4),
                            waits=[aps[m]], inc=st[pa])
            hhfree[pa] = wr
        wfree = umm
    wdin, wdout = p.hbm["L0_ffn_w_down"], p.hbm["wd_bf"]
    wdb = [p.sbuf(f"A2a_wd{i}", (128, H // 2), BF16) for i in range(2)]
    wdfree = [None, None]
    for ib2 in range(II // 64):
        pa = ib2 % 2
        r0 = ib2 * 64
        l = sc.dma("gpsimd",
                   wdb[pa][:],
                   wdin.rearrange("(q p) c -> p q c", p=128)[:, ib2 // 2, :][:, (ib2 % 2) * 1024:(ib2 % 2) * 1024 + 1024],
                   waits=[wdfree[pa]], inc=lwd[pa])
        w = sc.dma("sync",
                   wdout.rearrange("(q p) c -> p q c", p=128)[:, ib2 // 2, :][:, (ib2 % 2) * 1024:(ib2 % 2) * 1024 + 1024],
                   wdb[pa][:],
                   waits=[l], inc=swd[pa])
        wdfree[pa] = w
    p.ready["hs"] = [(st[0], st[0].n), (st[1], st[1].n)]
    p.ready["wd_bf"] = [(swd[0], swd[0].n), (swd[1], swd[1].n)]
    p.finish(st[0])
    p.finish(st[1])
    p.finish(swd[0])
    p.finish(swd[1])


def mk_mm2(e, dst, src, xn_sb, tok0):
    for kt in range(NKT):
        ins = e.matmul(dst[:], src[:, kt * 128:(kt + 1) * 128],
                       xn_sb[:, kt * T + tok0: kt * T + tok0 + 512],
                       start=(kt == 0), stop=(kt == NKT - 1))
    return ins


def stage_A2b(p: Prog):
    sc = p.sched
    hsd, wdb, gfd = p.hbm["hs"], p.hbm["wd_bf"], p.hbm["gf"]
    lw = sc.sem("A2b_lw")
    lh = [sc.sem("A2b_lh0"), sc.sem("A2b_lh1")]
    mm = sc.sem("A2b_mm")
    se = sc.sem("A2b_s")
    st = [sc.sem("A2b_st0"), sc.sem("A2b_st1")]
    NIB = II // 128
    wd_res = p.sbuf("A2b_wd", (128, NIB * 1024), BF16)
    hcol = [p.sbuf(f"A2b_h{i}", (128, NIB * 128), BF16) for i in range(2)]
    gfb = [p.sbuf(f"A2b_gf{i}", (128, 1024), BF16) for i in range(2)]
    psf = [[p.psum(f"A2b_ps{i}_{j}", (128, 512), F32) for j in range(2)]
           for i in range(2)]
    wd_rdy = None
    hfree = [None, None]
    psfree = [[None, None], [None, None]]
    gffree = [None, None]
    for half in range(2):
        for ib in range(NIB):
            wd_rdy = sc.dma("gpsimd", wd_res[:, ib * 1024:(ib + 1) * 1024],
                            wdb[ib * 128:(ib + 1) * 128,
                                half * 1024:(half + 1) * 1024],
                            waits=[p.rdy("wd_bf"),
                                   psfree[0][1] if ib == 0 else None,
                                   psfree[1][1] if ib == 0 else None],
                            inc=lw)
        for tt in range(NTT):
            pa = tt % 2
            lc = sc.dma("sync", hcol[pa].ap().rearrange("p (ib t) -> p ib t", ib=NIB),
                        hsd[tt].rearrange("ib p t -> p ib t"),
                        waits=[p.rdy("hs"), hfree[pa]], inc=lh[pa])

            def mmd(e, pa=pa):
                for ib in range(NIB):
                    for fg in range(2):
                        ins = e.matmul(psf[pa][fg][:],
                                       hcol[pa][:, ib * 128:(ib + 1) * 128],
                                       wd_res[:, ib * 1024 + fg * 512: ib * 1024 + fg * 512 + 512],
                                       start=(ib == 0), stop=(ib == NIB - 1))
                return ins
            g = sc.emit("tensor", mmd,
                        waits=[wd_rdy, lc, psfree[pa][0], psfree[pa][1]], inc=mm)
            hfree[pa] = g
            ev = None
            for fg in range(2):
                ev = sc.emit("scalar", lambda e, pa=pa, fg=fg:
                             e.activation(gfb[pa][:, fg * 512:(fg + 1) * 512],
                                          psf[pa][fg][:], AF.Silu),
                             waits=[g, gffree[pa]], inc=se)
                psfree[pa][fg] = ev
            wr = sc.dma("sync",
                        gfd[tt * 128:(tt + 1) * 128, half * 1024:(half + 1) * 1024],
                        gfb[pa][:], waits=[ev], inc=st[pa])
            gffree[pa] = wr
    p.ready["gf"] = [(st[0], st[0].n), (st[1], st[1].n)]
    p.finish(st[0])
    p.finish(st[1])



B_LIMIT_CH = None
B_LIMIT_BH = None
B_PRO = 99
B_NO_MEMSET = False
B_INST = 99
INV_STAGES = 1

def stage_B(p: Prog):
    sc = p.sched
    nc = p.nc
    qs, ks, vs, gs = p.hbm["qs"], p.hbm["ks"], p.hbm["vs"], p.hbm["gsig"]
    ogd = p.hbm["og"]
    rows_d = nc.dram_tensor("B_rows", [NBH * NCH, 256], F32, kind="Internal").ap()
    gbd = nc.dram_tensor("B_gbd", [NBH * NCH], F32, kind="Internal").ap()
    gcd = nc.dram_tensor("B_gcd", [NBH * NCH], F32, kind="Internal").ap()

    lb1 = sc.sem("B_l1"); lb2 = sc.sem("B_l2"); lb3 = sc.sem("B_l3")
    ve = sc.sem("B_v"); se = sc.sem("B_s"); gp = sc.sem("B_g")
    mm = sc.sem("B_mm"); rxs = sc.sem("B_rx")
    st = [sc.sem(f"B_st{i}") for i in range(4)]
    ld = [sc.sem(f"B_ld{i}") for i in range(4)]
    lr = [sc.sem(f"B_lr{i}") for i in range(4)]
    PBbc = p.psum("B_ps_bc2", (128, 512), F32)
    PBkk = p.psum("B_ps_kk2", (128, 512), F32)
    PBinv = p.psum("B_ps_inv2", (128, 512), F32)
    PBsd = p.psum("B_ps_sd", (128, 512), F32)
    PBu = p.psum("B_ps_u2", (128, 512), F32)
    PBo = p.psum("B_ps_o2", (128, 512), F32)
    PBfa = p.psum("B_ps_fa", (128, 768), BF16)
    PBfb = p.psum("B_ps_fb", (128, 256), BF16)
    pstr = PBbc

    tA = p.sbuf("B_tA", (16, T), F32)
    tB = p.sbuf("B_tB", (16, T), F32)
    tL1 = p.sbuf("B_tL1", (16, T), F32)
    tL2 = p.sbuf("B_tL2", (16, T), F32)
    tG = p.sbuf("B_tG", (16, T), F32)
    tH = p.sbuf("B_tH", (16, T), F32)
    on16 = p.sbuf("B_on16", (16, S), F32)
    sm16 = p.sbuf("B_sm16", (16, 96), F32)
    al = lambda: tA[:]
    bt = lambda: tB[:]
    lna_ = lambda: tL1[:]
    lnb_ = lambda: tL2[:]
    G_ = lambda: tG[:]
    Hp_ = lambda: tH[:]

    l1 = sc.dma("sync", tA[:], p.hbm["alpha"], waits=[p.rdy("alpha")], inc=lb1)
    l2 = sc.dma("sync", tB[:], p.hbm["beta"], waits=[p.rdy("beta")], inc=lb2)
    e1 = sc.emit("scalar", lambda e: e.activation(lna_(), al(), AF.Ln),
                 waits=[l1], inc=se)
    e2 = sc.emit("scalar", lambda e: e.activation(lnb_(), bt(), AF.Ln),
                 waits=[l2], inc=se)
    z1 = sc.emit("gpsimd", lambda e: e.memset(on16[:], 1.0), inc=gp)
    g1 = sc.emit("vector", lambda e: e.tensor_tensor_scan(
        G_()[:, 0:S], lna_()[:, 0:S], on16[:], 0.0, ALU.add, ALU.mult),
        waits=[e1, z1], inc=ve)
    g2 = sc.emit("vector", lambda e: e.tensor_tensor_scan(
        G_()[:, S:T], lna_()[:, S:T], on16[:], 0.0, ALU.add, ALU.mult),
        waits=[e1, z1], inc=ve)
    h1 = sc.emit("vector", lambda e: e.scalar_tensor_tensor(
        Hp_(), G_(), 2.0 * LNC, lna_(), ALU.add, ALU.subtract),
        waits=[g1, g2, e2], inc=ve)
    h2 = sc.emit("vector", lambda e: e.tensor_tensor(Hp_(), Hp_(), lnb_(), ALU.add),
                 waits=[h1], inc=ve)

    if B_PRO < 2:
        p.finish(st[0])
        return
    idmf = p.sbuf("B_idmf", (128, 128), F32)
    onesf = p.sbuf("B_onesf", (128, 128), F32)
    zi = sc.emit("gpsimd", lambda e: e.memset(onesf[:], 1.0), inc=gp)
    idr = sc.emit("gpsimd", lambda e: e.affine_select(
        idmf[:], onesf[:], [[1, 128]], ALU.is_equal, 0.0, base=0,
        channel_multiplier=-1), waits=[zi], inc=gp)
    onesb = p.consts["ones"]
    idm = p.consts["idm"]
    cmask = p.sbuf("B_cmask", (128, 256), F32)
    zt128 = p.sbuf("B_zt", (128, 128), F32)
    zn = sc.emit("gpsimd", lambda e: e.memset(zt128[:], 0.0), inc=gp)
    mk1 = sc.emit("gpsimd", lambda e: e.affine_select(
        cmask[:, 0:128], zt128[:], [[1, 128]], ALU.is_gt, -1e30, base=0,
        channel_multiplier=-1), waits=[zn], inc=gp)
    mk2 = sc.emit("gpsimd", lambda e: e.affine_select(
        cmask[:, 128:256], zt128[:], [[1, 128]], ALU.is_ge, -1e30, base=0,
        channel_multiplier=-1), waits=[zn], inc=gp)

    posNG = p.sbuf("B_posNG", (128, 512), F32)
    posB = p.sbuf("B_posB", (128, 512), F32)
    posBGC = p.sbuf("B_posBGC", (128, 512), F32)

    def transpose_seq(srcfn, dstT, deps):
        done = []
        for b_ in range(2):
            for blk in range(NCH):
                col = (b_ * 16 + blk) * 16
                tr = sc.emit("tensor", lambda e, b_=b_, blk=blk, srcfn=srcfn:
                             e.transpose(pstr[:, (blk % 4) * 128:(blk % 4) * 128 + 16],
                                         srcfn()[:, b_ * S + blk * 128: b_ * S + blk * 128 + 128],
                                         idmf[0:16, 0:16]),
                             waits=list(deps) + [idr,
                                    done[-1] if done else None],
                             inc=mm)
                ev = sc.emit("vector" if (blk % 2 == 0) else "scalar",
                             lambda e, col=col, blk=blk, dstT=dstT:
                             (e.tensor_copy(dstT[:, col:col + 16],
                                            pstr[:, (blk % 4) * 128:(blk % 4) * 128 + 16])
                              if blk % 2 == 0 else
                              e.activation(dstT[:, col:col + 16],
                                           pstr[:, (blk % 4) * 128:(blk % 4) * 128 + 16],
                                           AF.Copy)),
                             waits=[tr], inc=(ve if blk % 2 == 0 else se))
                done.append(ev)
        return done[-1], done[-2]

    if B_PRO < 3:
        p.finish(st[0])
        return
    tb1, tb2 = transpose_seq(bt, posB, [l2])
    if B_PRO < 4:
        p.finish(st[0])
        return
    gpv = sc.emit("vector", lambda e: e.tensor_scalar_add(lna_(), G_(), LNC),
                  waits=[h1], inc=ve)
    ngv = sc.emit("vector", lambda e: e.tensor_scalar_mul(al(), G_(), -1.0),
                  waits=[g1, g2, e1], inc=ve)
    tn1, tn2 = transpose_seq(al, posNG, [ngv])
    if B_PRO < 5:
        p.finish(st[0])
        return
    b1 = sc.emit("scalar", lambda e: e.activation(bt(), Hp_(), AF.Exp),
                 waits=[h2, tb1, tb2], inc=se)
    bgcv = sc.emit("vector", lambda e: e.tensor_scalar_mul(
        bt(), bt(), -float(np.exp(-LNC))), waits=[b1], inc=ve)
    tg1, tg2 = transpose_seq(bt, posBGC, [bgcv])

    if B_PRO < 6:
        p.finish(st[0])
        return
    zb = sc.emit("gpsimd", lambda e: e.memset(sm16[:, 0:32], 0.0), inc=gp)
    c1 = sc.emit("vector", lambda e: e.tensor_copy(
        sm16[:, 1:16], G_()[:, 127:2047:128]), waits=[g1, zb], inc=ve)
    c2 = sc.emit("vector", lambda e: e.tensor_copy(
        sm16[:, 17:32], G_()[:, S + 127:S + 2047:128]), waits=[g2, zb], inc=ve)
    c3 = sc.emit("vector", lambda e: e.tensor_copy(
        sm16[:, 32:48], G_()[:, 127:2048:128]), waits=[g1], inc=ve)
    c4 = sc.emit("vector", lambda e: e.tensor_copy(
        sm16[:, 48:64], G_()[:, S + 127:S + 2048:128]), waits=[g2], inc=ve)
    c5 = sc.emit("vector", lambda e: e.tensor_tensor(
        sm16[:, 64:96], sm16[:, 32:64], sm16[:, 0:32], ALU.subtract),
        waits=[c1, c2, c3, c4], inc=ve)
    c6 = sc.emit("scalar", lambda e: e.activation(sm16[:, 64:96], sm16[:, 64:96],
                                                  AF.Exp), waits=[c5], inc=se)
    rd = rows_d.rearrange("(b h c) w -> h b c w", b=2, h=16)
    for b_ in range(2):
        w1 = sc.dma("sync", rd[:, b_, :, 0:128],
                    Hp_()[:, b_ * S:(b_ + 1) * S].rearrange("h (c w) -> h c w", c=NCH),
                    waits=[h2], inc=rxs)
        w2 = sc.dma("sync", rd[:, b_, :, 128:256],
                    lna_()[:, b_ * S:(b_ + 1) * S].rearrange("h (c w) -> h c w", c=NCH),
                    waits=[gpv], inc=rxs)
        w6 = sc.dma("sync", gcd.rearrange("(b h c) -> h b c", b=2, h=16)[:, b_, :],
                    sm16[:, 64 + b_ * 16:64 + (b_ + 1) * 16],
                    waits=[c6], inc=rxs,
                    allow_slow_non_contiguous=True)
        w7 = sc.dma("sync", gbd.rearrange("(b h c) -> h b c", b=2, h=16)[:, b_, :],
                    sm16[:, b_ * 16:(b_ + 1) * 16],
                    waits=[zb, c1, c2], inc=rxs,
                    allow_slow_non_contiguous=True)

    if B_PRO < 7:
        p.finish(st[0])
        return
    rows_ready = (rxs, rxs.n)
    gcrow = p.sbuf("B_gcrow", (1, 512), F32)
    ones1f = p.sbuf("B_ones1f", (1, 128), F32)
    gC_pos = p.sbuf("B_gCpos", (128, 512), F32)
    zo = sc.emit("gpsimd", lambda e: e.memset(ones1f[:], 1.0), inc=gp)
    lg = sc.dma("sync", gcrow[:], gcd[None, :], waits=[rows_ready], inc=lb3)
    lb3_all = (lb3, lb3.n)
    bc1 = sc.emit("tensor", lambda e: e.matmul(pstr[:], ones1f[:], gcrow[:],
                                               start=True, stop=True),
                  waits=[zo, lb3_all, tg1, tg2], inc=mm)
    bc1e = sc.emit("scalar", lambda e: e.activation(gC_pos[:], pstr[:], AF.Copy),
                   waits=[bc1], inc=se)
    gB_neg = p.sbuf("B_gBneg", (128, 512), F32)
    lb5 = sc.sem("B_l5")
    lgb = sc.dma("sync", gcrow[:], gbd[None, :], waits=[rows_ready, bc1], inc=lb5)
    bc2 = sc.emit("tensor", lambda e: e.matmul(pstr[:], ones1f[:], gcrow[:],
                                               start=True, stop=True),
                  waits=[lgb, bc1e], inc=mm)
    bc2e = sc.emit("scalar", lambda e: e.activation(gB_neg[:], pstr[:], AF.Copy,
                                                    scale=-1.0),
                   waits=[bc2], inc=se)

    if B_PRO < 8:
        p.finish(st[0])
        return
    STf_t = p.sbuf("B_STf", (128, NBH * 128), F32)
    STb_t = p.sbuf("B_STb", (128, NBH * 128), BF16)
    STf = [STf_t[:, i * 128:(i + 1) * 128] for i in range(NBH)]
    STb = [STb_t[:, i * 128:(i + 1) * 128] for i in range(NBH)]
    if B_NO_MEMSET:
        stz = [None for _ in range(NBH)]
    else:
        z1_ = sc.emit("gpsimd", lambda e: e.memset(STf_t[:], 0.0), inc=gp)
        z2_ = sc.emit("gpsimd", lambda e: e.memset(STb_t[:], 0.0),
                      waits=[z1_], inc=gp)
        stz = [z2_ for _ in range(NBH)]


    DEP = 4
    qt = [p.sbuf(f"B_q{i}", (128, 128), BF16) for i in range(DEP)]
    kt = [p.sbuf(f"B_k{i}", (128, 128), BF16) for i in range(DEP)]
    vt = [p.sbuf(f"B_v{i}", (128, 128), BF16) for i in range(DEP)]
    gt = [p.sbuf(f"B_gt{i}", (128, 128), BF16) for i in range(DEP)]
    rowb = [p.sbuf(f"B_row{i}", (1, 256), F32) for i in range(DEP)]
    bci = [p.sbuf(f"B_bci{i}", (128, 256), F32) for i in range(DEP)]
    dm = [p.sbuf(f"B_dm{i}", (128, 256), F32) for i in range(DEP)]
    kkb = [p.sbuf(f"B_kkb{i}", (128, 128), BF16) for i in range(DEP)]
    kqb = [p.sbuf(f"B_kqb{i}", (128, 128), BF16) for i in range(DEP)]
    ndev = [p.sbuf(f"B_ndev{i}", (128, 128), BF16) for i in range(DEP)]
    nmath = [p.sbuf(f"B_nmath{i}", (128, 128), BF16) for i in range(DEP)]
    atb = [p.sbuf(f"B_atb{i}", (128, 128), BF16) for i in range(DEP)]
    ipn = [p.sbuf(f"B_ipn{i}", (128, 128), BF16) for i in range(DEP)]
    pdev = [[p.sbuf(f"B_pdev{i}_{j}", (128, 128), BF16) for j in range(2)]
            for i in range(DEP)]
    pmath = [[p.sbuf(f"B_pmath{i}_{j}", (128, 128), BF16) for j in range(2)]
             for i in range(DEP)]
    sdev = [[p.sbuf(f"B_sdev{i}_{j}", (128, 128), BF16) for j in range(2)]
            for i in range(DEP)]
    ktm = [p.sbuf(f"B_ktm{i}", (128, 128), BF16) for i in range(DEP)]
    kw = [p.sbuf(f"B_kw{i}", (128, 128), BF16) for i in range(DEP)]
    bv = [p.sbuf(f"B_bv{i}", (128, 128), BF16) for i in range(DEP)]
    wtb = [p.sbuf(f"B_wtb{i}", (128, 128), BF16) for i in range(DEP)]
    ub = [p.sbuf(f"B_ub{i}", (128, 128), BF16) for i in range(DEP)]
    udec = [p.sbuf(f"B_udec{i}", (128, 128), BF16) for i in range(DEP)]
    gcol = [p.sbuf(f"B_gcol{i}", (128, 1), F32) for i in range(DEP)]
    psc = [p.sbuf(f"B_psc{i}", (128, 128), F32) for i in range(DEP)]
    osb = [p.sbuf(f"B_osb{i}", (128, 128), F32) for i in range(DEP)]
    oc_ = [p.sbuf(f"B_oc{i}", (128, 128), F32) for i in range(DEP)]
    sqs = [p.sbuf(f"B_sqs{i}", (128, 128), F32) for i in range(DEP)]
    stt_ = [p.sbuf(f"B_stt{i}", (128, 4), F32) for i in range(DEP)]
    olnb = [p.sbuf(f"B_oln{i}", (128, 128), BF16) for i in range(DEP)]
    ogb = [p.sbuf(f"B_og{i}", (128, 128), BF16) for i in range(DEP)]
    lnw_col = p.sbuf("B_lnwc", (128, 1), F32)
    lnb_col = p.sbuf("B_lnbc", (128, 1), F32)
    lb4 = sc.sem("B_l4")
    lnc1 = sc.dma("sync", lnw_col[:], p.hbm["L0_attn_ln_w"][:, None], inc=lb4)
    lnc2 = sc.dma("sync", lnb_col[:], p.hbm["L0_attn_ln_b"][:, None], inc=lb4)
    lnc_all = (lb4, lb4.n)

    prev = {}
    prev["bc_free"] = bc2e
    V = [None] * (NCH * NBH)

    def phase1(i):
        c, bh = divmod(i, NBH)
        b_, h_ = bh // 16, bh % 16
        pa = i % DEP
        bq = (i % 2) * 256
        t0 = b_ * S + c * 128
        row_i = (b_ * 16 + h_) * 16 + c
        col = bh * 16 + c
        colT = (b_ * 16 + c) * 16 + h_
        d = {}
        sc.dma("gpsimd", qt[pa][:], qs[h_ * 128:(h_ + 1) * 128, t0:t0 + 128],
               waits=[p.rdy("qs"), prev.get(f"ld_free{pa}")], inc=ld[pa])
        sc.dma("gpsimd", kt[pa][:], ks[h_ * 128:(h_ + 1) * 128, t0:t0 + 128],
               waits=[p.rdy("ks")], inc=ld[pa])
        sc.dma("gpsimd", vt[pa][:], vs[h_ * 128:(h_ + 1) * 128, t0:t0 + 128],
               waits=[p.rdy("vs")], inc=ld[pa])
        d["lg"] = sc.dma("gpsimd", gt[pa][:],
                         gs[h_ * 128:(h_ + 1) * 128, t0:t0 + 128],
                         waits=[p.rdy("gsig")], inc=ld[pa])
        d["lq"] = d["lk"] = d["lv"] = d["lg"]
        l_r = sc.dma("scalar", rowb[pa][:], rows_d[row_i, 0:256][None, :],
                     waits=[rows_ready, prev.get(f"lr_free{pa}")], inc=lr[pa])
        bcm = sc.emit("tensor", lambda e, pa=pa: e.matmul(
            PBbc[:, bq:bq + 256], ones1f[:], rowb[pa][:], start=True, stop=True),
            waits=[l_r, zo, prev.get("bc_free")], inc=mm)
        prev[f"lr_free{pa}"] = bcm
        badd = sc.emit("vector", lambda e, pa=pa: e.tensor_tensor(
            bci[pa][:], PBbc[:, bq:bq + 256], cmask[:], ALU.add),
            waits=[bcm, mk1, mk2, prev.get(f"bci_free{pa}")], inc=ve)
        prev["bc_free"] = badd
        d["ex1"] = sc.emit("scalar", lambda e, pa=pa, colT=(b_ * 16 + c) * 16 + h_:
                           e.activation(dm[pa][:], bci[pa][:], AF.Exp,
                                        bias=posNG[:, colT:colT + 1]),
                           waits=[badd, tn1, tn2, prev.get(f"dm_free{pa}")],
                           inc=se)
        prev[f"bci_free{pa}"] = d["ex1"]
        d["gcr"] = sc.emit("scalar", lambda e, pa=pa, colT=colT, col=col:
                           e.activation(gcol[pa][:], posNG[:, colT:colT + 1],
                                        AF.Exp, scale=-1.0,
                                        bias=gB_neg[:, col:col + 1]),
                           waits=[tn1, tn2, bc2e,
                                  prev.get(f"gcol_free{pa}")], inc=se)
        kkm = sc.emit("tensor", lambda e, pa=pa, bq=bq: e.matmul(
            PBkk[:, bq:bq + 128], kt[pa][:], kt[pa][:], start=True, stop=True),
            waits=[d["lk"], prev.get(f"kk_free{i % 2}"),
                   prev.get(f"kk_free2{i % 2}")], inc=mm)
        kqm = sc.emit("tensor", lambda e, pa=pa, bq=bq: e.matmul(
            PBkk[:, bq + 128:bq + 256], kt[pa][:], qt[pa][:], start=True,
            stop=True),
            waits=[d["lq"], kkm], inc=mm)
        kke = sc.emit("scalar", lambda e, pa=pa, bq=bq: e.activation(
            kkb[pa][:], PBkk[:, bq:bq + 128], AF.Copy, scale=-1.0),
            waits=[kqm, prev.get(f"kkb_free{pa}")], inc=se)
        kqe = sc.emit("scalar", lambda e, pa=pa, bq=bq: e.activation(
            kqb[pa][:], PBkk[:, bq + 128:bq + 256], AF.Copy),
            waits=[kqm, prev.get(f"kqb_free{pa}")], inc=se)
        prev[f"kk_free{i % 2}"] = kqe
        prev[f"kk_free2{i % 2}"] = kke
        d["kke"], d["kqe"] = kke, kqe
        return d

    def phase2(i, d):
        pa = i % DEP
        qq = i % 2
        n1 = sc.emit("gpsimd", lambda e, pa=pa: e.tensor_tensor(
            ndev[pa][:], dm[pa][:, 0:128], kkb[pa][:], ALU.mult),
            waits=[d["ex1"], d["kke"], prev.get(f"ndev_free{pa}")], inc=gp)
        d["at2"] = sc.emit("gpsimd", lambda e, pa=pa: e.tensor_tensor(
            atb[pa][:], dm[pa][:, 128:256], kqb[pa][:], ALU.mult),
            waits=[d["ex1"], d["kqe"], prev.get(f"atb_free{pa}")], inc=gp)
        prev[f"dm_free{pa}"] = d["at2"]
        prev[f"kkb_free{pa}"] = n1
        prev[f"kqb_free{pa}"] = d["at2"]
        ntr = sc.emit("tensor", lambda e, pa=pa, qq=qq: e.transpose(
            PBfa[:, 256 + qq * 384:384 + qq * 384], ndev[pa][:], idm[:]),
            waits=[n1, p.consts["idm_rdy"], prev.get(f"fa3_free{qq}")], inc=mm)
        n2 = sc.emit("vector", lambda e, pa=pa, qq=qq: e.tensor_copy(
            nmath[pa][:], PBfa[:, 256 + qq * 384:384 + qq * 384]),
            waits=[ntr, prev.get(f"nmath_free{pa}")], inc=ve)
        prev[f"fa3_free{qq}"] = n2
        ip = sc.emit("gpsimd", lambda e, pa=pa: e.tensor_tensor(
            ipn[pa][:], ndev[pa][:], idm[:], ALU.add),
            waits=[n1, p.consts["idm_rdy"], prev.get(f"ipn_free{pa}")], inc=gp)
        scur, sdep = ipn[pa], ip
        cd, cm = ndev[pa], nmath[pa]
        cd_dep, cm_dep = n1, n2
        supd = None
        for k in range(1, INV_STAGES + 1):
            pd, pm = pdev[pa][k % 2], pmath[pa][k % 2]
            d1 = sc.emit("tensor", lambda e, cd=cd, cm=cm, qq=qq: e.matmul(
                PBinv[:, qq * 256:qq * 256 + 128], cm[:], cd[:], start=True,
                stop=True),
                waits=[cd_dep, cm_dep, prev.get(f"inv_free{qq}"),
                       prev.get(f"inv_free2{qq}")], inc=mm) \
                if k <= INV_STAGES - 1 else None
            d2 = sc.emit("tensor", lambda e, cd=cd, cm=cm, qq=qq: e.matmul(
                PBinv[:, qq * 256 + 128:qq * 256 + 256], cd[:], cm[:],
                start=True, stop=True),
                waits=[cd_dep, cm_dep, prev.get(f"inv_free{qq}"),
                       prev.get(f"inv_free2{qq}")], inc=mm)
            ieng, isem = (("vector", ve) if k % 2 else ("scalar", se))
            pde = sc.emit(ieng, lambda e, pd=pd, qq=qq: (
                e.tensor_copy(pd[:], PBinv[:, qq * 256:qq * 256 + 128])
                if e.__class__.__name__ != "BassScalarEngine"
                else e.activation(pd[:], PBinv[:, qq * 256:qq * 256 + 128],
                                  AF.Copy)),
                waits=[d1, d2, prev.get(f"pd{pa}{k%2}_free")], inc=isem) \
                if d1 is not None else None
            pme = sc.emit(ieng, lambda e, pm=pm, qq=qq: (
                e.tensor_copy(pm[:], PBinv[:, qq * 256 + 128:qq * 256 + 256])
                if e.__class__.__name__ != "BassScalarEngine"
                else e.activation(pm[:], PBinv[:, qq * 256 + 128:qq * 256 + 256],
                                  AF.Copy)),
                waits=[d2, prev.get(f"pm{pa}{k%2}_free")], inc=isem)
            prev[f"inv_free{qq}"] = pme
            prev[f"inv_free2{qq}"] = pde
            supd = sc.emit("tensor", lambda e, pm=pm, scur=scur, qq=qq: e.matmul(
                PBsd[:, qq * 256:qq * 256 + 128], pm[:], scur[:], start=True,
                stop=True),
                waits=[pme, sdep, prev.get(f"S_free{qq}")], inc=mm)
            prev[f"pd{pa}{k%2}_free"] = supd if pde is None else None
            prev[f"pm{pa}{k%2}_free"] = supd
            snew = sdev[pa][k % 2]
            sadd = sc.emit("vector", lambda e, snew=snew, scur=scur, qq=qq:
                           e.tensor_tensor(snew[:], PBsd[:, qq * 256:qq * 256 + 128],
                                           scur[:], ALU.add),
                           waits=[supd, sdep, prev.get(f"sdev{pa}{k%2}_free")],
                           inc=ve)
            prev[f"S_free{qq}"] = sadd
            cd, cm = pd, pm
            cd_dep, cm_dep = pde, pme
            scur, sdep = snew, sadd
        prev[f"ipn_free{pa}"] = sdep
        prev[f"ndev_free{pa}"] = sdep
        prev[f"nmath_free{pa}"] = sdep
        d["TT"], d["tt_dep"], d["supd"] = scur, sdep, supd
        return d

    def phase3(i, d):
        c, bh = divmod(i, NBH)
        pa = i % DEP
        qq = i % 2
        TT, tt_dep = d["TT"], d["tt_dep"]
        colT = ((bh // 16) * 16 + c) * 16 + (bh % 16)
        fq = qq * 384
        trk = sc.emit("tensor", lambda e, pa=pa, fq=fq: e.transpose(
            PBfa[:, fq:fq + 128], kt[pa][:], idm[:]),
            waits=[d["lk"], p.consts["idm_rdy"], prev.get(f"fa1_free{qq}")],
            inc=mm)
        trv = sc.emit("tensor", lambda e, pa=pa, fq=fq: e.transpose(
            PBfa[:, fq + 128:fq + 256], vt[pa][:], idm[:]),
            waits=[d["lv"], trk, prev.get(f"fa2_free{qq}")], inc=mm)
        kte = sc.emit("vector", lambda e, pa=pa, fq=fq: e.tensor_copy(
            ktm[pa][:], PBfa[:, fq:fq + 128]),
            waits=[trv, prev.get(f"ktm_free{pa}")], inc=ve)
        kwe = sc.emit("vector", lambda e, pa=pa, colT=colT, fq=fq: e.tensor_scalar(
            kw[pa][:], PBfa[:, fq:fq + 128], posBGC[:, colT:colT + 1], None,
            ALU.mult),
            waits=[trv, tg1, tg2, prev.get(f"kw_free{pa}")], inc=ve)
        bve = sc.emit("vector", lambda e, pa=pa, colT=colT, fq=fq: e.tensor_scalar(
            bv[pa][:], PBfa[:, fq + 128:fq + 256], posB[:, colT:colT + 1], None,
            ALU.mult),
            waits=[trv, tb1, tb2, prev.get(f"bv_free{pa}")], inc=ve)
        prev[f"fa1_free{qq}"] = kwe
        prev[f"fa2_free{qq}"] = bve
        uq = qq * 256
        wtm = sc.emit("tensor", lambda e, pa=pa, TT=TT, uq=uq: e.matmul(
            PBu[:, uq + 128:uq + 256], kw[pa][:], TT[:], start=True, stop=True),
            waits=[kwe, tt_dep, prev.get(f"U_free{qq}"),
                   prev.get(f"U_free2{qq}")], inc=mm)
        wte = sc.emit("vector", lambda e, pa=pa, uq=uq: e.tensor_copy(
            wtb[pa][:], PBu[:, uq + 128:uq + 256]),
            waits=[wtm, prev.get(f"wtb_free{pa}")], inc=ve)
        prev[f"kw_free{pa}"] = wtm
        um1 = sc.emit("tensor", lambda e, pa=pa, TT=TT, uq=uq: e.matmul(
            PBu[:, uq:uq + 128], TT[:], bv[pa][:], start=True, stop=False),
            waits=[tt_dep, bve, wte], inc=mm)
        um2 = sc.emit("tensor", lambda e, pa=pa, bh=bh, uq=uq: e.matmul(
            PBu[:, uq:uq + 128], wtb[pa][:], STb[bh], start=False, stop=True),
            waits=[um1, wte, stz[bh], prev.get(f"stb{bh}")], inc=mm)
        prev[f"sdev{pa}{INV_STAGES % 2}_free"] = um1
        prev[f"sdev{pa}{(INV_STAGES - 1) % 2}_free"] = d["supd"]
        prev[f"bv_free{pa}"] = um1
        prev[f"wtb_free{pa}"] = um2
        ue = sc.emit("scalar", lambda e, pa=pa, uq=uq: e.activation(
            ub[pa][:], PBu[:, uq:uq + 128], AF.Copy),
            waits=[um2, prev.get(f"ub_free{pa}")], inc=se)
        ude = sc.emit("vector", lambda e, pa=pa, uq=uq: e.tensor_scalar(
            udec[pa][:], PBu[:, uq:uq + 128], dm[pa][:, 127:128], None,
            ALU.mult),
            waits=[um2, ue, d["ex1"], prev.get(f"udec_free{pa}")], inc=ve)
        prev[f"U_free{qq}"] = ude
        prev[f"U_free2{qq}"] = ue
        om1 = sc.emit("tensor", lambda e, pa=pa, uq=uq: e.matmul(
            PBo[:, uq:uq + 128], atb[pa][:], ub[pa][:], start=True, stop=True),
            waits=[d["at2"], ue, prev.get(f"O_free{qq}")], inc=mm)
        om2 = sc.emit("tensor", lambda e, pa=pa, bh=bh, uq=uq: e.matmul(
            PBo[:, uq + 128:uq + 256], qt[pa][:], STb[bh], start=True,
            stop=True),
            waits=[d["lq"], om1, stz[bh], prev.get(f"stb{bh}")], inc=mm)
        pse = sc.emit("scalar", lambda e, pa=pa, uq=uq: e.activation(
            psc[pa][:], PBo[:, uq + 128:uq + 256], AF.Copy,
            scale=gcol[pa][:, 0:1]),
            waits=[om2, d["gcr"], prev.get(f"psc_free{pa}")], inc=se)
        prev[f"gcol_free{pa}"] = pse
        prev[f"atb_free{pa}"] = om1
        prev[f"ub_free{pa}"] = om1
        oad = sc.emit("vector", lambda e, pa=pa, uq=uq: e.tensor_tensor(
            osb[pa][:], PBo[:, uq:uq + 128], psc[pa][:], ALU.add),
            waits=[pse, om1, prev.get(f"osb_free{pa}")], inc=ve)
        prev[f"O_free{qq}"] = oad
        prev[f"psc_free{pa}"] = oad
        d.update(kte=kte, ude=ude, um2=um2, om2=om2, oad=oad)
        return d

    def phase4(i, d):
        c, bh = divmod(i, NBH)
        b_, h_ = bh // 16, bh % 16
        pa = i % DEP
        qq = i % 2
        sq_ = qq * 256
        t0 = b_ * S + c * 128
        col = bh * 16 + c
        dsm = sc.emit("tensor", lambda e, pa=pa, sq_=sq_: e.matmul(
            PBsd[:, sq_ + 128:sq_ + 256], ktm[pa][:], udec[pa][:], start=True,
            stop=True),
            waits=[d["kte"], d["ude"], prev.get(f"dst_free{qq}")], inc=mm)
        stu = sc.emit("vector", lambda e, pa=pa, bh=bh, col=col, sq_=sq_:
                      e.scalar_tensor_tensor(
                          STf[bh], STf[bh], gC_pos[:, col:col + 1],
                          PBsd[:, sq_ + 128:sq_ + 256], ALU.mult, ALU.add),
                      waits=[dsm, bc1e, stz[bh], d["um2"], d["om2"]], inc=ve)
        stbu = sc.emit("gpsimd", lambda e, bh=bh: e.tensor_copy(
            STb[bh], STf[bh]), waits=[stu], inc=gp)
        prev[f"stb{bh}"] = stbu
        prev[f"dst_free{qq}"] = stu
        prev[f"ktm_free{pa}"] = dsm
        prev[f"udec_free{pa}"] = dsm
        rs1 = sc.emit("vector", lambda e, pa=pa: e.reduce_sum(
            stt_[pa][:, 0:1], osb[pa][:], axis=AX.X),
            waits=[d["oad"]], inc=ve)
        ctr = sc.emit("vector", lambda e, pa=pa: e.tensor_scalar(
            oc_[pa][:], osb[pa][:], float(D), stt_[pa][:, 0:1],
            ALU.mult, ALU.subtract),
            waits=[rs1, prev.get(f"oc_free{pa}")], inc=ve)
        prev[f"osb_free{pa}"] = ctr
        sqa = sc.emit("scalar", lambda e, pa=pa: e.activation(
            sqs[pa][:], oc_[pa][:], AF.Square, accum_out=stt_[pa][:, 2:3]),
            waits=[ctr], inc=se)
        t1_ = sc.emit("vector", lambda e, pa=pa: e.tensor_scalar(
            stt_[pa][:, 3:4], stt_[pa][:, 2:3], 1.0 / D,
            float(D * D) * 1e-5, ALU.mult, ALU.add),
            waits=[sqa], inc=ve)
        ln2 = sc.emit("scalar", lambda e, pa=pa: e.activation(
            stt_[pa][:, 2:3], stt_[pa][:, 3:4], AF.Ln),
            waits=[t1_], inc=se)
        rse = sc.emit("scalar", lambda e, pa=pa: e.activation(
            stt_[pa][:, 0:1], stt_[pa][:, 2:3], AF.Exp, scale=-0.5),
            waits=[ln2], inc=se)
        nrm = sc.emit("vector", lambda e, pa=pa: e.tensor_scalar(
            olnb[pa][:], oc_[pa][:], stt_[pa][:, 0:1], None, ALU.mult),
            waits=[rse, prev.get(f"olnb_free{pa}")], inc=ve)
        prev[f"oc_free{pa}"] = nrm
        tro = sc.emit("tensor", lambda e, pa=pa, qq=qq: e.transpose(
            PBfb[:, qq * 128:qq * 128 + 128], olnb[pa][:], idm[:]),
            waits=[nrm, prev.get(f"fb_free{qq}")], inc=mm)
        prev[f"olnb_free{pa}"] = tro
        og1 = sc.emit("vector", lambda e, pa=pa, qq=qq: e.tensor_scalar(
            ogb[pa][:], PBfb[:, qq * 128:qq * 128 + 128], lnw_col[:, 0:1],
            lnb_col[:, 0:1], ALU.mult, ALU.add),
            waits=[tro, lnc_all, prev.get(f"og_free{pa}")], inc=ve)
        prev[f"fb_free{qq}"] = og1
        oge = sc.emit("vector", lambda e, pa=pa: e.tensor_tensor(
            ogb[pa][:], ogb[pa][:], gt[pa][:], ALU.mult),
            waits=[og1, d["lg"]], inc=ve)
        prev[f"ld_free{pa}"] = oge
        wr = sc.dma("sync", ogd[h_ * 128:(h_ + 1) * 128, t0:t0 + 128],
                    ogb[pa][:], waits=[oge], inc=st[pa])
        prev[f"og_free{pa}"] = wr

    NI = (B_LIMIT_CH if B_LIMIT_CH is not None else NCH) * \
         (B_LIMIT_BH if B_LIMIT_BH is not None else NBH)
    if B_LIMIT_BH is not None and B_LIMIT_BH != NBH:
        NI = (B_LIMIT_CH if B_LIMIT_CH is not None else NCH) * B_LIMIT_BH
    insts = []
    for c in range(B_LIMIT_CH if B_LIMIT_CH is not None else NCH):
        for bh in range(B_LIMIT_BH if B_LIMIT_BH is not None else NBH):
            insts.append(c * NBH + bh)
    dstate = {}
    for t in range(len(insts) + 3):
        if t < len(insts):
            dstate[insts[t]] = phase1(insts[t])
        if 1 <= t < len(insts) + 1:
            i = insts[t - 1]
            dstate[i] = phase2(i, dstate[i])
        if 2 <= t < len(insts) + 2:
            i = insts[t - 2]
            dstate[i] = phase3(i, dstate[i])
        if 3 <= t:
            i = insts[t - 3]
            phase4(i, dstate[i])
            del dstate[i]
    p.ready["og"] = [(s_, s_.n) for s_ in st]
    for s_ in st:
        p.finish(s_)



def stage_C(p: Prog):
    sc = p.sched
    ogd, wod = p.hbm["og"], p.hbm["L0_attn_Wo"]
    xd, gfd, x1d = p.hbm["x"], p.hbm["gf"], p.hbm["x1"]
    lwo = sc.sem("C_lw")
    lo = [sc.sem("C_lo0"), sc.sem("C_lo1")]
    lx = [sc.sem("C_lx0"), sc.sem("C_lx1")]
    lg = [sc.sem("C_lg0"), sc.sem("C_lg1")]
    mm = sc.sem("C_mm"); ve = sc.sem("C_v"); se = sc.sem("C_s")
    st = [sc.sem("C_st0"), sc.sem("C_st1")]
    wo_sb = p.sbuf("C_wo", (128, NKT * H), BF16)
    ogc = [p.sbuf(f"C_og{i}", (128, NKT * 128), BF16) for i in range(2)]
    xb = [p.sbuf(f"C_x{i}", (128, H), F32) for i in range(2)]
    gfb = [p.sbuf(f"C_gf{i}", (128, H), BF16) for i in range(2)]
    x1b = [p.sbuf(f"C_x1{i}", (128, H), F32) for i in range(2)]
    psf = [[p.psum(f"C_ps0_{j}", (128, 512), F32) for j in range(4)]]
    psf.append(psf[0])
    wo_rdy = None
    for kt in range(NKT):
        wo_rdy = sc.dma("gpsimd", wo_sb[:, kt * H:(kt + 1) * H],
                        wod[kt * 128:(kt + 1) * 128, :], inc=lwo)
    ogfree = [None, None]
    xfree = [None, None]
    gffree = [None, None]
    x1free = [None, None]
    psfree = [[None] * 4, [None] * 4]
    for tt in range(NTT):
        pa = tt % 2
        lc = sc.dma("sync", ogc[pa][:],
                    ogd.rearrange("(kt p) t -> p kt t", p=128)[:, :, tt * 128:(tt + 1) * 128],
                    waits=[p.rdy("og"), ogfree[pa]], inc=lo[pa])
        lxx = sc.dma("sync", xb[pa][:], xd[tt * 128:(tt + 1) * 128, :],
                     waits=[xfree[pa]], inc=lx[pa])
        lgg = sc.dma("sync", gfb[pa][:], gfd[tt * 128:(tt + 1) * 128, :],
                     waits=[p.rdy("gf"), gffree[pa]], inc=lg[pa])
        gmm = None
        for fg in range(4):
            def mmc(e, pa=pa, fg=fg):
                for kt in range(NKT):
                    ins = e.matmul(psf[pa][fg][:],
                                   ogc[pa][:, kt * 128:(kt + 1) * 128],
                                   wo_sb[:, kt * H + fg * 512: kt * H + fg * 512 + 512],
                                   start=(kt == 0), stop=(kt == NKT - 1))
                return ins
            gmm = sc.emit("tensor", mmc,
                          waits=[lc, wo_rdy, psfree[pa][fg],
                                 psfree[1 - pa][fg]], inc=mm)
            psfree[pa][fg] = gmm
        ogfree[pa] = gmm
        ev = None
        for fg in range(4):
            t1 = sc.emit("vector", lambda e, pa=pa, fg=fg: e.tensor_tensor(
                x1b[pa][:, fg * 512:(fg + 1) * 512], psf[pa][fg][:],
                gfb[pa][:, fg * 512:(fg + 1) * 512], ALU.mult),
                waits=[psfree[pa][fg], lgg, x1free[pa]], inc=ve)
            ev = sc.emit("vector", lambda e, pa=pa, fg=fg: e.tensor_tensor(
                x1b[pa][:, fg * 512:(fg + 1) * 512],
                x1b[pa][:, fg * 512:(fg + 1) * 512],
                xb[pa][:, fg * 512:(fg + 1) * 512], ALU.add),
                waits=[t1, lxx], inc=ve)
            psfree[pa][fg] = ev
        xfree[pa] = ev
        gffree[pa] = ev
        wr = sc.dma("sync", x1d[tt * 128:(tt + 1) * 128, :], x1b[pa][:],
                    waits=[ev], inc=st[pa])
        x1free[pa] = wr
    p.ready["x1"] = [(st[0], st[0].n), (st[1], st[1].n)]
    p.finish(st[0])
    p.finish(st[1])


def stage_E(p: Prog):
    stage_ln_generic(p, "x1", "xn1", "E")
    sc = p.sched
    xn_sb, xn_rdy = load_xn_res(p, "xn1")
    lw = [sc.sem("E_lw0"), sc.sem("E_lw1")]
    mm = sc.sem("E_mm"); ve = sc.sem("E_v"); se = sc.sem("E_s")
    gp = sc.sem("E_gp")
    st = [sc.sem("E_st0"), sc.sem("E_st1")]
    wbuf = [p.sbuf(f"E_w{i}", (128, NKT * 128), BF16) for i in range(2)]
    ot = [p.sbuf(f"E_o{i}", (128, 512), BF16) for i in range(2)]
    rt1 = p.sbuf("E_rt1", (64, 512), F32)
    rt2 = p.sbuf("E_rt2", (64, 512), F32)
    rt3 = p.sbuf("E_rt3", (64, 512), F32)
    ps = [p.psum(f"E_ps{i}", (128, 512), F32) for i in range(2)]
    pfr = p.psum("E_pfr", (64, 512), F32)
    cosb = p.sbuf("E_cos", (64, T), BF16)
    sinb = p.sbuf("E_sin", (64, T), BF16)
    invf = p.sbuf("E_invf", (1, 64), F32)
    tvec = p.sbuf("E_tvec", (1, 512), F32)
    frw = p.sbuf("E_frw", (64, 512), F32)
    fri = p.sbuf("E_fri", (64, 512), I32)

    z1 = sc.emit("gpsimd", lambda e: e.iota(invf[:], [[1, 64]], base=0,
                                            channel_multiplier=0,
                                            allow_small_or_imprecise_dtypes=True),
                 inc=gp)
    iv = sc.emit("scalar", lambda e: e.activation(
        invf[:], invf[:], AF.Exp, scale=-float(np.log(10000.0) / 64.0)),
        waits=[z1], inc=se)
    PI = float(np.pi)
    tabdep = []
    for tg in range(8):
        pos0 = (tg % 4) * 512
        zt = sc.emit("gpsimd", lambda e, pos0=pos0: e.iota(
            tvec[:], [[1, 512]], base=pos0, channel_multiplier=0,
            allow_small_or_imprecise_dtypes=True),
            waits=[tabdep[-1] if tabdep else None], inc=gp)
        fr = sc.emit("tensor", lambda e: e.matmul(pfr[:], invf[:], tvec[:],
                                                  start=True, stop=True),
                     waits=[zt, iv, tabdep[-1] if tabdep else None], inc=mm)
        m1 = sc.emit("vector", lambda e: e.tensor_scalar_mul(
            frw[:], pfr[:], 1.0 / (2 * PI)), waits=[fr], inc=ve)
        m2 = sc.emit("vector", lambda e: e.tensor_copy(fri[:], frw[:]),
                     waits=[m1], inc=ve)
        m3 = sc.emit("vector", lambda e: e.tensor_copy(rt1[:], fri[:]),
                     waits=[m2], inc=ve)
        m4 = sc.emit("vector", lambda e: e.tensor_tensor(
            frw[:], frw[:], rt1[:], ALU.subtract), waits=[m3], inc=ve)
        m5 = sc.emit("vector", lambda e: e.tensor_single_scalar(
            rt1[:], frw[:], 0.5, ALU.is_gt), waits=[m4], inc=ve)
        m6 = sc.emit("vector", lambda e: e.scalar_tensor_tensor(
            frw[:], rt1[:], -1.0, frw[:], ALU.mult, ALU.add),
            waits=[m5], inc=ve)
        s1 = sc.emit("scalar", lambda e, tg=tg: e.activation(
            sinb[:, tg * 512:(tg + 1) * 512], frw[:], AF.Sin,
            scale=2 * PI), waits=[m6], inc=se)
        c1 = sc.emit("vector", lambda e: e.tensor_scalar_add(frw[:], frw[:],
                                                             0.25),
                     waits=[s1], inc=ve)
        c2 = sc.emit("vector", lambda e: e.tensor_single_scalar(
            rt1[:], frw[:], 0.5, ALU.is_gt), waits=[c1], inc=ve)
        c3 = sc.emit("vector", lambda e: e.scalar_tensor_tensor(
            frw[:], rt1[:], -1.0, frw[:], ALU.mult, ALU.add),
            waits=[c2], inc=ve)
        c4 = sc.emit("scalar", lambda e, tg=tg: e.activation(
            cosb[:, tg * 512:(tg + 1) * 512], frw[:], AF.Sin,
            scale=2 * PI), waits=[c3], inc=se)
        tabdep.append(c4)
    tab_rdy = tabdep[-1]

    projs = [("L1_attn_Wq", "q1r", 16, True),
             ("L1_attn_Wk", "k1r", 4, True),
             ("L1_attn_Wv", "v1s", 4, False)]
    wfree = [None, None]
    psfree = [None, None]
    otfree = [None, None]
    n = 0
    for wname, oname, ndb, rope in projs:
        W = p.hbm[wname]
        dst = p.hbm[oname]
        for db in range(ndb):
            wp = n % 2
            lw_t = sc.dma(
                "gpsimd",
                wbuf[wp].ap().rearrange("p (kt c) -> p kt c", kt=NKT),
                W.rearrange("(kt p) n -> p kt n", p=128)[:, :, db * 128:(db + 1) * 128],
                waits=[wfree[wp]], inc=lw[wp])
            for tg in range(8):
                pp = (n * 8 + tg) % 2
                tok0 = tg * 512

                def mmp(e, wp=wp, tok0=tok0, pp=pp):
                    for kt in range(NKT):
                        ins = e.matmul(ps[pp][:],
                                       wbuf[wp][:, kt * 128:(kt + 1) * 128],
                                       xn_sb[:, kt * T + tok0: kt * T + tok0 + 512],
                                       start=(kt == 0), stop=(kt == NKT - 1))
                    return ins
                g = sc.emit("tensor", mmp, waits=[lw_t, xn_rdy, psfree[pp]], inc=mm)
                if rope:
                    cs = lambda: cosb[0:64, tok0:tok0 + 512]
                    sn = lambda: sinb[0:64, tok0:tok0 + 512]
                    t1 = sc.emit("vector", lambda e, pp=pp, tok0=tok0: e.tensor_tensor(
                        rt1[:], ps[pp][0:64, :], cosb[0:64, tok0:tok0 + 512], ALU.mult),
                        waits=[g, tab_rdy], inc=ve)
                    t2 = sc.emit("vector", lambda e, pp=pp, tok0=tok0: e.tensor_tensor(
                        rt2[:], ps[pp][64:128, :], sinb[0:64, tok0:tok0 + 512], ALU.mult),
                        waits=[g], inc=ve)
                    t3 = sc.emit("vector", lambda e, pp=pp, tok0=tok0: e.tensor_tensor(
                        rt3[:], ps[pp][0:64, :], sinb[0:64, tok0:tok0 + 512], ALU.mult),
                        waits=[g], inc=ve)
                    t4 = sc.emit("vector", lambda e, pp=pp, tok0=tok0: e.tensor_tensor(
                        ps[pp][64:128, :], ps[pp][64:128, :],
                        cosb[0:64, tok0:tok0 + 512], ALU.mult),
                        waits=[t2], inc=ve)
                    r1 = sc.emit("vector", lambda e, pp=pp: e.tensor_tensor(
                        ot[pp][0:64, :], rt1[:], rt2[:], ALU.subtract),
                        waits=[t1, t2, otfree[pp]], inc=ve)
                    r2 = sc.emit("vector", lambda e, pp=pp: e.tensor_tensor(
                        ot[pp][64:128, :], rt3[:], ps[pp][64:128, :], ALU.add),
                        waits=[t3, t4, r1], inc=ve)
                    a = r2
                    psfree[pp] = r2
                else:
                    a = sc.emit("scalar", lambda e, pp=pp: e.activation(
                        ot[pp][:], ps[pp][:], AF.Copy),
                        waits=[g, otfree[pp]], inc=se)
                    psfree[pp] = a
                wr = sc.dma("sync", dst[db * 128:(db + 1) * 128, tok0:tok0 + 512],
                            ot[pp][:], waits=[a], inc=st[pp])
                otfree[pp] = wr
            wfree[wp] = g
            n += 1
    for nm in ("q1r", "k1r", "v1s"):
        p.ready[nm] = [(st[0], st[0].n), (st[1], st[1].n)]
    p.finish(st[0])
    p.finish(st[1])



def stage_F(p: Prog):
    sc = p.sched
    qd, kd, vd = p.hbm["q1r"], p.hbm["k1r"], p.hbm["v1s"]
    wod, s1d = p.hbm["L1_attn_Wo"], p.hbm["s1"]
    lk = sc.sem("F_lk")
    lq = [sc.sem("F_lq0"), sc.sem("F_lq1")]
    lv = sc.sem("F_lv")
    lwo = [sc.sem("F_lw0"), sc.sem("F_lw1")]
    lat = [sc.sem("F_la0"), sc.sem("F_la1")]
    mm = sc.sem("F_mm"); ve = sc.sem("F_v"); se = sc.sem("F_s"); gp = sc.sem("F_g")
    st = [sc.sem("F_st0"), sc.sem("F_st1")]
    SCALE = float(D ** -0.5)
    EB = -10.0

    k_sb = p.sbuf("F_k", (128, 4 * T), BF16)
    vtm = p.sbuf("F_vtm", (128, 4 * 16 * 128), BF16)
    vcol = p.sbuf("F_vcol", (128, 4 * 128), BF16)
    qt = [p.sbuf(f"F_q{i}", (128, 512), BF16) for i in range(2)]
    e_sb = [p.sbuf(f"F_e{i}", (128, 16 * 512), BF16) for i in range(2)]
    denr = [p.sbuf(f"F_denr{i}", (1, 512), F32) for i in range(2)]
    rb = [p.sbuf(f"F_rb{i}", (128, 512), F32) for i in range(2)]
    ao = [p.sbuf(f"F_ao{i}", (128, 512), BF16) for i in range(2)]
    at_res = p.sbuf("F_atres", (128, 16 * S), BF16)
    ones1f = p.sbuf("F_ones1f", (1, 128), F32)
    wo_sb = [p.sbuf(f"F_wo{i}", (128, NKT * 512), BF16) for i in range(2)]
    s1b = [p.sbuf(f"F_s1{i}", (128, 512), BF16) for i in range(2)]
    pss = [p.psum(f"F_pss{i}", (128, 512), F32) for i in range(2)]
    psd = p.psum("F_psd", (1, 512), F32)
    psb_t = p.psum("F_psb", (128, 512), F32)
    psb = [psb_t, psb_t]
    psa = [p.psum(f"F_psa{i}", (128, 512), F32) for i in range(2)]
    psvt = p.psum("F_psvt", (128, 128), BF16)

    ebc = p.sbuf("F_ebc", (128, 1), F32)
    zo = sc.emit("gpsimd", lambda e: e.memset(ones1f[:], 1.0), inc=gp)
    zeb = sc.emit("gpsimd", lambda e: e.memset(ebc[:], EB), inc=gp)
    k_rdy = None
    for kv in range(4):
        k_rdy = sc.dma("sync", k_sb[:, kv * T:(kv + 1) * T],
                       kd[kv * 128:(kv + 1) * 128, :],
                       waits=[p.rdy("k1r")], inc=lk)
    idm = p.consts["idm"]

    qfree = [None, None]
    efree = [None, None]
    aofree = [None, None]
    ssfree = [None, None]
    avfree = [None, None]
    prev_den = [None, None]
    prev_rb = [None, None]
    prev_bc = [None, None]
    vtmfree = None
    vcolfree = None
    bcfree = None
    atfree = None
    s1free = [None, None]
    wofree = [None, None]
    n = 0
    for b_ in range(2):
        tdone = []
        for i_ in range(16):
            lvc = sc.dma("sync", vcol[:],
                         vd[:, b_ * S + i_ * 128: b_ * S + (i_ + 1) * 128]
                         .rearrange("(kv p) t -> p kv t", p=128),
                         waits=[p.rdy("v1s"), vcolfree], inc=lv)
            for kv in range(4):
                tr = sc.emit("tensor", lambda e, kv=kv, i_=i_: e.transpose(
                    psvt[:], vcol[:, kv * 128:(kv + 1) * 128], idm[:]),
                    waits=[lvc, p.consts["idm_rdy"],
                           tdone[-1] if tdone else atfree], inc=mm)
                ev = sc.emit("vector", lambda e, kv=kv, i_=i_: e.tensor_copy(
                    vtm[:, (kv * 16 + i_) * 128:(kv * 16 + i_) * 128 + 128],
                    psvt[:]),
                    waits=[tr, vtmfree], inc=ve)
                tdone.append(ev)
            vcolfree = tdone[-1]
        vtm_rdy = tdone[-1]
        for h_ in range(NH):
            kv = h_ // 4
            for qg in range(4):
                pa = n % 2
                q0 = b_ * S + qg * 512
                lqq = sc.dma("scalar", qt[pa][:],
                             qd[h_ * 128:(h_ + 1) * 128, q0:q0 + 512],
                             waits=[p.rdy("q1r"), qfree[pa]], inc=lq[pa])
                expp = [ssfree[0], ssfree[1]]
                exps = []
                for kt2 in range(16):
                    kp2 = kt2 % 2
                    sm_ = sc.emit("tensor", lambda e, pa=pa, kp2=kp2, kv=kv, kt2=kt2, b_=b_: e.matmul(
                        pss[kp2][:], k_sb[:, kv * T + b_ * S + kt2 * 128:
                                          kv * T + b_ * S + kt2 * 128 + 128],
                        qt[pa][:], start=True, stop=True),
                        waits=[k_rdy, lqq, expp[kp2]], inc=mm)
                    ex_ = sc.emit("scalar", lambda e, pa=pa, kp2=kp2, kt2=kt2: e.activation(
                        e_sb[pa][:, kt2 * 512:(kt2 + 1) * 512], pss[kp2][:],
                        AF.Exp, scale=SCALE, bias=ebc[:, 0:1]),
                        waits=[sm_, zeb, efree[pa]], inc=se)
                    expp[kp2] = ex_
                    exps.append(ex_)
                exl = exps[-1]
                ssfree[0], ssfree[1] = expp[0], expp[1]
                den_mm = None
                for kt2 in range(16):
                    den_mm = sc.emit("tensor", lambda e, pa=pa, kt2=kt2: e.matmul(
                        psd[0:1, :],
                        p.consts["ones"][:, 0:1],
                        e_sb[pa][:, kt2 * 512:(kt2 + 1) * 512],
                        start=(kt2 == 0), stop=(kt2 == 15)),
                        waits=[exps[kt2], p.consts["idm_rdy"],
                               prev_den[pa]], inc=mm)
                rcp = sc.emit("vector", lambda e, pa=pa: e.reciprocal(
                    denr[pa][:], psd[0:1, :]),
                    waits=[den_mm, prev_rb[pa]], inc=ve)
                prev_den[0] = rcp
                prev_den[1] = rcp
                bcm = sc.emit("tensor", lambda e, pa=pa: e.matmul(
                    psb[pa][:], ones1f[:], denr[pa][:], start=True, stop=True),
                    waits=[rcp, prev_bc[0], prev_bc[1]], inc=mm)
                rbe = sc.emit("scalar", lambda e, pa=pa: e.activation(
                    rb[pa][:], psb[pa][:], AF.Copy),
                    waits=[bcm], inc=se)
                prev_bc[pa] = rbe
                avm = None
                for kt2 in range(16):
                    avm = sc.emit("tensor", lambda e, pa=pa, kv=kv, kt2=kt2: e.matmul(
                        psa[pa][:], vtm[:, (kv * 16 + kt2) * 128:(kv * 16 + kt2) * 128 + 128],
                        e_sb[pa][:, kt2 * 512:(kt2 + 1) * 512],
                        start=(kt2 == 0), stop=(kt2 == 15)),
                        waits=[vtm_rdy, exl, avfree[pa]], inc=mm)
                efree[pa] = avm
                qfree[pa] = avm
                nrm = sc.emit("vector", lambda e, pa=pa: e.tensor_tensor(
                    ao[pa][:], psa[pa][:], rb[pa][:], ALU.mult),
                    waits=[avm, rbe, aofree[pa]], inc=ve)
                avfree[pa] = nrm
                prev_rb[pa] = nrm
                cp = sc.emit("vector", lambda e, pa=pa, h_=h_, qg=qg: e.tensor_copy(
                    at_res[:, h_ * S + qg * 512: h_ * S + qg * 512 + 512]
                    if False else
                    at_res.ap().rearrange("p (kt t) -> p kt t", kt=16)[:, h_, qg * 512:qg * 512 + 512],
                    ao[pa][:]),
                    waits=[nrm, atfree], inc=ve)
                aofree[pa] = cp
                n += 1
        at_rdy = cp
        for fg in range(4):
            wp = fg % 2
            lw_t = sc.dma("gpsimd",
                          wo_sb[wp].ap().rearrange("p (kt c) -> p kt c", kt=NKT),
                          wod.rearrange("(kt p) n -> p kt n", p=128)[:, :, fg * 512:(fg + 1) * 512],
                          waits=[wofree[wp]], inc=lwo[wp])
            for tt in range(16):
                pa = tt % 2

                def mmw(e, wp=wp, tt=tt, pa=pa):
                    for kt in range(NKT):
                        ins = e.matmul(psa[pa][:],
                                       at_res.ap().rearrange("p (kt t) -> p kt t", kt=16)[:, kt, tt * 128:(tt + 1) * 128],
                                       wo_sb[wp][:, kt * 512:(kt + 1) * 512],
                                       start=(kt == 0), stop=(kt == NKT - 1))
                    return ins
                g = sc.emit("tensor", mmw, waits=[lw_t, at_rdy, avfree[pa]], inc=mm)
                ev = sc.emit("scalar", lambda e, pa=pa: e.activation(
                    s1b[pa][:], psa[pa][:], AF.Copy),
                    waits=[g, s1free[pa]], inc=se)
                avfree[pa] = ev
                wr = sc.dma("sync",
                            s1d[b_ * S + tt * 128: b_ * S + (tt + 1) * 128,
                                fg * 512:(fg + 1) * 512],
                            s1b[pa][:], waits=[ev], inc=st[pa])
                s1free[pa] = wr
            wofree[wp] = g
        atfree = g
    p.ready["s1"] = [(st[0], st[0].n), (st[1], st[1].n)]
    p.finish(st[0])
    p.finish(st[1])


def stage_G0(p: Prog):
    sc = p.sched
    jobs = [("L1_ffn_gate_w", "wgT", II, H, None),
            ("L1_ffn_up_w", "wuT", II, H, None),
            ("L1_ffn_down_w", "wdT", H, II, "L1_ffn_down_scale")]
    li = [sc.sem("G0_li0"), sc.sem("G0_li1")]
    lsc = sc.sem("G0_lsc")
    mm = sc.sem("G0_mm"); ve = sc.sem("G0_v"); se = sc.sem("G0_s")
    st = [sc.sem("G0_st0"), sc.sem("G0_st1")]
    src_i = [p.sbuf(f"G0_i{i}", (128, 512), I32) for i in range(2)]
    src_b = [p.sbuf(f"G0_b{i}", (128, 512), BF16) for i in range(2)]
    dst_b = [p.sbuf(f"G0_d{i}", (128, 512), BF16) for i in range(2)]
    dscol = p.sbuf("G0_ds", (128, 1), F32)
    pst_t = p.psum("G0_ps", (128, 1024), BF16)
    pst = [pst_t[:, 0:512], pst_t[:, 512:1024]]
    idm = p.consts["idm"]
    ifree = [None, None]
    bfree = [None, None]
    dfree = [None, None]
    pfree = [None, None]
    n = 0
    for wname, dname, R, C_, scname in jobs:
        W, Dst = p.hbm[wname], p.hbm[dname]
        for rb_ in range(R // 128):
            lsc_t = None
            if scname is not None:
                lsc_t = sc.dma("gpsimd", dscol[:],
                               p.hbm[scname][rb_ * 128:(rb_ + 1) * 128][:, None],
                               waits=[bfree[0], bfree[1]], inc=lsc)
            for cb in range(C_ // 512):
                pa = n % 2
                l = sc.dma("sync", src_i[pa][:],
                           W[rb_ * 128:(rb_ + 1) * 128, cb * 512:(cb + 1) * 512],
                           waits=[ifree[pa]], inc=li[pa])
                if scname is None:
                    cv = sc.emit("vector", lambda e, pa=pa: e.tensor_scalar_sub(
                        src_b[pa][:], src_i[pa][:], 128.0),
                        waits=[l, bfree[pa]], inc=ve)
                else:
                    cv = sc.emit("vector", lambda e, pa=pa: e.tensor_scalar(
                        src_b[pa][:], src_i[pa][:], -128.0, dscol[:, 0:1],
                        ALU.add, ALU.mult),
                        waits=[l, lsc_t, bfree[pa]], inc=ve)
                ifree[pa] = cv

                def trg(e, pa=pa):
                    for j in range(4):
                        ins = e.transpose(pst[pa][:, j * 128:(j + 1) * 128],
                                          src_b[pa][:, j * 128:(j + 1) * 128],
                                          idm[:])
                    return ins
                tr = sc.emit("tensor", trg,
                             waits=[cv, p.consts["idm_rdy"], pfree[0],
                                    pfree[1]], inc=mm)
                bfree[pa] = tr
                ev = sc.emit("scalar", lambda e, pa=pa: e.activation(
                    dst_b[pa][:], pst[pa][:, :], AF.Copy),
                    waits=[tr, dfree[pa]], inc=se)
                pfree[pa] = ev
                wr = sc.dma("sync",
                            Dst.rearrange("(q p) r -> p q r", p=128)[:, 4 * cb:4 * cb + 4,
                                                                     rb_ * 128:(rb_ + 1) * 128],
                            dst_b[pa].ap().rearrange("p (j c) -> p j c", j=4),
                            waits=[ev], inc=st[pa])
                dfree[pa] = wr
                n += 1
    for nm in ("wgT", "wuT", "wdT"):
        p.ready[nm] = [(st[0], st[0].n), (st[1], st[1].n)]
    p.finish(st[0])
    p.finish(st[1])


def stage_G1(p: Prog):
    sc = p.sched
    xn_sb, xn_rdy = load_xn_res(p, "xn1")
    wg, wu, h1d = p.hbm["wgT"], p.hbm["wuT"], p.hbm["h1"]
    lgs = [sc.sem("G1_lg0"), sc.sem("G1_lg1")]
    lus = [sc.sem("G1_lu0"), sc.sem("G1_lu1")]
    lsc = sc.sem("G1_lsc")
    mm = sc.sem("G1_mm"); ve = sc.sem("G1_v"); se = sc.sem("G1_s")
    st = [sc.sem("G1_st0"), sc.sem("G1_st1")]
    wgb = [p.sbuf(f"G1_wg{i}", (128, NKT * 128), BF16) for i in range(2)]
    wub = [p.sbuf(f"G1_wu{i}", (128, NKT * 128), BF16) for i in range(2)]
    scg = [p.sbuf(f"G1_scg{i}", (128, 2), F32) for i in range(2)]
    hg = [p.sbuf(f"G1_hg{i}", (128, 512), F32) for i in range(2)]
    hb = [p.sbuf(f"G1_hb{i}", (128, 512), BF16) for i in range(2)]
    psg = [p.psum(f"G1_psg{i}", (128, 512), F32) for i in range(2)]
    psu = [p.psum(f"G1_psu{i}", (128, 512), F32) for i in range(2)]
    wfree = [None, None]
    scfree = [None, None]
    gfree = [[None, None], [None, None]]
    hgfree = [None, None]
    hbfree = [None, None]
    for ib in range(II // 128):
        wp = ib % 2
        lg = sc.dma("gpsimd", wgb[wp].ap().rearrange("p (kt c) -> p kt c", kt=NKT),
                    wg.rearrange("(kt p) n -> p kt n", p=128)[:, :, ib * 128:(ib + 1) * 128],
                    waits=[wfree[wp], p.rdy("wgT")], inc=lgs[wp])
        lu = sc.dma("gpsimd", wub[wp].ap().rearrange("p (kt c) -> p kt c", kt=NKT),
                    wu.rearrange("(kt p) n -> p kt n", p=128)[:, :, ib * 128:(ib + 1) * 128],
                    waits=[p.rdy("wuT")], inc=lgs[wp])
        ls1 = sc.dma("gpsimd", scg[wp][:, 0:1],
                     p.hbm["L1_ffn_gate_scale"][ib * 128:(ib + 1) * 128][:, None],
                     waits=[scfree[wp]], inc=lsc)
        ls2 = sc.dma("gpsimd", scg[wp][:, 1:2],
                     p.hbm["L1_ffn_up_scale"][ib * 128:(ib + 1) * 128][:, None],
                     inc=lsc)
        for tg in range(8):
            pa = (ib * 8 + tg) % 2
            tok0 = tg * 512
            g1 = sc.emit("tensor", lambda e, wp=wp, tok0=tok0, pa=pa:
                         mk_mm2(e, psg[pa], wgb[wp], xn_sb, tok0),
                         waits=[lg, xn_rdy, gfree[pa][0]], inc=mm)
            g2 = sc.emit("tensor", lambda e, wp=wp, tok0=tok0, pa=pa:
                         mk_mm2(e, psu[pa], wub[wp], xn_sb, tok0),
                         waits=[lu, gfree[pa][1]], inc=mm)
            e1 = sc.emit("scalar", lambda e, pa=pa, wp=wp:
                         e.activation(hg[pa][:], psg[pa][:], AF.Silu,
                                      scale=scg[wp][:, 0:1]),
                         waits=[g1, ls2, hgfree[pa]], inc=se)
            gfree[pa][0] = e1
            e2 = sc.emit("vector", lambda e, pa=pa, wp=wp:
                         e.scalar_tensor_tensor(hb[pa][:], psu[pa][:],
                                                scg[wp][:, 1:2], hg[pa][:],
                                                ALU.mult, ALU.mult),
                         waits=[g2, e1, hbfree[pa]], inc=ve)
            gfree[pa][1] = e2
            hgfree[pa] = e2
            wr = sc.dma("sync",
                        h1d[tg * 4:tg * 4 + 4, ib].rearrange("a p t -> p a t"),
                        hb[pa].ap().rearrange("p (a t) -> p a t", a=4),
                        waits=[e2], inc=st[pa])
            hbfree[pa] = wr
        wfree[wp] = g2
        scfree[wp] = e1
    p.ready["h1"] = [(st[0], st[0].n), (st[1], st[1].n)]
    p.finish(st[0])
    p.finish(st[1])


def stage_G2(p: Prog):
    sc = p.sched
    h1d, wdd = p.hbm["h1"], p.hbm["wdT"]
    x1d, s1d, outd = p.hbm["x1"], p.hbm["s1"], p.hbm["out"]
    lw = sc.sem("G2_lw")
    lh = [sc.sem("G2_lh0"), sc.sem("G2_lh1")]
    lx = [sc.sem("G2_lx0"), sc.sem("G2_lx1")]
    ls = [sc.sem("G2_ls0"), sc.sem("G2_ls1")]
    mm = sc.sem("G2_mm"); ve = sc.sem("G2_v"); se = sc.sem("G2_s")
    st = [sc.sem("G2_st0"), sc.sem("G2_st1")]
    NIB = II // 128
    wd_res = p.sbuf("G2_wd", (128, NIB * 1024), BF16)
    hcol = [p.sbuf(f"G2_h{i}", (128, NIB * 128), BF16) for i in range(2)]
    gfb = [p.sbuf(f"G2_gf{i}", (128, 1024), F32) for i in range(2)]
    xb = [p.sbuf(f"G2_x{i}", (128, 1024), F32) for i in range(2)]
    s1b = [p.sbuf(f"G2_s1{i}", (128, 1024), BF16) for i in range(2)]
    ob = [p.sbuf(f"G2_o{i}", (128, 1024), F32) for i in range(2)]
    psf = [[p.psum(f"G2_ps{i}_{j}", (128, 512), F32) for j in range(2)]
           for i in range(2)]
    wd_rdy = None
    hfree = [None, None]
    psfree = [[None, None], [None, None]]
    gffree = [None, None]
    xfree = [None, None]
    s1free = [None, None]
    ofree = [None, None]
    for half in range(2):
        for ib in range(NIB):
            wd_rdy = sc.dma("gpsimd", wd_res[:, ib * 1024:(ib + 1) * 1024],
                            wdd[ib * 128:(ib + 1) * 128,
                                half * 1024:(half + 1) * 1024],
                            waits=[p.rdy("wdT"),
                                   psfree[0][1] if ib == 0 else None,
                                   psfree[1][1] if ib == 0 else None],
                            inc=lw)
        for tt in range(NTT):
            pa = tt % 2
            lc = sc.dma("sync", hcol[pa].ap().rearrange("p (ib t) -> p ib t", ib=NIB),
                        h1d[tt].rearrange("ib p t -> p ib t"),
                        waits=[p.rdy("h1"), hfree[pa]], inc=lh[pa])
            lxx = sc.dma("sync", xb[pa][:],
                         x1d[tt * 128:(tt + 1) * 128,
                             half * 1024:(half + 1) * 1024],
                         waits=[p.rdy("x1"), xfree[pa]], inc=lx[pa])
            ls1 = sc.dma("scalar", s1b[pa][:],
                         s1d[tt * 128:(tt + 1) * 128,
                             half * 1024:(half + 1) * 1024],
                         waits=[p.rdy("s1"), s1free[pa]], inc=ls[pa])

            def mmd(e, pa=pa):
                for ib in range(NIB):
                    for fgx in range(2):
                        ins = e.matmul(psf[pa][fgx][:],
                                       hcol[pa][:, ib * 128:(ib + 1) * 128],
                                       wd_res[:, ib * 1024 + fgx * 512: ib * 1024 + fgx * 512 + 512],
                                       start=(ib == 0), stop=(ib == NIB - 1))
                return ins
            g = sc.emit("tensor", mmd,
                        waits=[wd_rdy, lc, psfree[pa][0], psfree[pa][1]], inc=mm)
            hfree[pa] = g
            ev = wr = None
            for fgx in range(2):
                ev = sc.emit("scalar", lambda e, pa=pa, fgx=fgx: e.activation(
                    gfb[pa][:, fgx * 512:(fgx + 1) * 512], psf[pa][fgx][:],
                    AF.Silu), waits=[g, gffree[pa]], inc=se)
                psfree[pa][fgx] = ev
            col0 = half * 1024
            m1 = sc.emit("vector", lambda e, pa=pa: e.tensor_tensor(
                ob[pa][:], gfb[pa][:, 0:1024],
                s1b[pa][:], ALU.mult),
                waits=[ev, ls1, ofree[pa]], inc=ve)
            gffree[pa] = m1
            m2 = sc.emit("vector", lambda e, pa=pa: e.tensor_tensor(
                ob[pa][:], ob[pa][:],
                xb[pa][:], ALU.add),
                waits=[m1, lxx], inc=ve)
            xfree[pa] = m2
            s1free[pa] = m2
            wr = sc.dma("sync",
                        outd[tt * 128:(tt + 1) * 128, half * 1024:(half + 1) * 1024],
                        ob[pa][:], waits=[m2], inc=st[pa])
            ofree[pa] = wr
    p.finish(st[0])
    p.finish(st[1])


STAGE_FN = {
    "A": stage_A,
    "A1": stage_A1,
    "A2a": stage_A2a,
    "A2b": stage_A2b,
    "B": stage_B,
    "C": stage_C,
    "E": stage_E,
    "F": stage_F,
    "G0": stage_G0,
    "G1": stage_G1,
    "G2": stage_G2,
}

N_CORES = 8


def _ffn0_perm():
    return np.arange(II).reshape(II // 4, 4).T.reshape(-1)


def _rope_perm():
    idx = []
    for h in range(NH):
        base = h * D
        idx.extend(base + np.arange(0, D, 2))
        idx.extend(base + np.arange(1, D, 2))
    return np.asarray(idx)


STAGE_PERSIST = {"G0"}


def build_program(stages):
    p = Prog(stages)
    with p.nc.Block() as block:
        build_consts(p)
        p.sched.flush(block)
        for st in stages:
            if st in STAGE_PERSIST:
                saved_sb, saved_ps = p.sb, p.psn
                p.sb, p.psn = {}, {}
                ng = p.next_guards
                p.next_guards = []
                STAGE_FN[st](p)
                p.sched.flush(block)
                p.sb, p.psn = saved_sb, saved_ps
                p.next_guards = ng
            else:
                p.begin_stage()
                STAGE_FN[st](p)
                p.sched.flush(block)
                p.end_stage()
    return p


def kernel(**inputs):
    xfull = np.ascontiguousarray(np.asarray(inputs["L0_attn_x"], dtype=np.float32))
    perm = _rope_perm()
    kperm = np.concatenate([g * D + np.arange(0, D, 2) for g in range(4)]
                           + [g * D + np.arange(1, D, 2) for g in range(4)])
    weights = {}
    for name in WEIGHTS:
        arr = np.asarray(inputs[name])
        weights[name] = np.ascontiguousarray(arr)
    wq = weights["L1_attn_Wq"]
    weights["L1_attn_Wq"] = np.ascontiguousarray(wq[:, perm])
    wk = weights["L1_attn_Wk"]
    kp = []
    for g in range(4):
        kp.extend(g * D + np.arange(0, D, 2))
        kp.extend(g * D + np.arange(1, D, 2))
    weights["L1_attn_Wk"] = np.ascontiguousarray(wk[:, np.asarray(kp)])
    fperm = _ffn0_perm()
    weights["L0_ffn_w_gate"] = np.ascontiguousarray(weights["L0_ffn_w_gate"][:, fperm])
    weights["L0_ffn_w_up"] = np.ascontiguousarray(weights["L0_ffn_w_up"][:, fperm])
    weights["L0_ffn_w_down"] = np.ascontiguousarray(weights["L0_ffn_w_down"][fperm, :])

    pool = [dict() for _ in range(N_CORES)]
    for c in range(N_CORES):
        pool[c]["x"] = np.ascontiguousarray(
            xfull[2 * c:2 * c + 2].reshape(T, H))
        for name, arr in weights.items():
            pool[c][name] = arr

    groups = [g for g in GROUPS if any(st in STAGE_FN for st in g)]
    outs = None
    for stages in groups:
        stages = [st for st in stages if st in STAGE_FN]
        if not stages:
            continue
        p = build_program(stages)
        in_maps = [{n: pool[c][n] for n in p.input_names} for c in range(N_CORES)]
        res = run_bass_kernel_spmd(p.nc, in_maps, core_ids=list(range(N_CORES)))
        for c in range(N_CORES):
            for n in p.output_names:
                pool[c][n] = np.asarray(res.results[c][n])
        outs = res
    out = np.stack([pool[c]["out"].reshape(2, S, H) for c in range(N_CORES)])
    return out.reshape(16, S, H).astype(np.float32)


# version 08
# speedup vs baseline: 1.0219x; 1.0084x over previous
import contextlib
import numpy as np
import ml_dtypes
import concourse.bass as bass
import concourse.mybir as mybir
from concourse.bass_utils import run_bass_kernel_spmd

F32 = mybir.dt.float32
BF16 = mybir.dt.bfloat16
I32 = mybir.dt.int32
AF = mybir.ActivationFunctionType
ALU = mybir.AluOpType
AX = mybir.AxisListType

S = 2048
T = 4096
H = 2048
NH = 16
D = 128
II = 8192
CH = 128
NCH = S // CH
NKT = H // 128
NTT = T // 128
NBH = 2 * NH
LNC = float(np.log(D ** -0.5))

ENGINES = ("sync", "gpsimd", "vector", "scalar", "tensor")


class Sem:
    __slots__ = ("h", "n", "name")

    def __init__(self, h, name):
        self.h = h
        self.n = 0
        self.name = name

    @property
    def now(self):
        return (self, self.n)


class Sched:

    def __init__(self, nc):
        self.nc = nc
        self.ctx = contextlib.ExitStack()
        self.ops = {e: [] for e in ENGINES}
        self.sems = {}
        self.seen = {e: {} for e in ENGINES}
        self.guards = []

    def sem(self, name):
        pool = name.split("_", 1)[1] if "_" in name else name
        if pool not in self.sems:
            h = self.ctx.enter_context(self.nc.semaphore(pool))
            self.sems[pool] = Sem(h, pool)
        return self.sems[pool]

    def emit(self, eng, fn, waits=(), inc=None, by=1):
        tick = None
        if inc is not None:
            inc.n += by
            tick = (inc, by)
        flat = []
        for w in waits:
            if w is None:
                continue
            if isinstance(w, list):
                flat.extend(x for x in w if x is not None)
            else:
                flat.append(w)
        flat = list(self.guards) + flat
        self.ops[eng].append((fn, flat, tick))
        return (inc, inc.n) if inc else None

    def dma(self, eng, out, in_, waits=(), inc=None, **kw):
        assert inc is not None
        return self.emit(eng, lambda e: e.dma_start(out=out, in_=in_, **kw),
                         waits=waits, inc=inc, by=16)

    def flush(self, block):
        for eng in ENGINES:
            if not self.ops[eng]:
                continue

            def body(e, ops_=self.ops[eng], seen=self.seen[eng]):
                for fn, waits, tick in ops_:
                    for sem, cnt in waits:
                        if seen.get(sem.name, -1) >= cnt:
                            continue
                        e.wait_ge(sem.h, cnt)
                        seen[sem.name] = cnt
                    ins = fn(e)
                    if tick is not None:
                        sem, by = tick
                        ins.then_inc(sem.h, by)
            getattr(block, eng)(body)
            self.ops[eng] = []


TENSORS = {
    "x":     ((T, H), F32, None),
    "xn":    ((H, T), BF16, "A"),
    "qs":    ((H, T), BF16, "A1"),
    "ks":    ((H, T), BF16, "A1"),
    "vs":    ((H, T), BF16, "A1"),
    "gsig":  ((H, T), BF16, "A1"),
    "alpha": ((NH, T), F32, "A1"),
    "beta":  ((NH, T), F32, "A1"),
    "hs":    ((NTT, II // 128, 128, 128), BF16, "A2a"),
    "wd_bf": ((II, H), BF16, "A2a"),
    "gf":    ((T, H), BF16, "A2b"),
    "og":    ((H, T), BF16, "B"),
    "x1":    ((T, H), F32, "C"),
    "xn1":   ((H, T), BF16, "E"),
    "q1r":   ((H, T), BF16, "E"),
    "k1r":   ((512, T), BF16, "E"),
    "v1s":   ((512, T), BF16, "E"),
    "s1":    ((T, H), BF16, "F"),
    "wgT":   ((H, II), BF16, "G0"),
    "wuT":   ((H, II), BF16, "G0"),
    "wdT":   ((II, H), BF16, "G0"),
    "h1":    ((NTT, II // 128, 128, 128), BF16, "G1"),
    "out":   ((T, H), F32, "G2"),
}
WEIGHTS = {
    "L0_attn_Wq": ((H, H), F32), "L0_attn_Wk": ((H, H), F32),
    "L0_attn_Wv": ((H, H), F32), "L0_attn_Wa": ((H, NH), F32),
    "L0_attn_ba": ((NH,), F32), "L0_attn_Wb": ((H, NH), F32),
    "L0_attn_bb": ((NH,), F32), "L0_attn_Wg": ((H, H), F32),
    "L0_attn_Wo": ((H, H), F32),
    "L0_attn_qcw": ((H, 4), F32), "L0_attn_qcb": ((H,), F32),
    "L0_attn_kcw": ((H, 4), F32), "L0_attn_kcb": ((H,), F32),
    "L0_attn_vcw": ((H, 4), F32), "L0_attn_vcb": ((H,), F32),
    "L0_attn_ln_w": ((D,), F32), "L0_attn_ln_b": ((D,), F32),
    "L0_ffn_w_gate": ((H, II), F32), "L0_ffn_w_up": ((H, II), F32),
    "L0_ffn_w_down": ((II, H), F32),
    "L1_attn_Wq": ((H, H), F32), "L1_attn_Wk": ((H, 512), F32),
    "L1_attn_Wv": ((H, 512), F32), "L1_attn_Wo": ((H, H), F32),
    "L1_ffn_gate_w": ((II, H), I32), "L1_ffn_gate_scale": ((II,), F32),
    "L1_ffn_up_w": ((II, H), I32), "L1_ffn_up_scale": ((II,), F32),
    "L1_ffn_down_w": ((H, II), I32), "L1_ffn_down_scale": ((H,), F32),
}
STAGE_DEPS = {
    "A":   ["x"],
    "A1":  ["xn", "L0_attn_Wq", "L0_attn_Wk", "L0_attn_Wv", "L0_attn_Wg",
            "L0_attn_Wa", "L0_attn_ba", "L0_attn_Wb", "L0_attn_bb",
            "L0_attn_qcw", "L0_attn_qcb", "L0_attn_kcw", "L0_attn_kcb",
            "L0_attn_vcw", "L0_attn_vcb"],
    "A2a": ["xn", "L0_ffn_w_gate", "L0_ffn_w_up", "L0_ffn_w_down"],
    "A2b": ["hs", "wd_bf"],
    "B":   ["qs", "ks", "vs", "gsig", "alpha", "beta",
            "L0_attn_ln_w", "L0_attn_ln_b"],
    "C":   ["og", "x", "gf", "L0_attn_Wo"],
    "E":   ["x1", "L1_attn_Wq", "L1_attn_Wk", "L1_attn_Wv"],
    "F":   ["q1r", "k1r", "v1s", "L1_attn_Wo"],
    "G0":  ["L1_ffn_gate_w", "L1_ffn_up_w", "L1_ffn_down_w",
            "L1_ffn_down_scale"],
    "G1":  ["xn1", "wgT", "wuT", "L1_ffn_gate_scale", "L1_ffn_up_scale"],
    "G2":  ["h1", "wdT", "x1", "s1"],
}
STAGE_OUTS = {
    "A": ["xn"], "A1": ["qs", "ks", "vs", "gsig", "alpha", "beta"],
    "A2a": ["hs", "wd_bf"], "A2b": ["gf"], "B": ["og"], "C": ["x1"],
    "E": ["xn1", "q1r", "k1r", "v1s"], "F": ["s1"],
    "G0": ["wgT", "wuT", "wdT"], "G1": ["h1"], "G2": ["out"],
}
ALL_STAGES = ["A", "A1", "A2a", "A2b", "B", "G0", "C", "E", "F", "G1", "G2"]

GROUPS = [list(ALL_STAGES)]


class Prog:

    def __init__(self, stages):
        self.stages = stages
        self.nc = bass.Bass("TRN2", target_bir_lowering=False, debug=False)
        self.sched = Sched(self.nc)
        need = set()
        for st in stages:
            need.update(STAGE_DEPS[st])
        prod_here = {t for st in stages for t in STAGE_OUTS[st]}
        later = [st for st in ALL_STAGES if st not in stages]
        self.hbm = {}
        self.input_names = []
        self.output_names = []
        for name in sorted(need | prod_here):
            if name in WEIGHTS:
                shape, dt = WEIGHTS[name]
                self.hbm[name] = self.nc.dram_tensor(
                    name, list(shape), dt, kind="ExternalInput").ap()
                self.input_names.append(name)
                continue
            shape, dt, prod = TENSORS[name]
            if name in prod_here:
                used_later = (name == "out") or any(
                    name in STAGE_DEPS[st2] for st2 in later)
                kind = "ExternalOutput" if used_later else "Internal"
                if kind == "ExternalOutput":
                    self.output_names.append(name)
            else:
                kind = "ExternalInput"
                self.input_names.append(name)
            self.hbm[name] = self.nc.dram_tensor(
                name, list(shape), dt, kind=kind).ap()
        self.sb = {}
        self.psn = {}
        self.stage_stack = None
        self.next_guards = []
        self.ready = {}

    def sbuf(self, name, shape, dt):
        if name not in self.sb:
            if self.stage_stack is not None:
                self.sb[name] = self.stage_stack.enter_context(
                    self.nc.sbuf_tensor(name, list(shape), dt,
                                        allow_name_mangling=True))
            else:
                self.sb[name] = self.nc.alloc_sbuf_tensor(
                    name, list(shape), dt, allow_name_mangling=True)
        return self.sb[name]

    def psum(self, name, shape, dt=F32):
        if name not in self.psn:
            if self.stage_stack is not None:
                self.psn[name] = self.stage_stack.enter_context(
                    self.nc.psum_tensor(name, list(shape), dt,
                                        allow_name_mangling=True))
            else:
                self.psn[name] = self.nc.alloc_psum_tensor(
                    name, list(shape), dt, allow_name_mangling=True)
        return self.psn[name]

    def begin_stage(self):
        self.stage_stack = contextlib.ExitStack()
        self.sb = {}
        self.psn = {}
        self.next_guards = []

    def end_stage(self):
        self.stage_stack.close()
        self.stage_stack = None
        self.sched.guards = list(self.next_guards)

    def rdy(self, name):
        return self.ready.get(name)

    def finish(self, st_sem):
        n = st_sem.n
        self.sched.emit("sync", lambda e, n=n: e.wait_ge(st_sem.h, n),
                        waits=())
        self.next_guards.append((st_sem, n))


def build_consts(p: Prog):
    sc = p.sched
    ones = p.sbuf("c_ones", (128, 128), BF16)
    idm = p.sbuf("c_idm", (128, 128), BF16)
    g_done = sc.sem("c_gp")
    z = sc.emit("gpsimd", lambda e: e.memset(ones[:], 1.0), inc=g_done)
    rdy = sc.emit("gpsimd",
                  lambda e: e.affine_select(idm[:], ones[:], [[1, 128]],
                                            ALU.is_equal, 0.0, base=0,
                                            channel_multiplier=-1),
                  waits=[z], inc=g_done)
    p.consts = {"ones": ones, "idm": idm, "idm_rdy": rdy}


def stage_A(p: Prog):
    stage_ln_generic(p, "x", "xn", "A")


def stage_ln_generic(p: Prog, srcname, dstname, tag):
    sc = p.sched
    x, xn = p.hbm[srcname], p.hbm[dstname]
    xbuf = [p.sbuf(f"{tag}_x{i}", (128, H), F32) for i in range(2)]
    st_t = p.sbuf(tag + "_stats", (128, 8), F32)
    xntm = [p.sbuf(f"{tag}_xntm{i}", (128, H), BF16) for i in range(2)]
    xnfm = [p.sbuf(f"{tag}_xnfm{i}", (128, H), BF16) for i in range(2)]
    pst = [p.psum(f"{tag}_pst{i}", (128, H), BF16) for i in range(2)]
    v, s_, tr = (sc.sem(n) for n in (tag + "_v", tag + "_s", tag + "_tr"))
    lx = [sc.sem(tag + "_lx0"), sc.sem(tag + "_lx1")]
    st = [sc.sem(tag + "_st0"), sc.sem(tag + "_st1")]
    idm = p.consts["idm"]
    idm_rdy = p.consts["idm_rdy"]

    free = [None, None]
    free_tm = [None, None]
    free_fm = [None, None]
    free_ps = [None, None]
    for tt in range(NTT):
        pa = tt % 2
        c0 = 4 * pa
        ld = sc.dma("sync", xbuf[pa][:], x[tt * 128:(tt + 1) * 128, :],
                    waits=[free[pa], p.rdy(srcname)], inc=lx[pa])
        r1 = sc.emit("vector", lambda e, pa=pa, c0=c0:
                     e.reduce_sum(st_t[:, c0:c0 + 1], xbuf[pa][:], axis=AX.X),
                     waits=[ld], inc=v)
        r2 = sc.emit("vector", lambda e, c0=c0:
                     e.tensor_scalar(st_t[:, c0 + 1:c0 + 2], st_t[:, c0:c0 + 1],
                                     1.0 / H, None, ALU.mult),
                     waits=[r1], inc=v)
        r3 = sc.emit("vector", lambda e, pa=pa, c0=c0:
                     e.tensor_scalar(xbuf[pa][:], xbuf[pa][:],
                                     st_t[:, c0 + 1:c0 + 2], None, ALU.subtract),
                     waits=[r2], inc=v)
        r4 = sc.emit("scalar", lambda e, pa=pa, c0=c0:
                     e.activation(xntm[pa][:], xbuf[pa][:], AF.Square,
                                  accum_out=st_t[:, c0 + 2:c0 + 3]),
                     waits=[r3, free_tm[pa]], inc=s_)
        r5 = sc.emit("vector", lambda e, c0=c0:
                     e.tensor_scalar(st_t[:, c0 + 3:c0 + 4], st_t[:, c0 + 2:c0 + 3],
                                     1.0 / H, 1e-5, ALU.mult, ALU.add),
                     waits=[r4], inc=v)
        r6 = sc.emit("scalar", lambda e, c0=c0:
                     e.activation(st_t[:, c0 + 2:c0 + 3], st_t[:, c0 + 3:c0 + 4],
                                  AF.Sqrt),
                     waits=[r5], inc=s_)
        r7 = sc.emit("vector", lambda e, c0=c0:
                     e.reciprocal(st_t[:, c0:c0 + 1], st_t[:, c0 + 2:c0 + 3]),
                     waits=[r6], inc=v)
        r8 = sc.emit("vector", lambda e, pa=pa, c0=c0:
                     e.tensor_scalar(xntm[pa][:], xbuf[pa][:], st_t[:, c0:c0 + 1],
                                     None, ALU.mult),
                     waits=[r7, r4], inc=v)
        free[pa] = r8

        def trans(e, pa=pa):
            for fb in range(16):
                ins = e.transpose(pst[pa][:, fb * 128:(fb + 1) * 128],
                                  xntm[pa][:, fb * 128:(fb + 1) * 128], idm[:])
            return ins
        r9 = sc.emit("tensor", trans, waits=[r8, idm_rdy, free_ps[pa]], inc=tr)
        free_tm[pa] = r9
        r10 = sc.emit("scalar", lambda e, pa=pa:
                      e.activation(xnfm[pa][:], pst[pa][:], AF.Copy),
                      waits=[r9, free_fm[pa]], inc=s_)
        free_ps[pa] = r10
        wr = sc.dma("sync",
                    xn.rearrange("(fb p) t -> p fb t", p=128)[:, :, tt * 128:(tt + 1) * 128],
                    xnfm[pa].ap().rearrange("p (fb c) -> p fb c", fb=16),
                    waits=[r10], inc=st[pa])
        free_fm[pa] = wr
    p.ready[dstname] = [(st[0], st[0].n), (st[1], st[1].n)]
    p.finish(st[0])
    p.finish(st[1])


def load_xn_res(p: Prog, name="xn"):
    sc = p.sched
    xn = p.hbm[name]
    xn_sb = p.sbuf(f"{name}_res", (128, NKT * T), BF16)
    lsem = sc.sem(f"{name}_res_l")
    for kt in range(NKT):
        sc.dma("sync", xn_sb[:, kt * T:(kt + 1) * T],
               xn[kt * 128:(kt + 1) * 128, :],
               waits=[p.rdy(name)], inc=lsem)
    return xn_sb, (lsem, lsem.n)


def stage_A1(p: Prog):
    sc = p.sched
    xn_sb, xn_rdy = load_xn_res(p)
    lw = [sc.sem("A1_lw0"), sc.sem("A1_lw1")]
    lcw = [sc.sem("A1_lcw0"), sc.sem("A1_lcw1")]
    lab = sc.sem("A1_lab")
    gp = sc.sem("A1_gp")
    mm = sc.sem("A1_mm")
    ve = sc.sem("A1_v")
    se = sc.sem("A1_s")
    st = [sc.sem("A1_st0"), sc.sem("A1_st1")]
    stab = sc.sem("A1_stab")
    wbuf = [p.sbuf(f"A1_w{i}", (128, NKT * 128), BF16) for i in range(2)]
    cwb = [p.sbuf(f"A1_cw{i}", (128, 5), F32) for i in range(2)]
    lin = p.sbuf("A1_lin", (128, 3 + S), F32)
    cv = [p.sbuf(f"A1_cv{i}", (128, 512), F32) for i in range(2)]
    ot = [p.sbuf(f"A1_o{i}", (128, 512), BF16) for i in range(2)]
    ps = [p.psum(f"A1_ps{i}", (128, 512), F32) for i in range(2)]

    projs = [("q", "L0_attn_Wq", "L0_attn_qcw", "L0_attn_qcb", "qs", AF.Silu),
             ("k", "L0_attn_Wk", "L0_attn_kcw", "L0_attn_kcb", "ks", AF.Silu),
             ("v", "L0_attn_Wv", "L0_attn_vcw", "L0_attn_vcb", "vs", AF.Silu),
             ("g", "L0_attn_Wg", None, None, "gsig", AF.Sigmoid)]
    wfree = [None, None]
    cwfree = [None, None]
    psfree = [None, None]
    cvfree = [None, None]
    otfree = [None, None]
    linfree = None
    n = 0
    for pi, (pname, wname, cwname, cbname, oname, act) in enumerate(projs):
        W = p.hbm[wname]
        dst = p.hbm[oname]
        for db in range(16):
            wp = n % 2
            lw_t = sc.dma(
                "gpsimd",
                wbuf[wp].ap().rearrange("p (kt c) -> p kt c", kt=NKT),
                W.rearrange("(kt p) n -> p kt n", p=128)[:, :, db * 128:(db + 1) * 128],
                waits=[wfree[wp]], inc=lw[wp])
            if cwname is not None:
                c1 = sc.dma("gpsimd", cwb[wp][:, 0:4],
                            p.hbm[cwname][db * 128:(db + 1) * 128, :],
                            waits=[cwfree[wp]], inc=lcw[wp])
                c2 = sc.dma("gpsimd", cwb[wp][:, 4:5],
                            p.hbm[cbname][db * 128:(db + 1) * 128][:, None],
                            inc=lcw[wp])
                cw_rdy = c2
            else:
                cw_rdy = None
            for bseq in range(2):
                if cwname is not None:
                    z = sc.emit("gpsimd", lambda e: e.memset(lin[:, 0:3], 0.0),
                                waits=[linfree], inc=gp)
                for tg in range(4):
                    pp = (n * 8 + bseq * 4 + tg) % 2
                    tok0 = bseq * S + tg * 512

                    def mmg(e, wp=wp, tok0=tok0, pp=pp):
                        for kt in range(NKT):
                            ins = e.matmul(ps[pp][:],
                                           wbuf[wp][:, kt * 128:(kt + 1) * 128],
                                           xn_sb[:, kt * T + tok0: kt * T + tok0 + 512],
                                           start=(kt == 0), stop=(kt == NKT - 1))
                        return ins
                    g = sc.emit("tensor", mmg,
                                waits=[lw_t, xn_rdy, psfree[pp]], inc=mm)
                    if cwname is not None:
                        ev = sc.emit("scalar", lambda e, tg=tg, pp=pp:
                                     e.activation(lin[:, 3 + tg * 512: 3 + (tg + 1) * 512],
                                                  ps[pp][:], AF.Copy),
                                     waits=[g, z], inc=se)
                        psfree[pp] = ev
                        t3 = sc.emit("vector", lambda e, tg=tg, pp=pp, wp=wp:
                                     e.tensor_scalar(cv[pp][:],
                                                     lin[:, 3 + tg * 512: 3 + tg * 512 + 512],
                                                     cwb[wp][:, 3:4], cwb[wp][:, 4:5],
                                                     ALU.mult, ALU.add),
                                     waits=[ev, cw_rdy, cvfree[pp]], inc=ve)
                        acc = t3
                        for j in (2, 1, 0):
                            acc = sc.emit("vector", lambda e, tg=tg, pp=pp, wp=wp, j=j:
                                          e.scalar_tensor_tensor(
                                              cv[pp][:],
                                              lin[:, j + tg * 512: j + tg * 512 + 512],
                                              cwb[wp][:, j:j + 1], cv[pp][:],
                                              ALU.mult, ALU.add),
                                          waits=[acc], inc=ve)
                        if tg == 3:
                            linfree = acc
                        a = sc.emit("scalar", lambda e, pp=pp, act=act:
                                    e.activation(ot[pp][:], cv[pp][:], act),
                                    waits=[acc, otfree[pp]], inc=se)
                        cvfree[pp] = a
                    else:
                        a = sc.emit("scalar", lambda e, pp=pp, act=act:
                                    e.activation(ot[pp][:], ps[pp][:], act),
                                    waits=[g, otfree[pp]], inc=se)
                        psfree[pp] = a
                    wr = sc.dma("sync",
                                dst[db * 128:(db + 1) * 128, tok0:tok0 + 512],
                                ot[pp][:], waits=[a], inc=st[pp])
                    otfree[pp] = wr
            wfree[wp] = g
            if cwname is not None:
                cwfree[wp] = t3
            n += 1

    wab = p.sbuf("A1_wab", (128, NKT * 16), BF16)
    bab = p.sbuf("A1_bab", (16, 1), F32)
    absb = [p.sbuf(f"A1_ab{i}", (16, 512), F32) for i in range(2)]
    psab = [p.psum(f"A1_psab{i}", (16, 512), F32) for i in range(2)]
    evp = [None, None]
    wrp = [None, None]
    ab_wfree = None
    ab_bfree = None
    for name, wname, bname, dname in (("a", "L0_attn_Wa", "L0_attn_ba", "alpha"),
                                      ("b", "L0_attn_Wb", "L0_attn_bb", "beta")):
        lw_t = None
        for kt in range(NKT):
            lw_t = sc.dma("gpsimd", wab[:, kt * 16:(kt + 1) * 16],
                          p.hbm[wname][kt * 128:(kt + 1) * 128, :],
                          waits=[ab_wfree if kt == 0 else None], inc=lab)
        lb = sc.dma("gpsimd", bab[:], p.hbm[bname][:, None],
                    waits=[ab_bfree], inc=lab)
        for tgg in range(8):
            pp = tgg % 2

            def mmab(e, tgg=tgg, pp=pp):
                for kt in range(NKT):
                    ins = e.matmul(psab[pp][:],
                                   wab[:, kt * 16:(kt + 1) * 16],
                                   xn_sb[:, kt * T + tgg * 512: kt * T + tgg * 512 + 512],
                                   start=(kt == 0), stop=(kt == NKT - 1))
                return ins
            g = sc.emit("tensor", mmab, waits=[lw_t, xn_rdy, evp[pp]], inc=mm)
            ev = sc.emit("scalar", lambda e, pp=pp:
                         e.activation(absb[pp][:], psab[pp][:], AF.Sigmoid,
                                      bias=bab[:]),
                         waits=[g, lb, wrp[pp]], inc=se)
            evp[pp] = ev
            wr = sc.dma("sync", p.hbm[dname][:, tgg * 512:(tgg + 1) * 512],
                        absb[pp][:], waits=[ev], inc=stab)
            wrp[pp] = wr
        ab_wfree = g
        ab_bfree = ev

    for nm in STAGE_OUTS["A1"]:
        p.ready[nm] = [(st[0], st[0].n), (st[1], st[1].n), (stab, stab.n)]
    p.finish(st[0])
    p.finish(st[1])
    p.finish(stab)



def stage_A2a(p: Prog):
    sc = p.sched
    xn_sb, xn_rdy = load_xn_res(p)
    wg, wu, hsd = p.hbm["L0_ffn_w_gate"], p.hbm["L0_ffn_w_up"], p.hbm["hs"]
    lgs = sc.sem("A2a_lg"); lus = sc.sem("A2a_lu")
    lwd = [sc.sem("A2a_lwd0"), sc.sem("A2a_lwd1")]
    mm = sc.sem("A2a_mm"); ve = sc.sem("A2a_v")
    se = sc.sem("A2a_s"); gp = sc.sem("A2a_gp")
    st = [sc.sem("A2a_st0"), sc.sem("A2a_st1")]
    swd = [sc.sem("A2a_swd0"), sc.sem("A2a_swd1")]
    NGB = 16
    wgb = p.sbuf("A2a_wg", (128, 4 * NKT * 128), BF16)
    wub = p.sbuf("A2a_wu", (128, 4 * NKT * 128), BF16)
    hg = [p.sbuf(f"A2a_hg{i}", (128, 512), F32) for i in range(2)]
    hh = [p.sbuf(f"A2a_hh{i}", (128, 4 * 512), BF16) for i in range(2)]
    sq = [p.sbuf(f"A2a_sq{i}", (128, 4 * 512), BF16) for i in range(2)]
    bcmp = [p.sbuf(f"A2a_b{i}", (128, 6 * 512), BF16) for i in range(2)]
    wt = [p.sbuf(f"A2a_wt{i}", (128, 4 * 512), BF16) for i in range(2)]
    psg = [p.psum(f"A2a_psg{i}", (128, 512), F32) for i in range(2)]
    psu = [p.psum(f"A2a_psu{i}", (128, 512), F32) for i in range(2)]

    wfree = None
    gfree = [[None, None], [None, None]]
    hgfree = [None, None]
    hhfree = [None, None]
    sqfree = [None, None]
    bfree = [None, None]
    wtfree = [None, None]
    PAIRS = [(0, 1), (0, 2), (0, 3), (1, 2), (1, 3), (2, 3)]
    for gb in range(NGB):
        lg = lu = None
        for m in range(4):
            col0 = (m * 2048 + gb * 128)
            lg = sc.dma("gpsimd",
                        wgb.ap().rearrange("p (m kt c) -> p (m kt) c", m=4, kt=NKT)[:, m * NKT:(m + 1) * NKT, :],
                        wg.rearrange("(kt p) n -> p kt n", p=128)[:, :, col0:col0 + 128],
                        waits=[wfree if m == 0 else None], inc=lgs)
            lu = sc.dma("gpsimd",
                        wub.ap().rearrange("p (m kt c) -> p (m kt) c", m=4, kt=NKT)[:, m * NKT:(m + 1) * NKT, :],
                        wu.rearrange("(kt p) n -> p kt n", p=128)[:, :, col0:col0 + 128],
                        inc=lus)
        for tg in range(8):
            pa = (gb * 8 + tg) % 2
            tok0 = tg * 512
            hdeps = []
            for m in range(4):
                gmm = sc.emit("tensor", lambda e, m=m, tok0=tok0, pa=pa:
                              mk_mm2(e, psg[pa], wgb[:, m * NKT * 128:(m + 1) * NKT * 128],
                                     xn_sb, tok0),
                              waits=[lg, xn_rdy, gfree[pa][0]], inc=mm)
                umm = sc.emit("tensor", lambda e, m=m, tok0=tok0, pa=pa:
                              mk_mm2(e, psu[pa], wub[:, m * NKT * 128:(m + 1) * NKT * 128],
                                     xn_sb, tok0),
                              waits=[lu, gfree[pa][1]], inc=mm)
                e1 = sc.emit("scalar", lambda e, pa=pa:
                             e.activation(hg[pa][:], psg[pa][:], AF.Silu),
                             waits=[gmm, hgfree[pa]], inc=se)
                gfree[pa][0] = e1
                e2 = sc.emit("vector", lambda e, pa=pa, m=m:
                             e.tensor_tensor(hh[pa][:, m * 512:(m + 1) * 512],
                                             hg[pa][:], psu[pa][:], ALU.mult),
                             waits=[umm, e1, hhfree[pa]], inc=ve)
                gfree[pa][1] = e2
                hgfree[pa] = e2
                hdeps.append(e2)
                s0 = sc.emit("gpsimd", lambda e, pa=pa, m=m:
                             e.tensor_tensor(sq[pa][:, m * 512:(m + 1) * 512],
                                             hh[pa][:, m * 512:(m + 1) * 512],
                                             hh[pa][:, m * 512:(m + 1) * 512], ALU.mult),
                             waits=[e2, sqfree[pa]], inc=gp)
                hdeps.append(s0)
            sqv = lambda m, pa: sq[pa][:, m * 512:(m + 1) * 512]
            cb = []
            for ci, (i, j) in enumerate(PAIRS):
                cb.append(sc.emit("vector", lambda e, pa=pa, ci=ci, i=i, j=j:
                          e.tensor_tensor(bcmp[pa][:, ci * 512:(ci + 1) * 512],
                                          sq[pa][:, i * 512:(i + 1) * 512],
                                          sq[pa][:, j * 512:(j + 1) * 512],
                                          ALU.is_ge),
                          waits=hdeps + [bfree[pa]], inc=ve))
            sqfree[pa] = cb[-1]
            B_ = lambda ci, pa=pa: bcmp[pa][:, ci * 512:(ci + 1) * 512]
            W_ = lambda i, pa=pa: wt[pa][:, i * 512:(i + 1) * 512]
            deps = cb + [wtfree[pa]]
            a0 = sc.emit("gpsimd", lambda e, pa=pa: e.tensor_add(
                wt[pa][:, 0:512], bcmp[pa][:, 0:512], bcmp[pa][:, 512:1024]),
                waits=deps, inc=gp)
            a1 = sc.emit("vector", lambda e, pa=pa: e.tensor_add(
                wt[pa][:, 512:1024], bcmp[pa][:, 3 * 512:4 * 512],
                bcmp[pa][:, 4 * 512:5 * 512]), waits=deps, inc=ve)
            a2 = sc.emit("gpsimd", lambda e, pa=pa: e.tensor_sub(
                wt[pa][:, 1024:1536], bcmp[pa][:, 5 * 512:6 * 512],
                bcmp[pa][:, 512:1024]), waits=deps, inc=gp)
            a3 = sc.emit("vector", lambda e, pa=pa: e.tensor_add(
                wt[pa][:, 1536:2048], bcmp[pa][:, 2 * 512:3 * 512],
                bcmp[pa][:, 4 * 512:5 * 512]), waits=deps, inc=ve)
            a0b = sc.emit("gpsimd", lambda e, pa=pa: e.tensor_add(
                wt[pa][:, 0:512], wt[pa][:, 0:512], bcmp[pa][:, 2 * 512:3 * 512]),
                waits=[a0], inc=gp)
            a1b = sc.emit("vector", lambda e, pa=pa: e.tensor_sub(
                wt[pa][:, 512:1024], wt[pa][:, 512:1024], bcmp[pa][:, 0:512]),
                waits=[a1], inc=ve)
            a2b = sc.emit("gpsimd", lambda e, pa=pa: e.tensor_sub(
                wt[pa][:, 1024:1536], wt[pa][:, 1024:1536], bcmp[pa][:, 3 * 512:4 * 512]),
                waits=[a2], inc=gp)
            a3b = sc.emit("vector", lambda e, pa=pa: e.tensor_add(
                wt[pa][:, 1536:2048], wt[pa][:, 1536:2048], bcmp[pa][:, 5 * 512:6 * 512]),
                waits=[a3], inc=ve)
            bfree[pa] = a3b
            thr = [(a0b, ALU.is_ge, 2.0, "gpsimd", gp), (a1b, ALU.is_ge, 1.0, "vector", ve),
                   (a2b, ALU.is_ge, 0.0, "gpsimd", gp), (a3b, ALU.is_le, 1.0, "vector", ve)]
            mdep = []
            for i, (dep, op, c, eng, sm) in enumerate(thr):
                mdep.append(sc.emit(eng, lambda e, pa=pa, i=i, op=op, c=c:
                            e.tensor_single_scalar(wt[pa][:, i * 512:(i + 1) * 512],
                                                   wt[pa][:, i * 512:(i + 1) * 512],
                                                   c, op),
                            waits=[dep], inc=sm))
            aps = []
            for i in range(4):
                eng, sm = ("vector", ve) if i % 2 == 0 else ("gpsimd", gp)
                aps.append(sc.emit(eng, lambda e, pa=pa, i=i:
                           e.tensor_tensor(hh[pa][:, i * 512:(i + 1) * 512],
                                           hh[pa][:, i * 512:(i + 1) * 512],
                                           wt[pa][:, i * 512:(i + 1) * 512], ALU.mult),
                           waits=[mdep[i]], inc=sm))
            wtfree[pa] = aps[3]
            wr = None
            for m in range(4):
                wr = sc.dma("sync",
                            hsd[tg * 4:tg * 4 + 4, m * 16 + gb]
                            .rearrange("a p t -> p a t"),
                            hh[pa][:, m * 512:(m + 1) * 512]
                            .rearrange("p (a t) -> p a t", a=4),
                            waits=[aps[m]], inc=st[pa])
            hhfree[pa] = wr
        wfree = umm
    wdin, wdout = p.hbm["L0_ffn_w_down"], p.hbm["wd_bf"]
    wdb = [p.sbuf(f"A2a_wd{i}", (128, H // 2), BF16) for i in range(2)]
    wdfree = [None, None]
    for ib2 in range(II // 64):
        pa = ib2 % 2
        r0 = ib2 * 64
        l = sc.dma("gpsimd",
                   wdb[pa][:],
                   wdin.rearrange("(q p) c -> p q c", p=128)[:, ib2 // 2, :][:, (ib2 % 2) * 1024:(ib2 % 2) * 1024 + 1024],
                   waits=[wdfree[pa]], inc=lwd[pa])
        w = sc.dma("sync",
                   wdout.rearrange("(q p) c -> p q c", p=128)[:, ib2 // 2, :][:, (ib2 % 2) * 1024:(ib2 % 2) * 1024 + 1024],
                   wdb[pa][:],
                   waits=[l], inc=swd[pa])
        wdfree[pa] = w
    p.ready["hs"] = [(st[0], st[0].n), (st[1], st[1].n)]
    p.ready["wd_bf"] = [(swd[0], swd[0].n), (swd[1], swd[1].n)]
    p.finish(st[0])
    p.finish(st[1])
    p.finish(swd[0])
    p.finish(swd[1])


def mk_mm2(e, dst, src, xn_sb, tok0):
    for kt in range(NKT):
        ins = e.matmul(dst[:], src[:, kt * 128:(kt + 1) * 128],
                       xn_sb[:, kt * T + tok0: kt * T + tok0 + 512],
                       start=(kt == 0), stop=(kt == NKT - 1))
    return ins


def stage_A2b(p: Prog):
    sc = p.sched
    hsd, wdb, gfd = p.hbm["hs"], p.hbm["wd_bf"], p.hbm["gf"]
    lw = sc.sem("A2b_lw")
    lh = [sc.sem("A2b_lh0"), sc.sem("A2b_lh1")]
    mm = sc.sem("A2b_mm")
    se = sc.sem("A2b_s")
    st = [sc.sem("A2b_st0"), sc.sem("A2b_st1")]
    NIB = II // 128
    wd_res = p.sbuf("A2b_wd", (128, NIB * 1024), BF16)
    hcol = [p.sbuf(f"A2b_h{i}", (128, NIB * 128), BF16) for i in range(2)]
    gfb = [p.sbuf(f"A2b_gf{i}", (128, 1024), BF16) for i in range(2)]
    psf = [[p.psum(f"A2b_ps{i}_{j}", (128, 512), F32) for j in range(2)]
           for i in range(2)]
    wd_rdy = None
    hfree = [None, None]
    psfree = [[None, None], [None, None]]
    gffree = [None, None]
    for half in range(2):
        for ib in range(NIB):
            wd_rdy = sc.dma("gpsimd", wd_res[:, ib * 1024:(ib + 1) * 1024],
                            wdb[ib * 128:(ib + 1) * 128,
                                half * 1024:(half + 1) * 1024],
                            waits=[p.rdy("wd_bf"),
                                   psfree[0][1] if ib == 0 else None,
                                   psfree[1][1] if ib == 0 else None],
                            inc=lw)
        for tt in range(NTT):
            pa = tt % 2
            lc = sc.dma("sync", hcol[pa].ap().rearrange("p (ib t) -> p ib t", ib=NIB),
                        hsd[tt].rearrange("ib p t -> p ib t"),
                        waits=[p.rdy("hs"), hfree[pa]], inc=lh[pa])

            def mmd(e, pa=pa):
                for ib in range(NIB):
                    for fg in range(2):
                        ins = e.matmul(psf[pa][fg][:],
                                       hcol[pa][:, ib * 128:(ib + 1) * 128],
                                       wd_res[:, ib * 1024 + fg * 512: ib * 1024 + fg * 512 + 512],
                                       start=(ib == 0), stop=(ib == NIB - 1))
                return ins
            g = sc.emit("tensor", mmd,
                        waits=[wd_rdy, lc, psfree[pa][0], psfree[pa][1]], inc=mm)
            hfree[pa] = g
            ev = None
            for fg in range(2):
                ev = sc.emit("scalar", lambda e, pa=pa, fg=fg:
                             e.activation(gfb[pa][:, fg * 512:(fg + 1) * 512],
                                          psf[pa][fg][:], AF.Silu),
                             waits=[g, gffree[pa]], inc=se)
                psfree[pa][fg] = ev
            wr = sc.dma("sync",
                        gfd[tt * 128:(tt + 1) * 128, half * 1024:(half + 1) * 1024],
                        gfb[pa][:], waits=[ev], inc=st[pa])
            gffree[pa] = wr
    p.ready["gf"] = [(st[0], st[0].n), (st[1], st[1].n)]
    p.finish(st[0])
    p.finish(st[1])



B_LIMIT_CH = None
B_LIMIT_BH = None
B_PRO = 99
B_NO_MEMSET = False
B_INST = 99
INV_STAGES = 1

def stage_B(p: Prog):
    sc = p.sched
    nc = p.nc
    qs, ks, vs, gs = p.hbm["qs"], p.hbm["ks"], p.hbm["vs"], p.hbm["gsig"]
    ogd = p.hbm["og"]
    rows_d = nc.dram_tensor("B_rows", [NBH * NCH, 256], F32, kind="Internal").ap()
    gbd = nc.dram_tensor("B_gbd", [NBH * NCH], F32, kind="Internal").ap()
    gcd = nc.dram_tensor("B_gcd", [NBH * NCH], F32, kind="Internal").ap()

    lb1 = sc.sem("B_l1"); lb2 = sc.sem("B_l2"); lb3 = sc.sem("B_l3")
    ve = sc.sem("B_v"); se = sc.sem("B_s"); gp = sc.sem("B_g")
    mm = sc.sem("B_mm"); rxs = sc.sem("B_rx")
    st = [sc.sem(f"B_st{i}") for i in range(4)]
    ld = [sc.sem(f"B_ld{i}") for i in range(4)]
    lr = [sc.sem(f"B_lr{i}") for i in range(4)]
    PBbc = p.psum("B_ps_bc2", (128, 512), F32)
    PBkk = p.psum("B_ps_kk2", (128, 512), F32)
    PBinv = p.psum("B_ps_inv2", (128, 512), F32)
    PBsd = p.psum("B_ps_sd", (128, 512), F32)
    PBu = p.psum("B_ps_u2", (128, 512), F32)
    PBo = p.psum("B_ps_o2", (128, 512), F32)
    PBfa = p.psum("B_ps_fa", (128, 768), BF16)
    PBfb = p.psum("B_ps_fb", (128, 256), BF16)
    pstr = PBbc

    tA = p.sbuf("B_tA", (16, T), F32)
    tB = p.sbuf("B_tB", (16, T), F32)
    tL1 = p.sbuf("B_tL1", (16, T), F32)
    tL2 = p.sbuf("B_tL2", (16, T), F32)
    tG = p.sbuf("B_tG", (16, T), F32)
    tH = p.sbuf("B_tH", (16, T), F32)
    on16 = p.sbuf("B_on16", (16, S), F32)
    sm16 = p.sbuf("B_sm16", (16, 96), F32)
    al = lambda: tA[:]
    bt = lambda: tB[:]
    lna_ = lambda: tL1[:]
    lnb_ = lambda: tL2[:]
    G_ = lambda: tG[:]
    Hp_ = lambda: tH[:]

    l1 = sc.dma("sync", tA[:], p.hbm["alpha"], waits=[p.rdy("alpha")], inc=lb1)
    l2 = sc.dma("sync", tB[:], p.hbm["beta"], waits=[p.rdy("beta")], inc=lb2)
    e1 = sc.emit("scalar", lambda e: e.activation(lna_(), al(), AF.Ln),
                 waits=[l1], inc=se)
    e2 = sc.emit("scalar", lambda e: e.activation(lnb_(), bt(), AF.Ln),
                 waits=[l2], inc=se)
    z1 = sc.emit("gpsimd", lambda e: e.memset(on16[:], 1.0), inc=gp)
    g1 = sc.emit("vector", lambda e: e.tensor_tensor_scan(
        G_()[:, 0:S], lna_()[:, 0:S], on16[:], 0.0, ALU.add, ALU.mult),
        waits=[e1, z1], inc=ve)
    g2 = sc.emit("vector", lambda e: e.tensor_tensor_scan(
        G_()[:, S:T], lna_()[:, S:T], on16[:], 0.0, ALU.add, ALU.mult),
        waits=[e1, z1], inc=ve)
    h1 = sc.emit("vector", lambda e: e.scalar_tensor_tensor(
        Hp_(), G_(), 2.0 * LNC, lna_(), ALU.add, ALU.subtract),
        waits=[g1, g2, e2], inc=ve)
    h2 = sc.emit("vector", lambda e: e.tensor_tensor(Hp_(), Hp_(), lnb_(), ALU.add),
                 waits=[h1], inc=ve)

    if B_PRO < 2:
        p.finish(st[0])
        return
    idmf = p.sbuf("B_idmf", (128, 128), F32)
    onesf = p.sbuf("B_onesf", (128, 128), F32)
    zi = sc.emit("gpsimd", lambda e: e.memset(onesf[:], 1.0), inc=gp)
    idr = sc.emit("gpsimd", lambda e: e.affine_select(
        idmf[:], onesf[:], [[1, 128]], ALU.is_equal, 0.0, base=0,
        channel_multiplier=-1), waits=[zi], inc=gp)
    onesb = p.consts["ones"]
    idm = p.consts["idm"]
    cmask = p.sbuf("B_cmask", (128, 256), F32)
    zt128 = p.sbuf("B_zt", (128, 128), F32)
    zn = sc.emit("gpsimd", lambda e: e.memset(zt128[:], 0.0), inc=gp)
    mk1 = sc.emit("gpsimd", lambda e: e.affine_select(
        cmask[:, 0:128], zt128[:], [[1, 128]], ALU.is_gt, -1e30, base=0,
        channel_multiplier=-1), waits=[zn], inc=gp)
    mk2 = sc.emit("gpsimd", lambda e: e.affine_select(
        cmask[:, 128:256], zt128[:], [[1, 128]], ALU.is_ge, -1e30, base=0,
        channel_multiplier=-1), waits=[zn], inc=gp)

    posNG = p.sbuf("B_posNG", (128, 512), F32)
    posB = p.sbuf("B_posB", (128, 512), F32)
    posBGC = p.sbuf("B_posBGC", (128, 512), F32)

    def transpose_seq(srcfn, dstT, deps):
        done = []
        for b_ in range(2):
            for blk in range(NCH):
                col = (b_ * 16 + blk) * 16
                tr = sc.emit("tensor", lambda e, b_=b_, blk=blk, srcfn=srcfn:
                             e.transpose(pstr[:, (blk % 4) * 128:(blk % 4) * 128 + 16],
                                         srcfn()[:, b_ * S + blk * 128: b_ * S + blk * 128 + 128],
                                         idmf[0:16, 0:16]),
                             waits=list(deps) + [idr,
                                    done[-1] if done else None],
                             inc=mm)
                ev = sc.emit("vector" if (blk % 2 == 0) else "scalar",
                             lambda e, col=col, blk=blk, dstT=dstT:
                             (e.tensor_copy(dstT[:, col:col + 16],
                                            pstr[:, (blk % 4) * 128:(blk % 4) * 128 + 16])
                              if blk % 2 == 0 else
                              e.activation(dstT[:, col:col + 16],
                                           pstr[:, (blk % 4) * 128:(blk % 4) * 128 + 16],
                                           AF.Copy)),
                             waits=[tr], inc=(ve if blk % 2 == 0 else se))
                done.append(ev)
        return done[-1], done[-2]

    if B_PRO < 3:
        p.finish(st[0])
        return
    tb1, tb2 = transpose_seq(bt, posB, [l2])
    if B_PRO < 4:
        p.finish(st[0])
        return
    gpv = sc.emit("vector", lambda e: e.tensor_scalar_add(lna_(), G_(), LNC),
                  waits=[h1], inc=ve)
    ngv = sc.emit("vector", lambda e: e.tensor_scalar_mul(al(), G_(), -1.0),
                  waits=[g1, g2, e1], inc=ve)
    tn1, tn2 = transpose_seq(al, posNG, [ngv])
    if B_PRO < 5:
        p.finish(st[0])
        return
    b1 = sc.emit("scalar", lambda e: e.activation(bt(), Hp_(), AF.Exp),
                 waits=[h2, tb1, tb2], inc=se)
    bgcv = sc.emit("vector", lambda e: e.tensor_scalar_mul(
        bt(), bt(), -float(np.exp(-LNC))), waits=[b1], inc=ve)
    tg1, tg2 = transpose_seq(bt, posBGC, [bgcv])

    if B_PRO < 6:
        p.finish(st[0])
        return
    zb = sc.emit("gpsimd", lambda e: e.memset(sm16[:, 0:32], 0.0), inc=gp)
    c1 = sc.emit("vector", lambda e: e.tensor_copy(
        sm16[:, 1:16], G_()[:, 127:2047:128]), waits=[g1, zb], inc=ve)
    c2 = sc.emit("vector", lambda e: e.tensor_copy(
        sm16[:, 17:32], G_()[:, S + 127:S + 2047:128]), waits=[g2, zb], inc=ve)
    c3 = sc.emit("vector", lambda e: e.tensor_copy(
        sm16[:, 32:48], G_()[:, 127:2048:128]), waits=[g1], inc=ve)
    c4 = sc.emit("vector", lambda e: e.tensor_copy(
        sm16[:, 48:64], G_()[:, S + 127:S + 2048:128]), waits=[g2], inc=ve)
    c5 = sc.emit("vector", lambda e: e.tensor_tensor(
        sm16[:, 64:96], sm16[:, 32:64], sm16[:, 0:32], ALU.subtract),
        waits=[c1, c2, c3, c4], inc=ve)
    c6 = sc.emit("scalar", lambda e: e.activation(sm16[:, 64:96], sm16[:, 64:96],
                                                  AF.Exp), waits=[c5], inc=se)
    rd = rows_d.rearrange("(b h c) w -> h b c w", b=2, h=16)
    for b_ in range(2):
        w1 = sc.dma("sync", rd[:, b_, :, 0:128],
                    Hp_()[:, b_ * S:(b_ + 1) * S].rearrange("h (c w) -> h c w", c=NCH),
                    waits=[h2], inc=rxs)
        w2 = sc.dma("sync", rd[:, b_, :, 128:256],
                    lna_()[:, b_ * S:(b_ + 1) * S].rearrange("h (c w) -> h c w", c=NCH),
                    waits=[gpv], inc=rxs)
        w6 = sc.dma("sync", gcd.rearrange("(b h c) -> h b c", b=2, h=16)[:, b_, :],
                    sm16[:, 64 + b_ * 16:64 + (b_ + 1) * 16],
                    waits=[c6], inc=rxs,
                    allow_slow_non_contiguous=True)
        w7 = sc.dma("sync", gbd.rearrange("(b h c) -> h b c", b=2, h=16)[:, b_, :],
                    sm16[:, b_ * 16:(b_ + 1) * 16],
                    waits=[zb, c1, c2], inc=rxs,
                    allow_slow_non_contiguous=True)

    if B_PRO < 7:
        p.finish(st[0])
        return
    rows_ready = (rxs, rxs.n)
    gcrow = p.sbuf("B_gcrow", (1, 512), F32)
    ones1f = p.sbuf("B_ones1f", (1, 128), F32)
    gC_pos = p.sbuf("B_gCpos", (128, 512), F32)
    zo = sc.emit("gpsimd", lambda e: e.memset(ones1f[:], 1.0), inc=gp)
    lg = sc.dma("sync", gcrow[:], gcd[None, :], waits=[rows_ready], inc=lb3)
    lb3_all = (lb3, lb3.n)
    bc1 = sc.emit("tensor", lambda e: e.matmul(pstr[:], ones1f[:], gcrow[:],
                                               start=True, stop=True),
                  waits=[zo, lb3_all, tg1, tg2], inc=mm)
    bc1e = sc.emit("scalar", lambda e: e.activation(gC_pos[:], pstr[:], AF.Copy),
                   waits=[bc1], inc=se)
    gB_neg = p.sbuf("B_gBneg", (128, 512), F32)
    lb5 = sc.sem("B_l5")
    lgb = sc.dma("sync", gcrow[:], gbd[None, :], waits=[rows_ready, bc1], inc=lb5)
    bc2 = sc.emit("tensor", lambda e: e.matmul(pstr[:], ones1f[:], gcrow[:],
                                               start=True, stop=True),
                  waits=[lgb, bc1e], inc=mm)
    bc2e = sc.emit("scalar", lambda e: e.activation(gB_neg[:], pstr[:], AF.Copy,
                                                    scale=-1.0),
                   waits=[bc2], inc=se)

    if B_PRO < 8:
        p.finish(st[0])
        return
    STf_t = p.sbuf("B_STf", (128, NBH * 128), F32)
    STb_t = p.sbuf("B_STb", (128, NBH * 128), BF16)
    STf = [STf_t[:, i * 128:(i + 1) * 128] for i in range(NBH)]
    STb = [STb_t[:, i * 128:(i + 1) * 128] for i in range(NBH)]
    if B_NO_MEMSET:
        stz = [None for _ in range(NBH)]
    else:
        z1_ = sc.emit("gpsimd", lambda e: e.memset(STf_t[:], 0.0), inc=gp)
        z2_ = sc.emit("gpsimd", lambda e: e.memset(STb_t[:], 0.0),
                      waits=[z1_], inc=gp)
        stz = [z2_ for _ in range(NBH)]


    DEP = 4
    qt = [p.sbuf(f"B_q{i}", (128, 128), BF16) for i in range(DEP)]
    kt = [p.sbuf(f"B_k{i}", (128, 128), BF16) for i in range(DEP)]
    vt = [p.sbuf(f"B_v{i}", (128, 128), BF16) for i in range(DEP)]
    gt = [p.sbuf(f"B_gt{i}", (128, 128), BF16) for i in range(DEP)]
    rowb = [p.sbuf(f"B_row{i}", (1, 256), F32) for i in range(DEP)]
    bci = [p.sbuf(f"B_bci{i}", (128, 256), F32) for i in range(DEP)]
    dm = [p.sbuf(f"B_dm{i}", (128, 256), F32) for i in range(DEP)]
    kkb = [p.sbuf(f"B_kkb{i}", (128, 128), BF16) for i in range(DEP)]
    kqb = [p.sbuf(f"B_kqb{i}", (128, 128), BF16) for i in range(DEP)]
    ndev = [p.sbuf(f"B_ndev{i}", (128, 128), BF16) for i in range(DEP)]
    nmath = [p.sbuf(f"B_nmath{i}", (128, 128), BF16) for i in range(DEP)]
    atb = [p.sbuf(f"B_atb{i}", (128, 128), BF16) for i in range(DEP)]
    ipn = [p.sbuf(f"B_ipn{i}", (128, 128), BF16) for i in range(DEP)]
    pdev = [[p.sbuf(f"B_pdev{i}_{j}", (128, 128), BF16) for j in range(2)]
            for i in range(DEP)]
    pmath = [[p.sbuf(f"B_pmath{i}_{j}", (128, 128), BF16) for j in range(2)]
             for i in range(DEP)]
    sdev = [[p.sbuf(f"B_sdev{i}_{j}", (128, 128), BF16) for j in range(2)]
            for i in range(DEP)]
    ktm = [p.sbuf(f"B_ktm{i}", (128, 128), BF16) for i in range(DEP)]
    kw = [p.sbuf(f"B_kw{i}", (128, 128), BF16) for i in range(DEP)]
    bv = [p.sbuf(f"B_bv{i}", (128, 128), BF16) for i in range(DEP)]
    wtb = [p.sbuf(f"B_wtb{i}", (128, 128), BF16) for i in range(DEP)]
    ub = [p.sbuf(f"B_ub{i}", (128, 128), BF16) for i in range(DEP)]
    udec = [p.sbuf(f"B_udec{i}", (128, 128), BF16) for i in range(DEP)]
    gcol = [p.sbuf(f"B_gcol{i}", (128, 1), F32) for i in range(DEP)]
    psc = [p.sbuf(f"B_psc{i}", (128, 128), F32) for i in range(DEP)]
    osb = [p.sbuf(f"B_osb{i}", (128, 128), F32) for i in range(DEP)]
    oc_ = [p.sbuf(f"B_oc{i}", (128, 128), F32) for i in range(DEP)]
    sqs = [p.sbuf(f"B_sqs{i}", (128, 128), F32) for i in range(DEP)]
    stt_ = [p.sbuf(f"B_stt{i}", (128, 4), F32) for i in range(DEP)]
    olnb = [p.sbuf(f"B_oln{i}", (128, 128), BF16) for i in range(DEP)]
    ogb = [p.sbuf(f"B_og{i}", (128, 128), BF16) for i in range(DEP)]
    lnw_col = p.sbuf("B_lnwc", (128, 1), F32)
    lnb_col = p.sbuf("B_lnbc", (128, 1), F32)
    lb4 = sc.sem("B_l4")
    lnc1 = sc.dma("sync", lnw_col[:], p.hbm["L0_attn_ln_w"][:, None], inc=lb4)
    lnc2 = sc.dma("sync", lnb_col[:], p.hbm["L0_attn_ln_b"][:, None], inc=lb4)
    lnc_all = (lb4, lb4.n)

    prev = {}
    prev["bc_free"] = bc2e
    V = [None] * (NCH * NBH)

    def phase1(i):
        c, bh = divmod(i, NBH)
        b_, h_ = bh // 16, bh % 16
        pa = i % DEP
        bq = (i % 2) * 256
        t0 = b_ * S + c * 128
        row_i = (b_ * 16 + h_) * 16 + c
        col = bh * 16 + c
        colT = (b_ * 16 + c) * 16 + h_
        d = {}
        sc.dma("gpsimd", qt[pa][:], qs[h_ * 128:(h_ + 1) * 128, t0:t0 + 128],
               waits=[p.rdy("qs"), prev.get(f"ld_free{pa}")], inc=ld[pa])
        sc.dma("gpsimd", kt[pa][:], ks[h_ * 128:(h_ + 1) * 128, t0:t0 + 128],
               waits=[p.rdy("ks")], inc=ld[pa])
        sc.dma("gpsimd", vt[pa][:], vs[h_ * 128:(h_ + 1) * 128, t0:t0 + 128],
               waits=[p.rdy("vs")], inc=ld[pa])
        d["lg"] = sc.dma("gpsimd", gt[pa][:],
                         gs[h_ * 128:(h_ + 1) * 128, t0:t0 + 128],
                         waits=[p.rdy("gsig")], inc=ld[pa])
        d["lq"] = d["lk"] = d["lv"] = d["lg"]
        l_r = sc.dma("scalar", rowb[pa][:], rows_d[row_i, 0:256][None, :],
                     waits=[rows_ready, prev.get(f"lr_free{pa}")], inc=lr[pa])
        bcm = sc.emit("tensor", lambda e, pa=pa: e.matmul(
            PBbc[:, bq:bq + 256], ones1f[:], rowb[pa][:], start=True, stop=True),
            waits=[l_r, zo, prev.get("bc_free")], inc=mm)
        prev[f"lr_free{pa}"] = bcm
        badd = sc.emit("vector", lambda e, pa=pa: e.tensor_tensor(
            bci[pa][:], PBbc[:, bq:bq + 256], cmask[:], ALU.add),
            waits=[bcm, mk1, mk2, prev.get(f"bci_free{pa}")], inc=ve)
        prev["bc_free"] = badd
        d["ex1"] = sc.emit("scalar", lambda e, pa=pa, colT=(b_ * 16 + c) * 16 + h_:
                           e.activation(dm[pa][:], bci[pa][:], AF.Exp,
                                        bias=posNG[:, colT:colT + 1]),
                           waits=[badd, tn1, tn2, prev.get(f"dm_free{pa}")],
                           inc=se)
        prev[f"bci_free{pa}"] = d["ex1"]
        d["gcr"] = sc.emit("scalar", lambda e, pa=pa, colT=colT, col=col:
                           e.activation(gcol[pa][:], posNG[:, colT:colT + 1],
                                        AF.Exp, scale=-1.0,
                                        bias=gB_neg[:, col:col + 1]),
                           waits=[tn1, tn2, bc2e,
                                  prev.get(f"gcol_free{pa}")], inc=se)
        kkm = sc.emit("tensor", lambda e, pa=pa, bq=bq: e.matmul(
            PBkk[:, bq:bq + 128], kt[pa][:], kt[pa][:], start=True, stop=True),
            waits=[d["lk"], prev.get(f"kk_free{i % 2}"),
                   prev.get(f"kk_free2{i % 2}")], inc=mm)
        kqm = sc.emit("tensor", lambda e, pa=pa, bq=bq: e.matmul(
            PBkk[:, bq + 128:bq + 256], kt[pa][:], qt[pa][:], start=True,
            stop=True),
            waits=[d["lq"], kkm], inc=mm)
        kke = sc.emit("scalar", lambda e, pa=pa, bq=bq: e.activation(
            kkb[pa][:], PBkk[:, bq:bq + 128], AF.Copy, scale=-1.0),
            waits=[kqm, prev.get(f"kkb_free{pa}")], inc=se)
        kqe = sc.emit("scalar", lambda e, pa=pa, bq=bq: e.activation(
            kqb[pa][:], PBkk[:, bq + 128:bq + 256], AF.Copy),
            waits=[kqm, prev.get(f"kqb_free{pa}")], inc=se)
        prev[f"kk_free{i % 2}"] = kqe
        prev[f"kk_free2{i % 2}"] = kke
        d["kke"], d["kqe"] = kke, kqe
        return d

    def phase2(i, d):
        pa = i % DEP
        qq = i % 2
        n1 = sc.emit("gpsimd", lambda e, pa=pa: e.tensor_tensor(
            ndev[pa][:], dm[pa][:, 0:128], kkb[pa][:], ALU.mult),
            waits=[d["ex1"], d["kke"], prev.get(f"ndev_free{pa}")], inc=gp)
        d["at2"] = sc.emit("gpsimd", lambda e, pa=pa: e.tensor_tensor(
            atb[pa][:], dm[pa][:, 128:256], kqb[pa][:], ALU.mult),
            waits=[d["ex1"], d["kqe"], prev.get(f"atb_free{pa}")], inc=gp)
        prev[f"dm_free{pa}"] = d["at2"]
        prev[f"kkb_free{pa}"] = n1
        prev[f"kqb_free{pa}"] = d["at2"]
        ntr = sc.emit("tensor", lambda e, pa=pa, qq=qq: e.transpose(
            PBfa[:, 256 + qq * 384:384 + qq * 384], ndev[pa][:], idm[:]),
            waits=[n1, p.consts["idm_rdy"], prev.get(f"fa3_free{qq}")], inc=mm)
        n2 = sc.emit("vector", lambda e, pa=pa, qq=qq: e.tensor_copy(
            nmath[pa][:], PBfa[:, 256 + qq * 384:384 + qq * 384]),
            waits=[ntr, prev.get(f"nmath_free{pa}")], inc=ve)
        prev[f"fa3_free{qq}"] = n2
        ip = sc.emit("gpsimd", lambda e, pa=pa: e.tensor_tensor(
            ipn[pa][:], ndev[pa][:], idm[:], ALU.add),
            waits=[n1, p.consts["idm_rdy"], prev.get(f"ipn_free{pa}")], inc=gp)
        scur, sdep = ipn[pa], ip
        cd, cm = ndev[pa], nmath[pa]
        cd_dep, cm_dep = n1, n2
        supd = None
        for k in range(1, INV_STAGES + 1):
            pd, pm = pdev[pa][k % 2], pmath[pa][k % 2]
            d1 = sc.emit("tensor", lambda e, cd=cd, cm=cm, qq=qq: e.matmul(
                PBinv[:, qq * 256:qq * 256 + 128], cm[:], cd[:], start=True,
                stop=True),
                waits=[cd_dep, cm_dep, prev.get(f"inv_free{qq}"),
                       prev.get(f"inv_free2{qq}")], inc=mm) \
                if k <= INV_STAGES - 1 else None
            d2 = sc.emit("tensor", lambda e, cd=cd, cm=cm, qq=qq: e.matmul(
                PBinv[:, qq * 256 + 128:qq * 256 + 256], cd[:], cm[:],
                start=True, stop=True),
                waits=[cd_dep, cm_dep, prev.get(f"inv_free{qq}"),
                       prev.get(f"inv_free2{qq}")], inc=mm)
            ieng, isem = (("vector", ve) if k % 2 else ("scalar", se))
            pde = sc.emit(ieng, lambda e, pd=pd, qq=qq: (
                e.tensor_copy(pd[:], PBinv[:, qq * 256:qq * 256 + 128])
                if e.__class__.__name__ != "BassScalarEngine"
                else e.activation(pd[:], PBinv[:, qq * 256:qq * 256 + 128],
                                  AF.Copy)),
                waits=[d1, d2, prev.get(f"pd{pa}{k%2}_free")], inc=isem) \
                if d1 is not None else None
            pme = sc.emit(ieng, lambda e, pm=pm, qq=qq: (
                e.tensor_copy(pm[:], PBinv[:, qq * 256 + 128:qq * 256 + 256])
                if e.__class__.__name__ != "BassScalarEngine"
                else e.activation(pm[:], PBinv[:, qq * 256 + 128:qq * 256 + 256],
                                  AF.Copy)),
                waits=[d2, prev.get(f"pm{pa}{k%2}_free")], inc=isem)
            prev[f"inv_free{qq}"] = pme
            prev[f"inv_free2{qq}"] = pde
            supd = sc.emit("tensor", lambda e, pm=pm, scur=scur, qq=qq: e.matmul(
                PBsd[:, qq * 256:qq * 256 + 128], pm[:], scur[:], start=True,
                stop=True),
                waits=[pme, sdep, prev.get(f"S_free{qq}")], inc=mm)
            prev[f"pd{pa}{k%2}_free"] = supd if pde is None else None
            prev[f"pm{pa}{k%2}_free"] = supd
            snew = sdev[pa][k % 2]
            sadd = sc.emit("vector", lambda e, snew=snew, scur=scur, qq=qq:
                           e.tensor_tensor(snew[:], PBsd[:, qq * 256:qq * 256 + 128],
                                           scur[:], ALU.add),
                           waits=[supd, sdep, prev.get(f"sdev{pa}{k%2}_free")],
                           inc=ve)
            prev[f"S_free{qq}"] = sadd
            cd, cm = pd, pm
            cd_dep, cm_dep = pde, pme
            scur, sdep = snew, sadd
        prev[f"ipn_free{pa}"] = sdep
        prev[f"ndev_free{pa}"] = sdep
        prev[f"nmath_free{pa}"] = sdep
        d["TT"], d["tt_dep"], d["supd"] = scur, sdep, supd
        return d

    def phase3(i, d):
        c, bh = divmod(i, NBH)
        pa = i % DEP
        qq = i % 2
        TT, tt_dep = d["TT"], d["tt_dep"]
        colT = ((bh // 16) * 16 + c) * 16 + (bh % 16)
        fq = qq * 384
        trk = sc.emit("tensor", lambda e, pa=pa, fq=fq: e.transpose(
            PBfa[:, fq:fq + 128], kt[pa][:], idm[:]),
            waits=[d["lk"], p.consts["idm_rdy"], prev.get(f"fa1_free{qq}")],
            inc=mm)
        trv = sc.emit("tensor", lambda e, pa=pa, fq=fq: e.transpose(
            PBfa[:, fq + 128:fq + 256], vt[pa][:], idm[:]),
            waits=[d["lv"], trk, prev.get(f"fa2_free{qq}")], inc=mm)
        kte = sc.emit("vector", lambda e, pa=pa, fq=fq: e.tensor_copy(
            ktm[pa][:], PBfa[:, fq:fq + 128]),
            waits=[trv, prev.get(f"ktm_free{pa}")], inc=ve)
        kwe = sc.emit("vector", lambda e, pa=pa, colT=colT, fq=fq: e.tensor_scalar(
            kw[pa][:], PBfa[:, fq:fq + 128], posBGC[:, colT:colT + 1], None,
            ALU.mult),
            waits=[trv, tg1, tg2, prev.get(f"kw_free{pa}")], inc=ve)
        bve = sc.emit("vector", lambda e, pa=pa, colT=colT, fq=fq: e.tensor_scalar(
            bv[pa][:], PBfa[:, fq + 128:fq + 256], posB[:, colT:colT + 1], None,
            ALU.mult),
            waits=[trv, tb1, tb2, prev.get(f"bv_free{pa}")], inc=ve)
        prev[f"fa1_free{qq}"] = kwe
        prev[f"fa2_free{qq}"] = bve
        uq = qq * 256
        wtm = sc.emit("tensor", lambda e, pa=pa, TT=TT, uq=uq: e.matmul(
            PBu[:, uq + 128:uq + 256], kw[pa][:], TT[:], start=True, stop=True),
            waits=[kwe, tt_dep, prev.get(f"U_free{qq}"),
                   prev.get(f"U_free2{qq}")], inc=mm)
        wte = sc.emit("vector", lambda e, pa=pa, uq=uq: e.tensor_copy(
            wtb[pa][:], PBu[:, uq + 128:uq + 256]),
            waits=[wtm, prev.get(f"wtb_free{pa}")], inc=ve)
        prev[f"kw_free{pa}"] = wtm
        um1 = sc.emit("tensor", lambda e, pa=pa, TT=TT, uq=uq: e.matmul(
            PBu[:, uq:uq + 128], TT[:], bv[pa][:], start=True, stop=False),
            waits=[tt_dep, bve, wte], inc=mm)
        um2 = sc.emit("tensor", lambda e, pa=pa, bh=bh, uq=uq: e.matmul(
            PBu[:, uq:uq + 128], wtb[pa][:], STb[bh], start=False, stop=True),
            waits=[um1, wte, stz[bh], prev.get(f"stb{bh}")], inc=mm)
        prev[f"sdev{pa}{INV_STAGES % 2}_free"] = um1
        prev[f"sdev{pa}{(INV_STAGES - 1) % 2}_free"] = d["supd"]
        prev[f"bv_free{pa}"] = um1
        prev[f"wtb_free{pa}"] = um2
        ue = sc.emit("scalar", lambda e, pa=pa, uq=uq: e.activation(
            ub[pa][:], PBu[:, uq:uq + 128], AF.Copy),
            waits=[um2, prev.get(f"ub_free{pa}")], inc=se)
        ude = sc.emit("vector", lambda e, pa=pa, uq=uq: e.tensor_scalar(
            udec[pa][:], PBu[:, uq:uq + 128], dm[pa][:, 127:128], None,
            ALU.mult),
            waits=[um2, ue, d["ex1"], prev.get(f"udec_free{pa}")], inc=ve)
        prev[f"U_free{qq}"] = ude
        prev[f"U_free2{qq}"] = ue
        om1 = sc.emit("tensor", lambda e, pa=pa, uq=uq: e.matmul(
            PBo[:, uq:uq + 128], atb[pa][:], ub[pa][:], start=True, stop=True),
            waits=[d["at2"], ue, prev.get(f"O_free{qq}")], inc=mm)
        om2 = sc.emit("tensor", lambda e, pa=pa, bh=bh, uq=uq: e.matmul(
            PBo[:, uq + 128:uq + 256], qt[pa][:], STb[bh], start=True,
            stop=True),
            waits=[d["lq"], om1, stz[bh], prev.get(f"stb{bh}")], inc=mm)
        pse = sc.emit("scalar", lambda e, pa=pa, uq=uq: e.activation(
            psc[pa][:], PBo[:, uq + 128:uq + 256], AF.Copy,
            scale=gcol[pa][:, 0:1]),
            waits=[om2, d["gcr"], prev.get(f"psc_free{pa}")], inc=se)
        prev[f"gcol_free{pa}"] = pse
        prev[f"atb_free{pa}"] = om1
        prev[f"ub_free{pa}"] = om1
        oad = sc.emit("vector", lambda e, pa=pa, uq=uq: e.tensor_tensor(
            osb[pa][:], PBo[:, uq:uq + 128], psc[pa][:], ALU.add),
            waits=[pse, om1, prev.get(f"osb_free{pa}")], inc=ve)
        prev[f"O_free{qq}"] = oad
        prev[f"psc_free{pa}"] = oad
        d.update(kte=kte, ude=ude, um2=um2, om2=om2, oad=oad)
        return d

    def phase4(i, d):
        c, bh = divmod(i, NBH)
        b_, h_ = bh // 16, bh % 16
        pa = i % DEP
        qq = i % 2
        sq_ = qq * 256
        t0 = b_ * S + c * 128
        col = bh * 16 + c
        dsm = sc.emit("tensor", lambda e, pa=pa, sq_=sq_: e.matmul(
            PBsd[:, sq_ + 128:sq_ + 256], ktm[pa][:], udec[pa][:], start=True,
            stop=True),
            waits=[d["kte"], d["ude"], prev.get(f"dst_free{qq}")], inc=mm)
        stu = sc.emit("vector", lambda e, pa=pa, bh=bh, col=col, sq_=sq_:
                      e.scalar_tensor_tensor(
                          STf[bh], STf[bh], gC_pos[:, col:col + 1],
                          PBsd[:, sq_ + 128:sq_ + 256], ALU.mult, ALU.add),
                      waits=[dsm, bc1e, stz[bh], d["um2"], d["om2"]], inc=ve)
        stbu = sc.emit("gpsimd", lambda e, bh=bh: e.tensor_copy(
            STb[bh], STf[bh]), waits=[stu], inc=gp)
        prev[f"stb{bh}"] = stbu
        prev[f"dst_free{qq}"] = stu
        prev[f"ktm_free{pa}"] = dsm
        prev[f"udec_free{pa}"] = dsm
        rs1 = sc.emit("vector", lambda e, pa=pa: e.reduce_sum(
            stt_[pa][:, 0:1], osb[pa][:], axis=AX.X),
            waits=[d["oad"]], inc=ve)
        ctr = sc.emit("vector", lambda e, pa=pa: e.tensor_scalar(
            oc_[pa][:], osb[pa][:], float(D), stt_[pa][:, 0:1],
            ALU.mult, ALU.subtract),
            waits=[rs1, prev.get(f"oc_free{pa}")], inc=ve)
        prev[f"osb_free{pa}"] = ctr
        sqa = sc.emit("scalar", lambda e, pa=pa: e.activation(
            sqs[pa][:], oc_[pa][:], AF.Square, accum_out=stt_[pa][:, 2:3]),
            waits=[ctr], inc=se)
        t1_ = sc.emit("vector", lambda e, pa=pa: e.tensor_scalar(
            stt_[pa][:, 3:4], stt_[pa][:, 2:3], 1.0 / D,
            float(D * D) * 1e-5, ALU.mult, ALU.add),
            waits=[sqa], inc=ve)
        ln2 = sc.emit("scalar", lambda e, pa=pa: e.activation(
            stt_[pa][:, 2:3], stt_[pa][:, 3:4], AF.Ln),
            waits=[t1_], inc=se)
        rse = sc.emit("scalar", lambda e, pa=pa: e.activation(
            stt_[pa][:, 0:1], stt_[pa][:, 2:3], AF.Exp, scale=-0.5),
            waits=[ln2], inc=se)
        nrm = sc.emit("vector", lambda e, pa=pa: e.tensor_scalar(
            olnb[pa][:], oc_[pa][:], stt_[pa][:, 0:1], None, ALU.mult),
            waits=[rse, prev.get(f"olnb_free{pa}")], inc=ve)
        prev[f"oc_free{pa}"] = nrm
        tro = sc.emit("tensor", lambda e, pa=pa, qq=qq: e.transpose(
            PBfb[:, qq * 128:qq * 128 + 128], olnb[pa][:], idm[:]),
            waits=[nrm, prev.get(f"fb_free{qq}")], inc=mm)
        prev[f"olnb_free{pa}"] = tro
        og1 = sc.emit("vector", lambda e, pa=pa, qq=qq: e.tensor_scalar(
            ogb[pa][:], PBfb[:, qq * 128:qq * 128 + 128], lnw_col[:, 0:1],
            lnb_col[:, 0:1], ALU.mult, ALU.add),
            waits=[tro, lnc_all, prev.get(f"og_free{pa}")], inc=ve)
        prev[f"fb_free{qq}"] = og1
        oge = sc.emit("vector", lambda e, pa=pa: e.tensor_tensor(
            ogb[pa][:], ogb[pa][:], gt[pa][:], ALU.mult),
            waits=[og1, d["lg"]], inc=ve)
        prev[f"ld_free{pa}"] = oge
        wr = sc.dma("sync", ogd[h_ * 128:(h_ + 1) * 128, t0:t0 + 128],
                    ogb[pa][:], waits=[oge], inc=st[pa])
        prev[f"og_free{pa}"] = wr

    NI = (B_LIMIT_CH if B_LIMIT_CH is not None else NCH) * \
         (B_LIMIT_BH if B_LIMIT_BH is not None else NBH)
    if B_LIMIT_BH is not None and B_LIMIT_BH != NBH:
        NI = (B_LIMIT_CH if B_LIMIT_CH is not None else NCH) * B_LIMIT_BH
    insts = []
    for c in range(B_LIMIT_CH if B_LIMIT_CH is not None else NCH):
        for bh in range(B_LIMIT_BH if B_LIMIT_BH is not None else NBH):
            insts.append(c * NBH + bh)
    dstate = {}
    for t in range(len(insts) + 3):
        if t < len(insts):
            dstate[insts[t]] = phase1(insts[t])
        if 1 <= t < len(insts) + 1:
            i = insts[t - 1]
            dstate[i] = phase2(i, dstate[i])
        if 2 <= t < len(insts) + 2:
            i = insts[t - 2]
            dstate[i] = phase3(i, dstate[i])
        if 3 <= t:
            i = insts[t - 3]
            phase4(i, dstate[i])
            del dstate[i]
    p.ready["og"] = [(s_, s_.n) for s_ in st]
    for s_ in st:
        p.finish(s_)



def stage_C(p: Prog):
    sc = p.sched
    ogd, wod = p.hbm["og"], p.hbm["L0_attn_Wo"]
    xd, gfd, x1d = p.hbm["x"], p.hbm["gf"], p.hbm["x1"]
    lwo = sc.sem("C_lw")
    lo = [sc.sem("C_lo0"), sc.sem("C_lo1")]
    lx = [sc.sem("C_lx0"), sc.sem("C_lx1")]
    lg = [sc.sem("C_lg0"), sc.sem("C_lg1")]
    mm = sc.sem("C_mm"); ve = sc.sem("C_v"); se = sc.sem("C_s")
    st = [sc.sem("C_st0"), sc.sem("C_st1")]
    wo_sb = p.sbuf("C_wo", (128, NKT * H), BF16)
    ogc = [p.sbuf(f"C_og{i}", (128, NKT * 128), BF16) for i in range(2)]
    xb = [p.sbuf(f"C_x{i}", (128, H), F32) for i in range(2)]
    gfb = [p.sbuf(f"C_gf{i}", (128, H), BF16) for i in range(2)]
    x1b = [p.sbuf(f"C_x1{i}", (128, H), F32) for i in range(2)]
    psf = [[p.psum(f"C_ps0_{j}", (128, 512), F32) for j in range(4)]]
    psf.append(psf[0])
    wo_rdy = None
    for kt in range(NKT):
        wo_rdy = sc.dma("gpsimd", wo_sb[:, kt * H:(kt + 1) * H],
                        wod[kt * 128:(kt + 1) * 128, :], inc=lwo)
    ogfree = [None, None]
    xfree = [None, None]
    gffree = [None, None]
    x1free = [None, None]
    psfree = [[None] * 4, [None] * 4]
    for tt in range(NTT):
        pa = tt % 2
        lc = sc.dma("sync", ogc[pa][:],
                    ogd.rearrange("(kt p) t -> p kt t", p=128)[:, :, tt * 128:(tt + 1) * 128],
                    waits=[p.rdy("og"), ogfree[pa]], inc=lo[pa])
        lxx = sc.dma("sync", xb[pa][:], xd[tt * 128:(tt + 1) * 128, :],
                     waits=[xfree[pa]], inc=lx[pa])
        lgg = sc.dma("sync", gfb[pa][:], gfd[tt * 128:(tt + 1) * 128, :],
                     waits=[p.rdy("gf"), gffree[pa]], inc=lg[pa])
        gmm = None
        for fg in range(4):
            def mmc(e, pa=pa, fg=fg):
                for kt in range(NKT):
                    ins = e.matmul(psf[pa][fg][:],
                                   ogc[pa][:, kt * 128:(kt + 1) * 128],
                                   wo_sb[:, kt * H + fg * 512: kt * H + fg * 512 + 512],
                                   start=(kt == 0), stop=(kt == NKT - 1))
                return ins
            gmm = sc.emit("tensor", mmc,
                          waits=[lc, wo_rdy, psfree[pa][fg],
                                 psfree[1 - pa][fg]], inc=mm)
            psfree[pa][fg] = gmm
        ogfree[pa] = gmm
        ev = None
        for fg in range(4):
            t1 = sc.emit("vector", lambda e, pa=pa, fg=fg: e.tensor_tensor(
                x1b[pa][:, fg * 512:(fg + 1) * 512], psf[pa][fg][:],
                gfb[pa][:, fg * 512:(fg + 1) * 512], ALU.mult),
                waits=[psfree[pa][fg], lgg, x1free[pa]], inc=ve)
            ev = sc.emit("vector", lambda e, pa=pa, fg=fg: e.tensor_tensor(
                x1b[pa][:, fg * 512:(fg + 1) * 512],
                x1b[pa][:, fg * 512:(fg + 1) * 512],
                xb[pa][:, fg * 512:(fg + 1) * 512], ALU.add),
                waits=[t1, lxx], inc=ve)
            psfree[pa][fg] = ev
        xfree[pa] = ev
        gffree[pa] = ev
        wr = sc.dma("sync", x1d[tt * 128:(tt + 1) * 128, :], x1b[pa][:],
                    waits=[ev], inc=st[pa])
        x1free[pa] = wr
    p.ready["x1"] = [(st[0], st[0].n), (st[1], st[1].n)]
    p.finish(st[0])
    p.finish(st[1])


def stage_E(p: Prog):
    stage_ln_generic(p, "x1", "xn1", "E")
    sc = p.sched
    xn_sb, xn_rdy = load_xn_res(p, "xn1")
    lw = [sc.sem("E_lw0"), sc.sem("E_lw1")]
    mm = sc.sem("E_mm"); ve = sc.sem("E_v"); se = sc.sem("E_s")
    gp = sc.sem("E_gp")
    st = [sc.sem("E_st0"), sc.sem("E_st1")]
    wbuf = [p.sbuf(f"E_w{i}", (128, NKT * 128), BF16) for i in range(2)]
    ot = [p.sbuf(f"E_o{i}", (128, 512), BF16) for i in range(2)]
    rt1 = p.sbuf("E_rt1", (64, 512), F32)
    rt2 = p.sbuf("E_rt2", (64, 512), F32)
    rt3 = p.sbuf("E_rt3", (64, 512), F32)
    ps = [p.psum(f"E_ps{i}", (128, 512), F32) for i in range(2)]
    pfr = p.psum("E_pfr", (64, 512), F32)
    cosb = p.sbuf("E_cos", (64, T), BF16)
    sinb = p.sbuf("E_sin", (64, T), BF16)
    invf = p.sbuf("E_invf", (1, 64), F32)
    tvec = p.sbuf("E_tvec", (1, 512), F32)
    frw = p.sbuf("E_frw", (64, 512), F32)
    fri = p.sbuf("E_fri", (64, 512), I32)

    z1 = sc.emit("gpsimd", lambda e: e.iota(invf[:], [[1, 64]], base=0,
                                            channel_multiplier=0,
                                            allow_small_or_imprecise_dtypes=True),
                 inc=gp)
    iv = sc.emit("scalar", lambda e: e.activation(
        invf[:], invf[:], AF.Exp, scale=-float(np.log(10000.0) / 64.0)),
        waits=[z1], inc=se)
    PI = float(np.pi)
    tabdep = []
    for tg in range(8):
        pos0 = (tg % 4) * 512
        zt = sc.emit("gpsimd", lambda e, pos0=pos0: e.iota(
            tvec[:], [[1, 512]], base=pos0, channel_multiplier=0,
            allow_small_or_imprecise_dtypes=True),
            waits=[tabdep[-1] if tabdep else None], inc=gp)
        fr = sc.emit("tensor", lambda e: e.matmul(pfr[:], invf[:], tvec[:],
                                                  start=True, stop=True),
                     waits=[zt, iv, tabdep[-1] if tabdep else None], inc=mm)
        m1 = sc.emit("vector", lambda e: e.tensor_scalar_mul(
            frw[:], pfr[:], 1.0 / (2 * PI)), waits=[fr], inc=ve)
        m2 = sc.emit("vector", lambda e: e.tensor_copy(fri[:], frw[:]),
                     waits=[m1], inc=ve)
        m3 = sc.emit("vector", lambda e: e.tensor_copy(rt1[:], fri[:]),
                     waits=[m2], inc=ve)
        m4 = sc.emit("vector", lambda e: e.tensor_tensor(
            frw[:], frw[:], rt1[:], ALU.subtract), waits=[m3], inc=ve)
        m5 = sc.emit("vector", lambda e: e.tensor_single_scalar(
            rt1[:], frw[:], 0.5, ALU.is_gt), waits=[m4], inc=ve)
        m6 = sc.emit("vector", lambda e: e.scalar_tensor_tensor(
            frw[:], rt1[:], -1.0, frw[:], ALU.mult, ALU.add),
            waits=[m5], inc=ve)
        s1 = sc.emit("scalar", lambda e, tg=tg: e.activation(
            sinb[:, tg * 512:(tg + 1) * 512], frw[:], AF.Sin,
            scale=2 * PI), waits=[m6], inc=se)
        c1 = sc.emit("vector", lambda e: e.tensor_scalar_add(frw[:], frw[:],
                                                             0.25),
                     waits=[s1], inc=ve)
        c2 = sc.emit("vector", lambda e: e.tensor_single_scalar(
            rt1[:], frw[:], 0.5, ALU.is_gt), waits=[c1], inc=ve)
        c3 = sc.emit("vector", lambda e: e.scalar_tensor_tensor(
            frw[:], rt1[:], -1.0, frw[:], ALU.mult, ALU.add),
            waits=[c2], inc=ve)
        c4 = sc.emit("scalar", lambda e, tg=tg: e.activation(
            cosb[:, tg * 512:(tg + 1) * 512], frw[:], AF.Sin,
            scale=2 * PI), waits=[c3], inc=se)
        tabdep.append(c4)
    tab_rdy = tabdep[-1]

    projs = [("L1_attn_Wq", "q1r", 16, True),
             ("L1_attn_Wk", "k1r", 4, True),
             ("L1_attn_Wv", "v1s", 4, False)]
    wfree = [None, None]
    psfree = [None, None]
    otfree = [None, None]
    n = 0
    for wname, oname, ndb, rope in projs:
        W = p.hbm[wname]
        dst = p.hbm[oname]
        for db in range(ndb):
            wp = n % 2
            lw_t = sc.dma(
                "gpsimd",
                wbuf[wp].ap().rearrange("p (kt c) -> p kt c", kt=NKT),
                W.rearrange("(kt p) n -> p kt n", p=128)[:, :, db * 128:(db + 1) * 128],
                waits=[wfree[wp]], inc=lw[wp])
            for tg in range(8):
                pp = (n * 8 + tg) % 2
                tok0 = tg * 512

                def mmp(e, wp=wp, tok0=tok0, pp=pp):
                    for kt in range(NKT):
                        ins = e.matmul(ps[pp][:],
                                       wbuf[wp][:, kt * 128:(kt + 1) * 128],
                                       xn_sb[:, kt * T + tok0: kt * T + tok0 + 512],
                                       start=(kt == 0), stop=(kt == NKT - 1))
                    return ins
                g = sc.emit("tensor", mmp, waits=[lw_t, xn_rdy, psfree[pp]], inc=mm)
                if rope:
                    cs = lambda: cosb[0:64, tok0:tok0 + 512]
                    sn = lambda: sinb[0:64, tok0:tok0 + 512]
                    t1 = sc.emit("vector", lambda e, pp=pp, tok0=tok0: e.tensor_tensor(
                        rt1[:], ps[pp][0:64, :], cosb[0:64, tok0:tok0 + 512], ALU.mult),
                        waits=[g, tab_rdy], inc=ve)
                    t2 = sc.emit("vector", lambda e, pp=pp, tok0=tok0: e.tensor_tensor(
                        rt2[:], ps[pp][64:128, :], sinb[0:64, tok0:tok0 + 512], ALU.mult),
                        waits=[g], inc=ve)
                    t3 = sc.emit("vector", lambda e, pp=pp, tok0=tok0: e.tensor_tensor(
                        rt3[:], ps[pp][0:64, :], sinb[0:64, tok0:tok0 + 512], ALU.mult),
                        waits=[g], inc=ve)
                    t4 = sc.emit("vector", lambda e, pp=pp, tok0=tok0: e.tensor_tensor(
                        ps[pp][64:128, :], ps[pp][64:128, :],
                        cosb[0:64, tok0:tok0 + 512], ALU.mult),
                        waits=[t2], inc=ve)
                    r1 = sc.emit("vector", lambda e, pp=pp: e.tensor_tensor(
                        ot[pp][0:64, :], rt1[:], rt2[:], ALU.subtract),
                        waits=[t1, t2, otfree[pp]], inc=ve)
                    r2 = sc.emit("vector", lambda e, pp=pp: e.tensor_tensor(
                        ot[pp][64:128, :], rt3[:], ps[pp][64:128, :], ALU.add),
                        waits=[t3, t4, r1], inc=ve)
                    a = r2
                    psfree[pp] = r2
                else:
                    a = sc.emit("scalar", lambda e, pp=pp: e.activation(
                        ot[pp][:], ps[pp][:], AF.Copy),
                        waits=[g, otfree[pp]], inc=se)
                    psfree[pp] = a
                wr = sc.dma("sync", dst[db * 128:(db + 1) * 128, tok0:tok0 + 512],
                            ot[pp][:], waits=[a], inc=st[pp])
                otfree[pp] = wr
            wfree[wp] = g
            n += 1
    for nm in ("q1r", "k1r", "v1s"):
        p.ready[nm] = [(st[0], st[0].n), (st[1], st[1].n)]
    p.finish(st[0])
    p.finish(st[1])



def stage_F(p: Prog):
    sc = p.sched
    qd, kd, vd = p.hbm["q1r"], p.hbm["k1r"], p.hbm["v1s"]
    wod, s1d = p.hbm["L1_attn_Wo"], p.hbm["s1"]
    lk = sc.sem("F_lk")
    lq = [sc.sem("F_lq0"), sc.sem("F_lq1")]
    lv = sc.sem("F_lv")
    lwo = [sc.sem("F_lw0"), sc.sem("F_lw1")]
    lat = [sc.sem("F_la0"), sc.sem("F_la1")]
    mm = sc.sem("F_mm"); ve = sc.sem("F_v"); se = sc.sem("F_s"); gp = sc.sem("F_g")
    st = [sc.sem("F_st0"), sc.sem("F_st1")]
    SCALE = float(D ** -0.5)
    EB = -10.0

    k_sb = p.sbuf("F_k", (128, 4 * T), BF16)
    vtm = p.sbuf("F_vtm", (128, 4 * 16 * 128), BF16)
    vcol = p.sbuf("F_vcol", (128, 4 * 128), BF16)
    qt = [p.sbuf(f"F_q{i}", (128, 512), BF16) for i in range(2)]
    e_sb = [p.sbuf(f"F_e{i}", (128, 16 * 512), BF16) for i in range(2)]
    denr = [p.sbuf(f"F_denr{i}", (1, 512), F32) for i in range(2)]
    rb = [p.sbuf(f"F_rb{i}", (128, 512), F32) for i in range(2)]
    ao = [p.sbuf(f"F_ao{i}", (128, 512), BF16) for i in range(2)]
    at_res = p.sbuf("F_atres", (128, 16 * S), BF16)
    ones1f = p.sbuf("F_ones1f", (1, 128), F32)
    wo_sb = [p.sbuf(f"F_wo{i}", (128, NKT * 512), BF16) for i in range(2)]
    s1b = [p.sbuf(f"F_s1{i}", (128, 512), BF16) for i in range(2)]
    pss = [p.psum(f"F_pss{i}", (128, 512), F32) for i in range(2)]
    psd = p.psum("F_psd", (1, 512), F32)
    psb_t = p.psum("F_psb", (128, 512), F32)
    psb = [psb_t, psb_t]
    psa = [p.psum(f"F_psa{i}", (128, 512), F32) for i in range(2)]
    psvt = p.psum("F_psvt", (128, 128), BF16)

    ebc = p.sbuf("F_ebc", (128, 1), F32)
    zo = sc.emit("gpsimd", lambda e: e.memset(ones1f[:], 1.0), inc=gp)
    zeb = sc.emit("gpsimd", lambda e: e.memset(ebc[:], EB), inc=gp)
    k_rdy = None
    for kv in range(4):
        k_rdy = sc.dma("sync", k_sb[:, kv * T:(kv + 1) * T],
                       kd[kv * 128:(kv + 1) * 128, :],
                       waits=[p.rdy("k1r")], inc=lk)
    idm = p.consts["idm"]

    qfree = [None, None]
    efree = [None, None]
    aofree = [None, None]
    ssfree = [None, None]
    avfree = [None, None]
    prev_den = [None, None]
    prev_rb = [None, None]
    prev_bc = [None, None]
    vtmfree = None
    vcolfree = None
    bcfree = None
    atfree = None
    s1free = [None, None]
    wofree = [None, None]
    n = 0
    for b_ in range(2):
        tdone = []
        for i_ in range(16):
            lvc = sc.dma("sync", vcol[:],
                         vd[:, b_ * S + i_ * 128: b_ * S + (i_ + 1) * 128]
                         .rearrange("(kv p) t -> p kv t", p=128),
                         waits=[p.rdy("v1s"), vcolfree], inc=lv)
            for kv in range(4):
                tr = sc.emit("tensor", lambda e, kv=kv, i_=i_: e.transpose(
                    psvt[:], vcol[:, kv * 128:(kv + 1) * 128], idm[:]),
                    waits=[lvc, p.consts["idm_rdy"],
                           tdone[-1] if tdone else atfree], inc=mm)
                ev = sc.emit("vector", lambda e, kv=kv, i_=i_: e.tensor_copy(
                    vtm[:, (kv * 16 + i_) * 128:(kv * 16 + i_) * 128 + 128],
                    psvt[:]),
                    waits=[tr, vtmfree], inc=ve)
                tdone.append(ev)
            vcolfree = tdone[-1]
        vtm_rdy = tdone[-1]
        for h_ in range(NH):
            kv = h_ // 4
            for qg in range(4):
                pa = n % 2
                q0 = b_ * S + qg * 512
                lqq = sc.dma("scalar", qt[pa][:],
                             qd[h_ * 128:(h_ + 1) * 128, q0:q0 + 512],
                             waits=[p.rdy("q1r"), qfree[pa]], inc=lq[pa])
                expp = [ssfree[0], ssfree[1]]
                exps = []
                for kt2 in range(16):
                    kp2 = kt2 % 2
                    sm_ = sc.emit("tensor", lambda e, pa=pa, kp2=kp2, kv=kv, kt2=kt2, b_=b_: e.matmul(
                        pss[kp2][:], k_sb[:, kv * T + b_ * S + kt2 * 128:
                                          kv * T + b_ * S + kt2 * 128 + 128],
                        qt[pa][:], start=True, stop=True),
                        waits=[k_rdy, lqq, expp[kp2]], inc=mm)
                    ex_ = sc.emit("scalar", lambda e, pa=pa, kp2=kp2, kt2=kt2: e.activation(
                        e_sb[pa][:, kt2 * 512:(kt2 + 1) * 512], pss[kp2][:],
                        AF.Exp, scale=SCALE, bias=ebc[:, 0:1]),
                        waits=[sm_, zeb, efree[pa]], inc=se)
                    expp[kp2] = ex_
                    exps.append(ex_)
                exl = exps[-1]
                ssfree[0], ssfree[1] = expp[0], expp[1]
                den_mm = None
                for kt2 in range(16):
                    den_mm = sc.emit("tensor", lambda e, pa=pa, kt2=kt2: e.matmul(
                        psd[0:1, :],
                        p.consts["ones"][:, 0:1],
                        e_sb[pa][:, kt2 * 512:(kt2 + 1) * 512],
                        start=(kt2 == 0), stop=(kt2 == 15)),
                        waits=[exps[kt2], p.consts["idm_rdy"],
                               prev_den[pa]], inc=mm)
                rcp = sc.emit("vector", lambda e, pa=pa: e.reciprocal(
                    denr[pa][:], psd[0:1, :]),
                    waits=[den_mm, prev_rb[pa]], inc=ve)
                prev_den[0] = rcp
                prev_den[1] = rcp
                bcm = sc.emit("tensor", lambda e, pa=pa: e.matmul(
                    psb[pa][:], ones1f[:], denr[pa][:], start=True, stop=True),
                    waits=[rcp, prev_bc[0], prev_bc[1]], inc=mm)
                rbe = sc.emit("scalar", lambda e, pa=pa: e.activation(
                    rb[pa][:], psb[pa][:], AF.Copy),
                    waits=[bcm], inc=se)
                prev_bc[pa] = rbe
                avm = None
                for kt2 in range(16):
                    avm = sc.emit("tensor", lambda e, pa=pa, kv=kv, kt2=kt2: e.matmul(
                        psa[pa][:], vtm[:, (kv * 16 + kt2) * 128:(kv * 16 + kt2) * 128 + 128],
                        e_sb[pa][:, kt2 * 512:(kt2 + 1) * 512],
                        start=(kt2 == 0), stop=(kt2 == 15)),
                        waits=[vtm_rdy, exl, avfree[pa]], inc=mm)
                efree[pa] = avm
                qfree[pa] = avm
                nrm = sc.emit("vector", lambda e, pa=pa: e.tensor_tensor(
                    ao[pa][:], psa[pa][:], rb[pa][:], ALU.mult),
                    waits=[avm, rbe, aofree[pa]], inc=ve)
                avfree[pa] = nrm
                prev_rb[pa] = nrm
                cp = sc.emit("vector", lambda e, pa=pa, h_=h_, qg=qg: e.tensor_copy(
                    at_res[:, h_ * S + qg * 512: h_ * S + qg * 512 + 512]
                    if False else
                    at_res.ap().rearrange("p (kt t) -> p kt t", kt=16)[:, h_, qg * 512:qg * 512 + 512],
                    ao[pa][:]),
                    waits=[nrm, atfree], inc=ve)
                aofree[pa] = cp
                n += 1
        at_rdy = cp
        for fg in range(4):
            wp = fg % 2
            lw_t = sc.dma("gpsimd",
                          wo_sb[wp].ap().rearrange("p (kt c) -> p kt c", kt=NKT),
                          wod.rearrange("(kt p) n -> p kt n", p=128)[:, :, fg * 512:(fg + 1) * 512],
                          waits=[wofree[wp]], inc=lwo[wp])
            for tt in range(16):
                pa = tt % 2

                def mmw(e, wp=wp, tt=tt, pa=pa):
                    for kt in range(NKT):
                        ins = e.matmul(psa[pa][:],
                                       at_res.ap().rearrange("p (kt t) -> p kt t", kt=16)[:, kt, tt * 128:(tt + 1) * 128],
                                       wo_sb[wp][:, kt * 512:(kt + 1) * 512],
                                       start=(kt == 0), stop=(kt == NKT - 1))
                    return ins
                g = sc.emit("tensor", mmw, waits=[lw_t, at_rdy, avfree[pa]], inc=mm)
                ev = sc.emit("scalar", lambda e, pa=pa: e.activation(
                    s1b[pa][:], psa[pa][:], AF.Copy),
                    waits=[g, s1free[pa]], inc=se)
                avfree[pa] = ev
                wr = sc.dma("sync",
                            s1d[b_ * S + tt * 128: b_ * S + (tt + 1) * 128,
                                fg * 512:(fg + 1) * 512],
                            s1b[pa][:], waits=[ev], inc=st[pa])
                s1free[pa] = wr
            wofree[wp] = g
        atfree = g
    p.ready["s1"] = [(st[0], st[0].n), (st[1], st[1].n)]
    p.finish(st[0])
    p.finish(st[1])


def stage_G0(p: Prog):
    sc = p.sched
    jobs = [("L1_ffn_gate_w", "wgT", II, H, None),
            ("L1_ffn_up_w", "wuT", II, H, None),
            ("L1_ffn_down_w", "wdT", H, II, "L1_ffn_down_scale")]
    li = [sc.sem("G0_li0"), sc.sem("G0_li1")]
    lsc = sc.sem("G0_lsc")
    mm = sc.sem("G0_mm"); ve = sc.sem("G0_v"); se = sc.sem("G0_s")
    st = [sc.sem("G0_st0"), sc.sem("G0_st1")]
    src_i = [p.sbuf(f"G0_i{i}", (128, 512), I32) for i in range(2)]
    src_b = [p.sbuf(f"G0_b{i}", (128, 512), BF16) for i in range(2)]
    dst_b = [p.sbuf(f"G0_d{i}", (128, 512), BF16) for i in range(2)]
    dscol = p.sbuf("G0_ds", (128, 1), F32)
    pst_t = p.psum("G0_ps", (128, 1024), BF16)
    pst = [pst_t[:, 0:512], pst_t[:, 512:1024]]
    idm = p.consts["idm"]
    ifree = [None, None]
    bfree = [None, None]
    dfree = [None, None]
    pfree = [None, None]
    n = 0
    for wname, dname, R, C_, scname in jobs:
        W, Dst = p.hbm[wname], p.hbm[dname]
        for rb_ in range(R // 128):
            lsc_t = None
            if scname is not None:
                lsc_t = sc.dma("gpsimd", dscol[:],
                               p.hbm[scname][rb_ * 128:(rb_ + 1) * 128][:, None],
                               waits=[bfree[0], bfree[1]], inc=lsc)
            for cb in range(C_ // 512):
                pa = n % 2
                l = sc.dma("sync", src_i[pa][:],
                           W[rb_ * 128:(rb_ + 1) * 128, cb * 512:(cb + 1) * 512],
                           waits=[ifree[pa]], inc=li[pa])
                if scname is None:
                    cv = sc.emit("vector", lambda e, pa=pa: e.tensor_scalar_sub(
                        src_b[pa][:], src_i[pa][:], 128.0),
                        waits=[l, bfree[pa]], inc=ve)
                else:
                    cv = sc.emit("vector", lambda e, pa=pa: e.tensor_scalar(
                        src_b[pa][:], src_i[pa][:], -128.0, dscol[:, 0:1],
                        ALU.add, ALU.mult),
                        waits=[l, lsc_t, bfree[pa]], inc=ve)
                ifree[pa] = cv

                def trg(e, pa=pa):
                    for j in range(4):
                        ins = e.transpose(pst[pa][:, j * 128:(j + 1) * 128],
                                          src_b[pa][:, j * 128:(j + 1) * 128],
                                          idm[:])
                    return ins
                tr = sc.emit("tensor", trg,
                             waits=[cv, p.consts["idm_rdy"], pfree[0],
                                    pfree[1]], inc=mm)
                bfree[pa] = tr
                ev = sc.emit("scalar", lambda e, pa=pa: e.activation(
                    dst_b[pa][:], pst[pa][:, :], AF.Copy),
                    waits=[tr, dfree[pa]], inc=se)
                pfree[pa] = ev
                wr = sc.dma("sync",
                            Dst.rearrange("(q p) r -> p q r", p=128)[:, 4 * cb:4 * cb + 4,
                                                                     rb_ * 128:(rb_ + 1) * 128],
                            dst_b[pa].ap().rearrange("p (j c) -> p j c", j=4),
                            waits=[ev], inc=st[pa])
                dfree[pa] = wr
                n += 1
    for nm in ("wgT", "wuT", "wdT"):
        p.ready[nm] = [(st[0], st[0].n), (st[1], st[1].n)]
    p.finish(st[0])
    p.finish(st[1])


def stage_G1(p: Prog):
    sc = p.sched
    xn_sb, xn_rdy = load_xn_res(p, "xn1")
    wg, wu, h1d = p.hbm["wgT"], p.hbm["wuT"], p.hbm["h1"]
    lgs = [sc.sem("G1_lg0"), sc.sem("G1_lg1")]
    lus = [sc.sem("G1_lu0"), sc.sem("G1_lu1")]
    lsc = sc.sem("G1_lsc")
    mm = sc.sem("G1_mm"); ve = sc.sem("G1_v"); se = sc.sem("G1_s")
    st = [sc.sem("G1_st0"), sc.sem("G1_st1")]
    wgb = [p.sbuf(f"G1_wg{i}", (128, NKT * 128), BF16) for i in range(2)]
    wub = [p.sbuf(f"G1_wu{i}", (128, NKT * 128), BF16) for i in range(2)]
    scg = [p.sbuf(f"G1_scg{i}", (128, 2), F32) for i in range(2)]
    hg = [p.sbuf(f"G1_hg{i}", (128, 512), F32) for i in range(2)]
    hb = [p.sbuf(f"G1_hb{i}", (128, 512), BF16) for i in range(2)]
    psg = [p.psum(f"G1_psg{i}", (128, 512), F32) for i in range(2)]
    psu = [p.psum(f"G1_psu{i}", (128, 512), F32) for i in range(2)]
    wfree = [None, None]
    scfree = [None, None]
    gfree = [[None, None], [None, None]]
    hgfree = [None, None]
    hbfree = [None, None]
    for ib in range(II // 128):
        wp = ib % 2
        lg = sc.dma("gpsimd", wgb[wp].ap().rearrange("p (kt c) -> p kt c", kt=NKT),
                    wg.rearrange("(kt p) n -> p kt n", p=128)[:, :, ib * 128:(ib + 1) * 128],
                    waits=[wfree[wp], p.rdy("wgT")], inc=lgs[wp])
        lu = sc.dma("scalar", wub[wp].ap().rearrange("p (kt c) -> p kt c", kt=NKT),
                    wu.rearrange("(kt p) n -> p kt n", p=128)[:, :, ib * 128:(ib + 1) * 128],
                    waits=[p.rdy("wuT")], inc=lgs[wp])
        ls1 = sc.dma("gpsimd", scg[wp][:, 0:1],
                     p.hbm["L1_ffn_gate_scale"][ib * 128:(ib + 1) * 128][:, None],
                     waits=[scfree[wp]], inc=lsc)
        ls2 = sc.dma("gpsimd", scg[wp][:, 1:2],
                     p.hbm["L1_ffn_up_scale"][ib * 128:(ib + 1) * 128][:, None],
                     inc=lsc)
        for tg in range(8):
            pa = (ib * 8 + tg) % 2
            tok0 = tg * 512
            g1 = sc.emit("tensor", lambda e, wp=wp, tok0=tok0, pa=pa:
                         mk_mm2(e, psg[pa], wgb[wp], xn_sb, tok0),
                         waits=[lg, xn_rdy, gfree[pa][0]], inc=mm)
            g2 = sc.emit("tensor", lambda e, wp=wp, tok0=tok0, pa=pa:
                         mk_mm2(e, psu[pa], wub[wp], xn_sb, tok0),
                         waits=[lu, gfree[pa][1]], inc=mm)
            e1 = sc.emit("scalar", lambda e, pa=pa, wp=wp:
                         e.activation(hg[pa][:], psg[pa][:], AF.Silu,
                                      scale=scg[wp][:, 0:1]),
                         waits=[g1, ls2, hgfree[pa]], inc=se)
            gfree[pa][0] = e1
            e2 = sc.emit("vector", lambda e, pa=pa, wp=wp:
                         e.scalar_tensor_tensor(hb[pa][:], psu[pa][:],
                                                scg[wp][:, 1:2], hg[pa][:],
                                                ALU.mult, ALU.mult),
                         waits=[g2, e1, hbfree[pa]], inc=ve)
            gfree[pa][1] = e2
            hgfree[pa] = e2
            wr = sc.dma("sync",
                        h1d[tg * 4:tg * 4 + 4, ib].rearrange("a p t -> p a t"),
                        hb[pa].ap().rearrange("p (a t) -> p a t", a=4),
                        waits=[e2], inc=st[pa])
            hbfree[pa] = wr
        wfree[wp] = g2
        scfree[wp] = e1
    p.ready["h1"] = [(st[0], st[0].n), (st[1], st[1].n)]
    p.finish(st[0])
    p.finish(st[1])


def stage_G2(p: Prog):
    sc = p.sched
    h1d, wdd = p.hbm["h1"], p.hbm["wdT"]
    x1d, s1d, outd = p.hbm["x1"], p.hbm["s1"], p.hbm["out"]
    lw = sc.sem("G2_lw")
    lh = [sc.sem("G2_lh0"), sc.sem("G2_lh1")]
    lx = [sc.sem("G2_lx0"), sc.sem("G2_lx1")]
    ls = [sc.sem("G2_ls0"), sc.sem("G2_ls1")]
    mm = sc.sem("G2_mm"); ve = sc.sem("G2_v"); se = sc.sem("G2_s")
    st = [sc.sem("G2_st0"), sc.sem("G2_st1")]
    NIB = II // 128
    wd_res = p.sbuf("G2_wd", (128, NIB * 1024), BF16)
    hcol = [p.sbuf(f"G2_h{i}", (128, NIB * 128), BF16) for i in range(2)]
    gfb = [p.sbuf(f"G2_gf{i}", (128, 1024), F32) for i in range(2)]
    xb = [p.sbuf(f"G2_x{i}", (128, 1024), F32) for i in range(2)]
    s1b = [p.sbuf(f"G2_s1{i}", (128, 1024), BF16) for i in range(2)]
    ob = [p.sbuf(f"G2_o{i}", (128, 1024), F32) for i in range(2)]
    psf = [[p.psum(f"G2_ps{i}_{j}", (128, 512), F32) for j in range(2)]
           for i in range(2)]
    wd_rdy = None
    hfree = [None, None]
    psfree = [[None, None], [None, None]]
    gffree = [None, None]
    xfree = [None, None]
    s1free = [None, None]
    ofree = [None, None]
    for half in range(2):
        for ib in range(NIB):
            wd_rdy = sc.dma("gpsimd", wd_res[:, ib * 1024:(ib + 1) * 1024],
                            wdd[ib * 128:(ib + 1) * 128,
                                half * 1024:(half + 1) * 1024],
                            waits=[p.rdy("wdT"),
                                   psfree[0][1] if ib == 0 else None,
                                   psfree[1][1] if ib == 0 else None],
                            inc=lw)
        for tt in range(NTT):
            pa = tt % 2
            lc = sc.dma("sync", hcol[pa].ap().rearrange("p (ib t) -> p ib t", ib=NIB),
                        h1d[tt].rearrange("ib p t -> p ib t"),
                        waits=[p.rdy("h1"), hfree[pa]], inc=lh[pa])
            lxx = sc.dma("sync", xb[pa][:],
                         x1d[tt * 128:(tt + 1) * 128,
                             half * 1024:(half + 1) * 1024],
                         waits=[p.rdy("x1"), xfree[pa]], inc=lx[pa])
            ls1 = sc.dma("scalar", s1b[pa][:],
                         s1d[tt * 128:(tt + 1) * 128,
                             half * 1024:(half + 1) * 1024],
                         waits=[p.rdy("s1"), s1free[pa]], inc=ls[pa])

            def mmd(e, pa=pa):
                for ib in range(NIB):
                    for fgx in range(2):
                        ins = e.matmul(psf[pa][fgx][:],
                                       hcol[pa][:, ib * 128:(ib + 1) * 128],
                                       wd_res[:, ib * 1024 + fgx * 512: ib * 1024 + fgx * 512 + 512],
                                       start=(ib == 0), stop=(ib == NIB - 1))
                return ins
            g = sc.emit("tensor", mmd,
                        waits=[wd_rdy, lc, psfree[pa][0], psfree[pa][1]], inc=mm)
            hfree[pa] = g
            ev = wr = None
            for fgx in range(2):
                ev = sc.emit("scalar", lambda e, pa=pa, fgx=fgx: e.activation(
                    gfb[pa][:, fgx * 512:(fgx + 1) * 512], psf[pa][fgx][:],
                    AF.Silu), waits=[g, gffree[pa]], inc=se)
                psfree[pa][fgx] = ev
            col0 = half * 1024
            m1 = sc.emit("vector", lambda e, pa=pa: e.tensor_tensor(
                ob[pa][:], gfb[pa][:, 0:1024],
                s1b[pa][:], ALU.mult),
                waits=[ev, ls1, ofree[pa]], inc=ve)
            gffree[pa] = m1
            m2 = sc.emit("vector", lambda e, pa=pa: e.tensor_tensor(
                ob[pa][:], ob[pa][:],
                xb[pa][:], ALU.add),
                waits=[m1, lxx], inc=ve)
            xfree[pa] = m2
            s1free[pa] = m2
            wr = sc.dma("sync",
                        outd[tt * 128:(tt + 1) * 128, half * 1024:(half + 1) * 1024],
                        ob[pa][:], waits=[m2], inc=st[pa])
            ofree[pa] = wr
    p.finish(st[0])
    p.finish(st[1])


STAGE_FN = {
    "A": stage_A,
    "A1": stage_A1,
    "A2a": stage_A2a,
    "A2b": stage_A2b,
    "B": stage_B,
    "C": stage_C,
    "E": stage_E,
    "F": stage_F,
    "G0": stage_G0,
    "G1": stage_G1,
    "G2": stage_G2,
}

N_CORES = 8


def _ffn0_perm():
    return np.arange(II).reshape(II // 4, 4).T.reshape(-1)


def _rope_perm():
    idx = []
    for h in range(NH):
        base = h * D
        idx.extend(base + np.arange(0, D, 2))
        idx.extend(base + np.arange(1, D, 2))
    return np.asarray(idx)


STAGE_PERSIST = {"G0"}


def build_program(stages):
    p = Prog(stages)
    with p.nc.Block() as block:
        build_consts(p)
        p.sched.flush(block)
        for st in stages:
            if st in STAGE_PERSIST:
                saved_sb, saved_ps = p.sb, p.psn
                p.sb, p.psn = {}, {}
                ng = p.next_guards
                p.next_guards = []
                STAGE_FN[st](p)
                p.sched.flush(block)
                p.sb, p.psn = saved_sb, saved_ps
                p.next_guards = ng
            else:
                p.begin_stage()
                STAGE_FN[st](p)
                p.sched.flush(block)
                p.end_stage()
    return p


def kernel(**inputs):
    xfull = np.ascontiguousarray(np.asarray(inputs["L0_attn_x"], dtype=np.float32))
    perm = _rope_perm()
    kperm = np.concatenate([g * D + np.arange(0, D, 2) for g in range(4)]
                           + [g * D + np.arange(1, D, 2) for g in range(4)])
    weights = {}
    for name in WEIGHTS:
        arr = np.asarray(inputs[name])
        weights[name] = np.ascontiguousarray(arr)
    wq = weights["L1_attn_Wq"]
    weights["L1_attn_Wq"] = np.ascontiguousarray(wq[:, perm])
    wk = weights["L1_attn_Wk"]
    kp = []
    for g in range(4):
        kp.extend(g * D + np.arange(0, D, 2))
        kp.extend(g * D + np.arange(1, D, 2))
    weights["L1_attn_Wk"] = np.ascontiguousarray(wk[:, np.asarray(kp)])
    fperm = _ffn0_perm()
    weights["L0_ffn_w_gate"] = np.ascontiguousarray(weights["L0_ffn_w_gate"][:, fperm])
    weights["L0_ffn_w_up"] = np.ascontiguousarray(weights["L0_ffn_w_up"][:, fperm])
    weights["L0_ffn_w_down"] = np.ascontiguousarray(weights["L0_ffn_w_down"][fperm, :])

    pool = [dict() for _ in range(N_CORES)]
    for c in range(N_CORES):
        pool[c]["x"] = np.ascontiguousarray(
            xfull[2 * c:2 * c + 2].reshape(T, H))
        for name, arr in weights.items():
            pool[c][name] = arr

    groups = [g for g in GROUPS if any(st in STAGE_FN for st in g)]
    outs = None
    for stages in groups:
        stages = [st for st in stages if st in STAGE_FN]
        if not stages:
            continue
        p = build_program(stages)
        in_maps = [{n: pool[c][n] for n in p.input_names} for c in range(N_CORES)]
        res = run_bass_kernel_spmd(p.nc, in_maps, core_ids=list(range(N_CORES)))
        for c in range(N_CORES):
            for n in p.output_names:
                pool[c][n] = np.asarray(res.results[c][n])
        outs = res
    out = np.stack([pool[c]["out"].reshape(2, S, H) for c in range(N_CORES)])
    return out.reshape(16, S, H).astype(np.float32)
